# version 01
# baseline (speedup 1.0000x reference)
import numpy as np
import ml_dtypes
import concourse.bass as bass
import concourse.mybir as mybir
from concourse.bass_utils import run_bass_kernel_spmd

F32 = mybir.dt.float32
BF16 = mybir.dt.bfloat16
AF = mybir.ActivationFunctionType
ALU = mybir.AluOpType
AX = mybir.AxisListType

S = 2048
NT = 16
D = 1024
KT = 8
HA = 16
HF = 8
HD = 64
INNER = 512
HID = 2730
HID_T = 22
WIN = 512
NEG = -3.0e38
LN_EPS = 1e-5

ENGINES = ("tensor", "vector", "scalar", "sync", "gpsimd")


class Rec:
    def __init__(self, nc):
        self.nc = nc
        self.ops = {e: [] for e in ENGINES}
        self.sems = {}
        self.cnt = {}
        self.last_wait = {e: {} for e in ENGINES}

    def sem(self, name):
        self.sems[name] = self.nc.semaphore(name).__enter__()
        self.cnt[name] = 0

    SELF_SEM = {"vector": "vE", "scalar": "aE", "gpsimd": "gE"}

    def op(self, eng, fn, inc=None, inc_n=1, is_dma=False):
        own = None if is_dma else self.SELF_SEM.get(eng)
        if own is not None:
            self.wait(eng, (own, self.cnt[own]))
            if inc is None:
                inc = own
        if inc is None:
            self.ops[eng].append(fn)
            return None
        self.cnt[inc] += inc_n
        tgt = self.cnt[inc]
        sem = self.sems[inc]
        if own is not None and inc != own:
            self.cnt[own] += 1
            tgt2 = self.cnt[own]
            sem2 = self.sems[own]
            self.ops[eng].append(
                lambda e, f=fn, s=sem, n=inc_n, s2=sem2: f(e).then_inc(s, n).then_inc(s2, 1))
            return (inc, tgt)
        self.ops[eng].append(lambda e, f=fn, s=sem, n=inc_n: f(e).then_inc(s, n))
        return (inc, tgt)

    def dma_b(self, q, bsem, dst, src, **kw):
        return self.op(q, lambda e: e.dma_start(dst, src, **kw), inc=bsem,
                       inc_n=16, is_dma=True)

    def dma(self, q, dst, src, **kw):
        inc = {"sync": "qS", "scalar": "qA", "gpsimd": "qG"}[q]
        return self.op(q, lambda e: e.dma_start(dst, src, **kw), inc=inc,
                       inc_n=16, is_dma=True)

    def dmat(self, q, dst, src):
        inc = {"sync": "qS", "scalar": "qA"}[q]
        return self.op(q, lambda e: e.dma_start(dst, src, transpose=True),
                       inc=inc, inc_n=16, is_dma=True)

    def wait(self, eng, ev):
        if ev is None:
            return
        sname, val = ev
        if sname in ("qS", "qA", "qG"):
            val = self.cnt[sname]
        lw = self.last_wait[eng]
        if lw.get(sname, -1) >= val:
            return
        lw[sname] = val
        sem = self.sems[sname]
        self.ops[eng].append(lambda e, s=sem, v=val: e.wait_ge(s, v))

    def marker(self, sname):
        return (sname, self.cnt[sname])

    def opc(self, eng, fn):
        inc = {"vector": "vE", "scalar": "aE", "gpsimd": "qG"}[eng]
        ev = self.op(eng, fn, inc=inc)
        self.wait(eng, ev)
        return ev

    def fence(self, engs=ENGINES, sems=("qS", "qA", "qG", "tE", "vE", "aE", "gE",
                                        "bG0", "bG1", "bG2", "bG3", "bS0", "bS1",
                                        "bS2", "bS3", "bA0", "bA1")):
        marks = [self.marker(s) for s in sems if self.cnt[s] > 0]
        for e in engs:
            for m in marks:
                self.wait(e, m)

    def replay(self, block):
        for eng in ENGINES:
            ops = self.ops[eng]

            def body(e, ops=ops):
                for f in ops:
                    f(e)

            getattr(block, eng)(body)


class Arena:
    def __init__(self, nc, elems_f32):
        self.t = nc.alloc_sbuf_tensor("arena", [128, elems_f32], F32)
        self.free_list = [(0, elems_f32)]
        self.allocs = {}

    def alloc(self, name, elems, dtype):
        f32e = elems if dtype == F32 else (elems + 1) // 2
        align = 16
        for i, (off, size) in enumerate(self.free_list):
            aoff = (off + align - 1) // align * align
            pad = aoff - off
            if size >= f32e + pad:
                rest = size - f32e - pad
                frags = []
                if pad:
                    frags.append((off, pad))
                if rest:
                    frags.append((aoff + f32e, rest))
                self.free_list[i:i + 1] = frags
                self.allocs[name] = (aoff, f32e)
                ap = self.t[:, aoff:aoff + f32e]
                if dtype == BF16:
                    ap = ap.bitcast(BF16)[:, 0:elems]
                return ap
        raise MemoryError(f"arena: cannot alloc {name} ({f32e}); free={self.free_list}")

    def free(self, name):
        off, elems = self.allocs.pop(name)
        self.free_list.append((off, elems))
        self.free_list.sort()
        merged = []
        for o, sz in self.free_list:
            if merged and merged[-1][0] + merged[-1][1] == o:
                merged[-1] = (merged[-1][0], merged[-1][1] + sz)
            else:
                merged.append([o, sz])
        self.free_list = [tuple(x) for x in merged]


class PsumMgr:
    def __init__(self, nc):
        self.t = nc.alloc_psum_tensor("psum", [128, 4096], F32)
        self.free_banks = set(range(8))
        self.allocs = {}

    def alloc(self, name, banks, dtype=F32):
        for b0 in range(8 - banks + 1):
            if all(b0 + j in self.free_banks for j in range(banks)):
                for j in range(banks):
                    self.free_banks.discard(b0 + j)
                self.allocs[name] = (b0, banks)
                ap = self.t[:, b0 * 512:(b0 + banks) * 512]
                return ap.bitcast(BF16) if dtype == BF16 else ap
        raise MemoryError(f"psum {name}: need {banks}, free={sorted(self.free_banks)}")

    def free(self, name):
        b0, banks = self.allocs.pop(name)
        for j in range(banks):
            self.free_banks.add(b0 + j)



def host_constants():
    c = {}
    c["ident"] = np.eye(128, dtype=ml_dtypes.bfloat16)
    c["onesrow"] = np.ones((1, 64), dtype=ml_dtypes.bfloat16)
    r = np.arange(128)[:, None]
    cc = np.arange(640)[None, :]
    c["mask"] = np.where((cc >= r) & (cc <= r + 512), 0.0, NEG).astype(np.float32)
    half = np.arange(0, HD, 2, dtype=np.float64)
    inv_freq = 1.0 / 10000.0 ** (half / HD)
    t = np.arange(S, dtype=np.float64)
    freqs = t[:, None] * inv_freq[None, :]
    freqs = np.concatenate([freqs, freqs], axis=-1)
    base = (half + 0.4 * HD) / (1.4 * HD)
    power = (t - S // 2) / float(WIN // 2)
    scale = base[None, :] ** power[:, None]
    scale = np.concatenate([scale, scale], axis=-1)
    cos, sin = np.cos(freqs), np.sin(freqs)
    sign = np.concatenate([-np.ones(32), np.ones(32)])[None, :]

    def tok(x):
        return np.ascontiguousarray(
            x.reshape(NT, 128, HD).transpose(1, 0, 2).reshape(128, NT * HD)
        ).astype(np.float32)

    c["tqc"] = tok(cos * scale)
    c["tqs"] = tok(sin * scale * sign)
    c["tkc"] = tok(cos / scale)
    c["tks"] = tok(sin / scale * sign)
    return c


def host_layout(inputs):
    g = {}
    w3 = np.asarray(inputs["L0_attn_w_qkv"]).reshape(D, HA, 3, HD)
    g["wq_a"] = np.ascontiguousarray(w3[:, :, 0, :].reshape(D, D))
    g["wk_a"] = np.ascontiguousarray(w3[:, :, 1, :].reshape(D, D))
    g["wv_a"] = np.ascontiguousarray(w3[:, :, 2, :].reshape(D, D))
    g["wout_a"] = np.ascontiguousarray(np.asarray(inputs["L0_attn_w_out"]))
    g["bout_a"] = np.ascontiguousarray(np.asarray(inputs["L0_attn_b_out"]).reshape(1, D))
    g["gamma_a"] = np.ascontiguousarray(np.asarray(inputs["L0_attn_gamma"]).reshape(1, D))
    g["beta_a"] = np.ascontiguousarray(np.asarray(inputs["L0_attn_beta"]).reshape(1, D))
    for l in range(2):
        wf = np.asarray(inputs["L0_ffn_wqkv"])[l]
        g[f"wq_f{l}"] = np.ascontiguousarray(wf[:, 0:INNER])
        g[f"wk_f{l}"] = np.ascontiguousarray(wf[:, INNER:2 * INNER])
        g[f"wv_f{l}"] = np.ascontiguousarray(wf[:, 2 * INNER:3 * INNER])
        g[f"qs_f{l}"] = np.ascontiguousarray(
            np.asarray(inputs["L0_ffn_q_scale"])[l].reshape(1, HD))
        g[f"ks_f{l}"] = np.ascontiguousarray(
            np.asarray(inputs["L0_ffn_k_scale"])[l].reshape(1, HD))
        g[f"wo_f{l}"] = np.ascontiguousarray(np.asarray(inputs["L0_ffn_wo"])[l])
        g[f"lg_f{l}"] = np.ascontiguousarray(
            np.asarray(inputs["L0_ffn_ln_g"])[l].reshape(1, D))
        g[f"lb_f{l}"] = np.ascontiguousarray(
            np.asarray(inputs["L0_ffn_ln_b"])[l].reshape(1, D))
        w1 = np.asarray(inputs["L0_ffn_w1"])[l]
        g[f"w1a_f{l}"] = np.ascontiguousarray(w1[:, 0:HID])
        g[f"w1g_f{l}"] = np.ascontiguousarray(w1[:, HID:2 * HID])
        b1 = np.asarray(inputs["L0_ffn_b1"])[l]
        b1a = np.zeros((HID_T, 128), np.float32)
        b1g = np.zeros((HID_T, 128), np.float32)
        b1a.reshape(-1)[:HID] = b1[0:HID]
        b1g.reshape(-1)[:HID] = b1[HID:2 * HID]
        g[f"b1a_f{l}"] = np.ascontiguousarray(b1a.T)
        g[f"b1g_f{l}"] = np.ascontiguousarray(b1g.T)
        w2p = np.zeros((HID_T * 128, D), np.float32)
        w2p[:HID] = np.asarray(inputs["L0_ffn_w2"])[l]
        g[f"w2_f{l}"] = w2p
        g[f"b2_f{l}"] = np.ascontiguousarray(
            np.asarray(inputs["L0_ffn_b2"])[l].reshape(1, D))
    return g



def build(debug_taps=(), phases=("p0", "p1", "p2a", "p2b", "p3")):
    nc = bass.Bass("TRN2", target_bir_lowering=False)
    dr = {}
    dr["x"] = nc.dram_tensor("x", [S, D], F32, kind="ExternalInput").ap()
    f32_ins = {
        "wq_a": [D, D], "wk_a": [D, D], "wv_a": [D, D], "wout_a": [D, D],
        "bout_a": [1, D], "gamma_a": [1, D], "beta_a": [1, D],
        "mask": [128, 640],
        "tqc": [128, NT * HD], "tqs": [128, NT * HD],
        "tkc": [128, NT * HD], "tks": [128, NT * HD],
    }
    for l in range(2):
        f32_ins.update({
            f"wq_f{l}": [D, INNER], f"wk_f{l}": [D, INNER], f"wv_f{l}": [D, INNER],
            f"qs_f{l}": [1, HD], f"ks_f{l}": [1, HD],
            f"wo_f{l}": [INNER, D], f"lg_f{l}": [1, D], f"lb_f{l}": [1, D],
            f"w1a_f{l}": [D, HID], f"w1g_f{l}": [D, HID],
            f"b1a_f{l}": [128, HID_T], f"b1g_f{l}": [128, HID_T],
            f"w2_f{l}": [HID_T * 128, D], f"b2_f{l}": [1, D],
        })
    for n, shp in f32_ins.items():
        dr[n] = nc.dram_tensor(n, shp, F32, kind="ExternalInput").ap()
    dr["ident"] = nc.dram_tensor("ident", [128, 128], BF16, kind="ExternalInput").ap()
    dr["onesrow"] = nc.dram_tensor("onesrow", [1, 64], BF16, kind="ExternalInput").ap()
    dr["out"] = nc.dram_tensor("out", [S, D], F32, kind="ExternalOutput").ap()

    scr = {
        "tp": nc.dram_tensor("scr_tp", [S, D], BF16, kind="Internal").ap(),
        "tq": nc.dram_tensor("scr_tq", [S, INNER], BF16, kind="Internal").ap(),
        "tk": nc.dram_tensor("scr_tk", [S, INNER], BF16, kind="Internal").ap(),
        "xn": nc.dram_tensor("scr_xn", [S, D], F32, kind="Internal").ap(),
        "s": nc.dram_tensor("scr_s", [S, D], F32, kind="Internal").ap(),
    }
    tap_shapes = {
        "xn": ([S, D], F32), "sout": ([S, D], F32),
        "qTa": ([128, KT * S], BF16), "kTa": ([128, KT * S], BF16),
        "va": ([128, NT * HA * 65], BF16),
        "qf": ([128, NT * INNER], BF16), "kf": ([128, NT * INNER], BF16),
        "vf": ([128, NT * INNER], BF16), "of": ([128, 4 * S], BF16),
        "z1": ([S, D], F32), "z2": ([S, D], F32), "zf": ([S, D], F32),
        "xnT": ([128, KT * (S + 16)], BF16),
        "vb": ([128, NT * HA * 65], BF16),
        "h2T": ([128, KT * S], BF16),
    }
    taps = {}
    for n in debug_taps:
        shp, dt = tap_shapes[n]
        taps[n] = nc.dram_tensor("tap_" + n, shp, dt, kind="ExternalOutput").ap()

    r = Rec(nc)
    for s_ in ("qS", "qA", "qG", "tE", "vE", "aE", "gE",
               "bG0", "bG1", "bG2", "bG3", "bS0", "bS1", "bS2", "bS3", "bA0", "bA1"):
        r.sem(s_)
    avail = nc.sbuf_bytes_remaining // 4 - 16
    ar = Arena(nc, avail)
    P = PsumMgr(nc)
    b = _B(nc, r, ar, P, dr, scr, taps, set(debug_taps))
    b.prelude()
    if "p0" in phases:
        b.p0()
    if "p1" in phases:
        b.p1()
    if "p2a" in phases:
        b.p2(0)
    if "p2b" in phases:
        b.p2(1)
    if "p3" in phases:
        b.p3()
    b.fin()
    blk = nc.Block().__enter__()
    r.replay(blk)
    blk.__exit__(None, None, None)
    return nc


class _B:
    def __init__(self, nc, r, ar, P, dr, scr, taps, debug):
        self.nc, self.r, self.ar, self.P = nc, r, ar, P
        self.dr, self.scr, self.taps, self.debug = dr, scr, taps, debug
        self.B = {}

    def alloc(self, name, elems, dtype):
        self.B[name] = self.ar.alloc(name, elems, dtype)
        return self.B[name]

    def free(self, *names):
        for n in names:
            self.ar.free(n)
            del self.B[n]

    def tap(self, name, src_ap):
        if name in self.debug:
            self.r.fence(engs=("scalar",))
            self.r.dma("scalar", self.taps[name], src_ap)

    def prelude(self):
        r, dr = self.r, self.dr
        idb = self.alloc("ident", 128, BF16)
        ob = self.alloc("ones64", 64, BF16)
        mk = self.alloc("mask", 640, F32)
        r.dma("sync", idb[:, 0:128], dr["ident"])
        r.dma("sync", ob[64:65, 0:64], dr["onesrow"])
        r.dma("sync", mk[:, 0:640], dr["mask"])
        for n in ("gamma_a", "beta_a", "bout_a"):
            buf = self.alloc(n, D, F32)
            r.dma("sync", buf[:, 0:D], dr[n].broadcast_to([128, D]))

    def p0(self):
        r, dr, scr = self.r, self.dr, self.scr
        x = self.alloc("xw", NT * D, F32)
        xt = x.rearrange("p (t d) -> p t d", t=NT)
        st = self.alloc("st0", 4 * NT, F32)
        st3 = st.rearrange("p (a t) -> p a t", a=4)
        sq = self.alloc("sq0", D, F32)
        for t in range(NT):
            ev = r.dma_b("scalar", "bA0", xt[:, t, :], dr["x"][t * 128:(t + 1) * 128, :])
            r.wait("scalar", ev)
            r.op("scalar", lambda e, t=t: e.activation(
                sq[:, 0:D], xt[:, t, :], AF.Square, accum_out=st3[:, 1, t:t + 1]))
            r.op("scalar", lambda e, t=t: e.activation(
                sq[:, 0:D], xt[:, t, :], AF.Copy, accum_out=st3[:, 0, t:t + 1]),
                inc="aE")
        r.wait("vector", r.marker("aE"))
        r.opc("vector", lambda e: e.tensor_scalar(
            st3[:, 0, :], st3[:, 0, :], 1.0 / D, None, ALU.mult))
        r.opc("vector", lambda e: e.tensor_scalar(
            st3[:, 1, :], st3[:, 1, :], 1.0 / D, None, ALU.mult))
        r.opc("vector", lambda e: e.tensor_tensor(
            st3[:, 2, :], st3[:, 0, :], st3[:, 0, :], ALU.mult))
        r.opc("vector", lambda e: e.tensor_tensor(
            st3[:, 1, :], st3[:, 1, :], st3[:, 2, :], ALU.subtract))
        ev = r.op("vector", lambda e: e.tensor_scalar(
            st3[:, 1, :], st3[:, 1, :], LN_EPS, None, ALU.add), inc="vE")
        r.wait("scalar", ev)
        r.opc("scalar", lambda e: e.activation(st3[:, 2, :], st3[:, 1, :], AF.Ln))
        ev = r.op("scalar", lambda e: e.activation(
            st3[:, 2, :], st3[:, 2, :], AF.Exp, scale=-0.5), inc="aE")
        r.wait("vector", ev)
        for t in range(NT):
            ev = r.op("vector", lambda e, t=t: e.tensor_scalar(
                xt[:, t, :], xt[:, t, :], st3[:, 0, t:t + 1], st3[:, 2, t:t + 1],
                ALU.subtract, ALU.mult), inc="vE")
            r.wait("scalar", ev)
            r.wait("gpsimd", ev)
            r.dma("scalar", scr["xn"][t * 128:(t + 1) * 128, :], xt[:, t, :])
            r.dma("gpsimd", scr["tp"][t * 128:(t + 1) * 128, :], xt[:, t, :])
        self.tap("xn", x[:, 0:NT * D])
        dump = r.marker("qG")
        xnT = self.alloc("xnT", KT * (S + 16), BF16)
        xT3 = xnT.rearrange("p (k s) -> p k s", k=KT)
        r.op("vector", lambda e: e.memset(xT3[:, :, 0:16], 0.0), inc="vE")
        r.wait("sync", dump)
        for k in range(KT):
            r.dmat("sync", xT3[:, k, 16:S + 16], scr["tp"][:, k * 128:(k + 1) * 128])
        self.tap("xnT", xnT[:, 0:KT * (S + 16)])
        self.free("xw", "st0", "sq0")

    def p1(self):
        r, dr = self.r, self.dr
        r.fence()
        xT3 = self.B["xnT"].rearrange("p (k s) -> p k s", k=KT)
        qT = self.alloc("qTa", KT * S, BF16)
        kT = self.alloc("kTa", KT * S, BF16)
        va = self.alloc("va", NT * HA * 65, BF16)
        qT3 = qT.rearrange("p (k s) -> p k s", k=KT)
        kT3 = kT.rearrange("p (k s) -> p k s", k=KT)
        va3 = va.rearrange("p (t f) -> p t f", t=NT)
        va4 = va.rearrange("p (t h c) -> p t h c", t=NT, h=HA)
        r.op("vector", lambda e: e.memset(va4[:, :, :, 64:65], 1.0), inc="vE")

        wqk = self.alloc("wqk", 16 * 128, BF16)
        wqk3 = wqk.rearrange("p (b c) -> p b c", b=16)
        wv = self.alloc("wv", 16 * 512, BF16)
        wv3 = wv.rearrange("p (b c) -> p b c", b=16)
        slot_ev = {}
        ps = [self.P.alloc(f"pq{i}", 1) for i in range(4)]
        pev = [None] * 4
        pi = [0]
        r.wait("tensor", r.marker("qS"))

        def qk_part(wname, dstT, scale):
            for m in range(8):
                slots = []
                bs = "bG" + str(m % 2)
                for k in range(KT):
                    i = (m * KT + k) % 16
                    r.wait("gpsimd", slot_ev.get(("qk", i)))
                    r.dma_b("gpsimd", bs, wqk3[:, i, :],
                            dr[wname][k * 128:(k + 1) * 128, m * 128:(m + 1) * 128])
                    slots.append((i, None))
                batch_ev = r.marker(bs)
                for c in range(4):
                    p = pi[0] % 4
                    pi[0] += 1
                    r.wait("tensor", pev[p])
                    r.wait("tensor", batch_ev)
                    for k in range(KT):
                        i, ev = slots[k]
                        r.op("tensor", lambda e, p=p, i=i, k=k, c=c: e.matmul(
                            ps[p][:, 0:512], wqk3[:, i, :],
                            xT3[:, k, 16 + c * 512:16 + (c + 1) * 512],
                            start=(k == 0), stop=(k == KT - 1)),
                            inc="tE" if k == KT - 1 else None)
                    mm = r.marker("tE")
                    r.wait("vector", mm)
                    if scale == 1.0:
                        pev[p] = r.op("vector", lambda e, p=p, m=m, c=c: e.tensor_copy(
                            dstT[:, m, c * 512:(c + 1) * 512], ps[p][:, 0:512]),
                            inc="vE")
                    else:
                        pev[p] = r.op("vector", lambda e, p=p, m=m, c=c: e.tensor_scalar(
                            dstT[:, m, c * 512:(c + 1) * 512], ps[p][:, 0:512],
                            scale, None, ALU.mult), inc="vE")
                for (i, _e) in slots:
                    slot_ev[("qk", i)] = r.marker("tE")

        qk_part("wq_a", qT3, 0.125)
        qk_part("wk_a", kT3, 1.0)

        for nch in range(2):
            slots = []
            bs = "bG" + str(2 + nch)
            for k in range(KT):
                i = nch * 8 + k
                r.dma_b("gpsimd", bs, wv3[:, i, :],
                        dr["wv_a"][k * 128:(k + 1) * 128, nch * 512:(nch + 1) * 512])
                slots.append((i, None))
            batch_ev = r.marker(bs)
            for t in range(NT):
                p = pi[0] % 4
                pi[0] += 1
                r.wait("tensor", pev[p])
                r.wait("tensor", batch_ev)
                for k in range(KT):
                    i, ev = slots[k]
                    r.op("tensor", lambda e, p=p, i=i, k=k, t=t: e.matmul(
                        ps[p][:, 0:512], xT3[:, k, 16 + t * 128:16 + (t + 1) * 128],
                        wv3[:, i, :], start=(k == 0), stop=(k == KT - 1)),
                        inc="tE" if k == KT - 1 else None)
                mm = r.marker("tE")
                r.wait("vector", mm)
                dst = va4[:, t, nch * 8:(nch + 1) * 8, 0:64]
                srcv = ps[p][:, 0:512].rearrange("p (h c) -> p h c", h=8)
                pev[p] = r.op("vector", lambda e, dst=dst, srcv=srcv: e.tensor_copy(
                    dst, srcv), inc="vE")
            for (i, _e) in slots:
                slot_ev[("wv", i)] = r.marker("tE")

        for i in range(4):
            self.P.free(f"pq{i}")
        self.free("wqk", "wv")
        self.tap("qTa", self.B["qTa"][:, 0:KT * S])
        self.tap("kTa", self.B["kTa"][:, 0:KT * S])
        self.tap("va", self.B["va"][:, 0:NT * HA * 65])
        r.fence()
        self.free("xnT")

        wob = self.alloc("wob", KT * D, BF16)
        wo3 = wob.rearrange("p (k n) -> p k n", k=KT)
        for k in range(KT):
            r.dma_b("gpsimd", "bG0", wo3[:, k, :],
                    dr["wout_a"][k * 128:(k + 1) * 128, :])
        wout_ev = r.marker("bG0")

        wst = self.alloc("wst", 2 * 512, BF16)
        ws3 = wst.rearrange("p (b c) -> p b c", b=2)
        oT = self.alloc("oT", 2 * 8 * 512, BF16)
        oT4 = oT.rearrange("p (b k c) -> p b k c", b=2, k=8)
        rr = self.alloc("rr", 512, F32)
        rrb = self.alloc("rrb", 512, BF16)
        ub = self.alloc("ub", 512, BF16)
        xnc = self.alloc("xnc", 2 * 4 * D, F32)
        xc4 = xnc.rearrange("p (b t d) -> p b t d", b=2, t=4)
        yw = self.alloc("yw", D, F32)
        tw = self.alloc("tw", D, F32)
        sw = self.alloc("sw", D, F32)
        sta = self.alloc("sta", 8, F32)

        ps_sc = [self.P.alloc(f"sc{i}", 1) for i in range(2)]
        ps_u = [self.P.alloc(f"u{i}", 1) for i in range(2)]
        ps_bc = self.P.alloc("bc", 1)
        ps_at = self.P.alloc("at", 2)

        sc_guard = [None, None]
        ws_guard = [None, None]
        u_guard = [None, None]
        oT_guard = [None, None]
        bc_guard = None
        rr_guard = None
        at_guard = None
        yw_guard = None
        tw_guard = None
        sw_guard = None

        ones64v = self.B["ones64"]
        boutv = self.B["bout_a"]
        gammav = self.B["gamma_a"]
        betav = self.B["beta_a"]
        xnc_guard = [None, None]
        for c in range(4):
            ob = c % 2
            r.wait("vector", oT_guard[ob])
            r.wait("scalar", xnc_guard[ob])
            for t in range(4):
                r.dma_b("scalar", "bA" + str(ob), xc4[:, ob, t, :],
                        self.scr["xn"][(4 * c + t) * 128:(4 * c + t + 1) * 128, :])
            xn_ev = r.marker("bA" + str(ob))
            for h in range(HA):
                kc, kp = h // 2, (h % 2) * 64
                up = h % 2
                r.wait("tensor", u_guard[up])
                for t in range(NT):
                    sp = t % 2
                    r.wait("tensor", sc_guard[sp])
                    ev_mm = r.op("tensor", lambda e, kc=kc, kp=kp, t=t, c=c, sp=sp: e.matmul(
                        ps_sc[sp][:, 0:512],
                        kT3[kp:kp + 64, kc, t * 128:(t + 1) * 128],
                        qT3[kp:kp + 64, kc, c * 512:(c + 1) * 512],
                        start=True, stop=True, tile_position=(kp, 0)), inc="tE")
                    eng = "vector" if t % 2 == 0 else "scalar"
                    r.wait(eng, ev_mm)
                    r.wait(eng, ws_guard[sp])
                    if eng == "vector":
                        ev_ev = r.op("vector", lambda e, sp=sp: e.tensor_scalar(
                            ws3[:, sp, :], ps_sc[sp][:, 0:512], 0.0, None,
                            ALU.max), inc="vE")
                    else:
                        ev_ev = r.op("scalar", lambda e, sp=sp: e.activation(
                            ws3[:, sp, :], ps_sc[sp][:, 0:512], AF.Relu), inc="aE")
                    r.wait("tensor", ev_ev)
                    r.op("tensor", lambda e, h=h, t=t, sp=sp, up=up: e.matmul(
                        ps_u[up][0:65, 0:512], va3[:, t, 65 * h:65 * h + 65],
                        ws3[:, sp, :], start=(t == 0), stop=(t == NT - 1)),
                        inc="tE")
                    sc_guard[sp] = r.marker("tE")
                    ws_guard[sp] = sc_guard[sp]
                u_done = r.marker("tE")
                r.wait("vector", u_done)
                r.wait("vector", rr_guard)
                r.op("vector", lambda e, up=up: e.tensor_scalar(
                    rr[64:65, 0:512], ps_u[up][64:65, 0:512], 1e-6, None, ALU.add))
                with_lp = self.nc.allow_low_precision
                def _recip(e):
                    with with_lp("bf16 normalizer is a matmul operand"):
                        return e.reciprocal(rrb[64:65, 0:512], rr[64:65, 0:512])
                ev_r = r.op("vector", _recip, inc="vE")
                u_guard[up] = r.op("vector", lambda e, up=up: e.tensor_copy(
                    ub[0:64, 0:512], ps_u[up][0:64, 0:512]), inc="vE")
                r.wait("tensor", ev_r)
                r.wait("tensor", bc_guard)
                ev_bc = r.op("tensor", lambda e: e.matmul(
                    ps_bc[0:64, 0:512], ones64v[64:65, 0:64],
                    rrb[64:65, 0:512], start=True, stop=True), inc="tE")
                rr_guard = ev_bc
                r.wait("vector", ev_bc)
                dsto = oT4[:, ob, h // 2, :][kp:kp + 64, :]
                bc_guard = r.op("vector", lambda e, dsto=dsto: e.tensor_tensor(
                    dsto, ub[0:64, 0:512], ps_bc[0:64, 0:512], ALU.mult), inc="vE")
            oT_ready = r.marker("vE")

            r.wait("tensor", wout_ev)
            r.wait("tensor", oT_ready)
            for t in range(4):
                stile = 4 * c + t
                r.wait("tensor", at_guard)
                for k in range(KT):
                    for nch in range(2):
                        r.op("tensor", lambda e, ob=ob, k=k, t=t, nch=nch: e.matmul(
                            ps_at[:, nch * 512:(nch + 1) * 512],
                            oT4[:, ob, k, t * 128:(t + 1) * 128],
                            wo3[:, k, nch * 512:(nch + 1) * 512],
                            start=(k == 0), stop=(k == KT - 1)),
                            inc="tE" if (k == KT - 1 and nch == 1) else None)
                mm = r.marker("tE")
                r.wait("vector", mm)
                r.wait("vector", xn_ev)
                r.wait("vector", yw_guard)
                at_guard = r.op("vector", lambda e: e.scalar_tensor_tensor(
                    yw[:, 0:D], ps_at[:, 0:1024], 1.0, boutv[:, 0:D],
                    ALU.mult, ALU.add), inc="vE")
                r.wait("gpsimd", at_guard)
                ev_y = r.op("gpsimd", lambda e, ob=ob, t=t: e.tensor_tensor(
                    yw[:, 0:D], yw[:, 0:D], xc4[:, ob, t, :], ALU.add),
                    inc="gE", inc_n=1)
                xnc_guard[ob] = ev_y
                r.wait("scalar", ev_y)
                r.wait("scalar", tw_guard)
                r.op("scalar", lambda e: e.activation(
                    tw[:, 0:D], yw[:, 0:D], AF.Square, accum_out=sta[:, 1:2]))
                ev = r.op("scalar", lambda e: e.activation(
                    tw[:, 0:D], yw[:, 0:D], AF.Copy, accum_out=sta[:, 0:1]),
                    inc="aE")
                r.wait("vector", ev)
                r.opc("vector", lambda e: e.tensor_scalar(
                    sta[:, 0:1], sta[:, 0:1], 1.0 / D, None, ALU.mult))
                r.opc("vector", lambda e: e.tensor_scalar(
                    sta[:, 2:3], sta[:, 1:2], 1.0 / D, None, ALU.mult))
                r.opc("vector", lambda e: e.tensor_tensor(
                    sta[:, 3:4], sta[:, 0:1], sta[:, 0:1], ALU.mult))
                r.opc("vector", lambda e: e.tensor_tensor(
                    sta[:, 2:3], sta[:, 2:3], sta[:, 3:4], ALU.subtract))
                ev = r.op("vector", lambda e: e.tensor_scalar(
                    sta[:, 2:3], sta[:, 2:3], LN_EPS, None, ALU.add), inc="vE")
                r.wait("scalar", ev)
                r.opc("scalar", lambda e: e.activation(sta[:, 2:3], sta[:, 2:3], AF.Ln))
                ev = r.op("scalar", lambda e: e.activation(
                    sta[:, 2:3], sta[:, 2:3], AF.Exp, scale=-0.5), inc="aE")
                r.wait("vector", ev)
                r.wait("vector", tw_guard)
                yw_guard = r.op("vector", lambda e: e.tensor_scalar(
                    tw[:, 0:D], yw[:, 0:D], sta[:, 0:1], sta[:, 2:3],
                    ALU.subtract, ALU.mult), inc="vE")
                r.wait("gpsimd", yw_guard)
                r.wait("gpsimd", sw_guard)
                tw_guard = r.op("gpsimd", lambda e: e.tensor_tensor(
                    sw[:, 0:D], tw[:, 0:D], gammav[:, 0:D], ALU.mult),
                    inc="gE", inc_n=1)
                ev = r.op("gpsimd", lambda e: e.tensor_tensor(
                    sw[:, 0:D], sw[:, 0:D], betav[:, 0:D], ALU.add),
                    inc="gE", inc_n=1)
                r.wait("sync", ev)
                sw_guard = r.dma("sync", self.scr["s"][stile * 128:(stile + 1) * 128, :],
                                 sw[:, 0:D])
            oT_guard[ob] = r.marker("tE")

        self.tap("vb", self.B["va"][:, 0:NT * HA * 65])
        if "sout" in self.debug:
            r.fence(engs=("scalar",))
            r.dma("scalar", self.taps["sout"], self.scr["s"])
        r.fence()
        for i in range(2):
            self.P.free(f"sc{i}")
            self.P.free(f"u{i}")
        self.P.free("bc")
        self.P.free("at")
        self.free("qTa", "kTa", "va", "wob", "wst", "oT", "rr", "rrb", "ub",
                  "xnc", "yw", "tw", "sw", "sta")
        self.free("gamma_a", "beta_a", "bout_a")

    def p2(self, l):
        r, dr, scr = self.r, self.dr, self.scr
        r.fence()
        if l == 0:
            self.alloc("z", NT * D, F32)
        z = self.B["z"]
        zt = z.rearrange("p (t d) -> p t d", t=NT)
        if l == 0:
            for t in range(NT):
                r.dma("scalar", zt[:, t, :], scr["xn"][t * 128:(t + 1) * 128, :])
        else:
            for t in range(NT):
                r.dma("gpsimd", scr["tp"][t * 128:(t + 1) * 128, :], zt[:, t, :])
        zT = self.alloc("zT", KT * (S + 16), BF16)
        zT3 = zT.rearrange("p (k s) -> p k s", k=KT)
        zcol_ev = r.op("vector", lambda e: e.memset(zT3[:, :, 0:16], 0.0), inc="vE")
        r.wait("sync", r.marker("qG"))
        for k in range(KT):
            r.op("sync", lambda e, k=k: e.dma_start(
                zT3[:, k, 16:S + 16], scr["tp"][:, k * 128:(k + 1) * 128],
                transpose=True), inc="bS1", inc_n=16, is_dma=True)
        zT_ev = r.marker("bS1")
        z_ev = r.marker("qA")

        qs = self.alloc("qs8", HD, F32)
        ks = self.alloc("ks8", HD, F32)
        r.dma_b("sync", "bS0", qs[:, 0:HD], dr[f"qs_f{l}"].broadcast_to([128, HD]))
        r.dma_b("sync", "bS0", ks[:, 0:HD], dr[f"ks_f{l}"].broadcast_to([128, HD]))
        tabs = {}
        for tn in ("tqc", "tqs", "tkc", "tks"):
            tabs[tn] = self.alloc(tn, NT * HD, F32)
            r.dma_b("scalar", "bA0", tabs[tn][:, 0:NT * HD], dr[tn])
        tab3 = {tn: tabs[tn].rearrange("p (t d) -> p t d", t=NT) for tn in tabs}
        small_ev = r.marker("bS0")
        tab_ev = r.marker("bA0")
        r.wait("vector", small_ev)
        r.op("vector", lambda e: e.tensor_scalar(
            qs[:, 0:HD], qs[:, 0:HD], 8.0, None, ALU.mult), inc="vE")

        wb = {}
        for wn in ("wq", "wk", "wv"):
            buf = self.alloc(wn + "f", KT * 512, BF16)
            w3 = buf.rearrange("p (k c) -> p k c", k=KT)
            for k in range(KT):
                r.dma_b("gpsimd", "bG0", w3[:, k, :],
                        dr[f"{wn}_f{l}"][k * 128:(k + 1) * 128, :])
            wb[wn] = w3
        wf_ev = r.marker("bG0")

        qtok = self.alloc("qtok", NT * INNER, BF16)
        ktok = self.alloc("ktok", NT * INNER, BF16)
        vf = self.alloc("vf", NT * INNER, BF16)
        q3 = qtok.rearrange("p (t c) -> p t c", t=NT)
        k3 = ktok.rearrange("p (t c) -> p t c", t=NT)
        v3 = vf.rearrange("p (t c) -> p t c", t=NT)

        qw = self.alloc("qw", 512, F32)
        qq = self.alloc("qq", 512, F32)
        q1 = self.alloc("q1", 512, F32)
        t1 = self.alloc("t1", 512, F32)
        t2 = self.alloc("t2", 512, F32)
        nrm = self.alloc("nrm", 16, F32)

        ps = [self.P.alloc(f"pf{i}", 1) for i in range(4)]
        pev = [None] * 4
        pi = [0]
        r.wait("tensor", zT_ev)
        r.wait("tensor", zcol_ev)
        r.wait("tensor", wf_ev)
        r.wait("vector", tab_ev)
        chain = None

        for t in range(NT):
            for wn, dtok in (("wq", q3), ("wk", k3), ("wv", v3)):
                p = pi[0] % 4
                pi[0] += 1
                r.wait("tensor", pev[p])
                for k in range(KT):
                    coff = 16 if k < 4 else 15
                    r.op("tensor", lambda e, p=p, wn=wn, k=k, t=t, coff=coff: e.matmul(
                        ps[p][:, 0:512],
                        zT3[:, k, coff + t * 128:coff + t * 128 + 128],
                        wb[wn][:, k, :], start=(k == 0), stop=(k == KT - 1)),
                        inc="tE" if k == KT - 1 else None)
                mm = r.marker("tE")
                if wn == "wv":
                    r.wait("vector", mm)
                    pev[p] = r.op("vector", lambda e, p=p, t=t: e.tensor_copy(
                        v3[:, t, :], ps[p][:, 0:512]), inc="vE")
                    continue
                is_q = wn == "wq"
                sc = qs if is_q else ks
                ctn, stn = ("tqc", "tqs") if is_q else ("tkc", "tks")
                r.wait("vector", mm)
                r.wait("vector", chain)
                pev[p] = r.op("vector", lambda e, p=p: e.tensor_copy(
                    qw[:, 0:512], ps[p][:, 0:512]), inc="vE")
                r.wait("scalar", pev[p])
                ev1 = r.op("scalar", lambda e: e.activation(
                    qq[:, 0:512], qw[:, 0:512], AF.Square), inc="aE")
                r.wait("vector", ev1)
                r.opc("vector", lambda e: e.reduce_sum(
                    nrm[:, 0:8], qq[:, 0:512].rearrange("p (h d) -> p h d", h=8),
                    axis=AX.X))
                ev2 = r.op("vector", lambda e: e.tensor_scalar(
                    nrm[:, 0:8], nrm[:, 0:8], 1e-24, None, ALU.max), inc="vE")
                r.wait("scalar", ev2)
                r.opc("scalar", lambda e: e.activation(nrm[:, 8:16], nrm[:, 0:8], AF.Ln))
                ev3 = r.op("scalar", lambda e: e.activation(
                    nrm[:, 8:16], nrm[:, 8:16], AF.Exp, scale=-0.5), inc="aE")
                r.wait("vector", ev3)
                r.op("vector", lambda e: e.tensor_tensor(
                    q1[:, 0:512].rearrange("p (h d) -> p h d", h=8),
                    qw[:, 0:512].rearrange("p (h d) -> p h d", h=8),
                    nrm[:, 8:16].rearrange("p (h o) -> p h o", o=1
                                           ).broadcast_to([128, 8, HD]),
                    ALU.mult))
                r.op("vector", lambda e, sc=sc: e.tensor_tensor(
                    qw[:, 0:512].rearrange("p (h d) -> p h d", h=8),
                    q1[:, 0:512].rearrange("p (h d) -> p h d", h=8),
                    sc[:, 0:HD].rearrange("p (o d) -> p o d", o=1
                                          ).broadcast_to([128, 8, HD]),
                    ALU.mult))
                r.op("vector", lambda e, ctn=ctn, t=t: e.tensor_tensor(
                    t1[:, 0:512].rearrange("p (h d) -> p h d", h=8),
                    qw[:, 0:512].rearrange("p (h d) -> p h d", h=8),
                    tab3[ctn][:, t, :].rearrange("p (o d) -> p o d", o=1
                                                 ).broadcast_to([128, 8, HD]),
                    ALU.mult))
                q14 = qw[:, 0:512].rearrange("p (h r d) -> p h r d", h=8, r=2)
                t24 = t2[:, 0:512].rearrange("p (h r d) -> p h r d", h=8, r=2)
                st4 = tab3[stn][:, t, :].rearrange("p (o r d) -> p o r d", o=1, r=2)
                r.op("vector", lambda e, q14=q14, t24=t24, st4=st4: e.tensor_tensor(
                    t24[:, :, 0:1, :], q14[:, :, 1:2, :],
                    st4[:, :, 0:1, :].broadcast_to([128, 8, 1, 32]), ALU.mult))
                r.op("vector", lambda e, q14=q14, t24=t24, st4=st4: e.tensor_tensor(
                    t24[:, :, 1:2, :], q14[:, :, 0:1, :],
                    st4[:, :, 1:2, :].broadcast_to([128, 8, 1, 32]), ALU.mult))
                chain = r.op("vector", lambda e, dtok=dtok, t=t: e.tensor_tensor(
                    dtok[:, t, :], t1[:, 0:512], t2[:, 0:512], ALU.add), inc="vE")

        for i in range(4):
            self.P.free(f"pf{i}")
        self.free("wqf", "wkf", "wvf", "qw", "qq", "q1", "t1", "t2",
                  "tqc", "tqs", "tkc", "tks", "qs8", "ks8")
        self.tap("qf", self.B["qtok"][:, 0:NT * INNER])
        self.tap("kf", self.B["ktok"][:, 0:NT * INNER])
        self.tap("vf", self.B["vf"][:, 0:NT * INNER])
        r.fence()
        self.free("zT")

        for t in range(NT):
            r.dma("scalar", scr["tq"][t * 128:(t + 1) * 128, :], q3[:, t, :])
            r.dma("scalar", scr["tk"][t * 128:(t + 1) * 128, :], k3[:, t, :])
        dmp = r.marker("qA")
        qfT = self.alloc("qfT", 4 * S, BF16)
        kfT = self.alloc("kfT", 4 * S, BF16)
        qfT3 = qfT.rearrange("p (k s) -> p k s", k=4)
        kfT3 = kfT.rearrange("p (k s) -> p k s", k=4)
        r.wait("sync", dmp)
        for k in range(4):
            r.op("sync", lambda e, k=k: e.dma_start(
                qfT3[:, k, :], scr["tq"][:, k * 128:(k + 1) * 128],
                transpose=True), inc="bS1", inc_n=16, is_dma=True)
            r.op("sync", lambda e, k=k: e.dma_start(
                kfT3[:, k, :], scr["tk"][:, k * 128:(k + 1) * 128],
                transpose=True), inc="bS1", inc_n=16, is_dma=True)
        qkT_ev = r.marker("bS1")
        r.fence(engs=("vector", "scalar", "gpsimd", "tensor"), sems=("qA",))
        self.free("qtok", "ktok")

        wob = self.alloc("wobf", 4 * D, BF16)
        wo3 = wob.rearrange("p (k n) -> p k n", k=4)
        for k in range(4):
            r.dma_b("gpsimd", "bG1", wo3[:, k, :],
                    dr[f"wo_f{l}"][k * 128:(k + 1) * 128, :])
        wo_ev = r.marker("bG1")

        of = self.alloc("of", 4 * S, BF16)
        of3 = of.rearrange("p (k s) -> p k s", k=4)
        msc = self.alloc("msc", 2 * 640, F32)
        msc3 = msc.rearrange("p (b c) -> p b c", b=2)
        attn = self.alloc("attn", 2 * 640, BF16)
        attn3 = attn.rearrange("p (b c) -> p b c", b=2)
        atT = self.alloc("atT", 2 * 640, BF16)
        atT3 = atT.rearrange("p (b c) -> p b c", b=2)
        cols = self.alloc("cols", 8, F32)

        ps_sc = [self.P.alloc(f"fs{i}", 2) for i in range(2)]
        ps_tp = [self.P.alloc(f"ft{i}", 1, BF16) for i in range(2)]
        ps_o = [self.P.alloc(f"fo{i}", 1) for i in range(2)]

        pssc_guard = [None, None]
        msc_guard = [None, None]
        attn_guard = [None, None]
        pstp_guard = [None, None]
        atT_guard = [None, None]
        pso_guard = [None, None]

        r.wait("tensor", qkT_ev)
        mkv = self.B["mask"]
        identv = self.B["ident"]

        for h in range(HF):
            kc, kp = h // 2, (h % 2) * 64
            for a in range(NT):
                wmax = min(a, 4)
                W = (wmax + 1) * 128
                k0 = a - wmax
                par = (h * NT + a) % 2
                r.wait("tensor", pssc_guard[par])
                nleft, off = W, 0
                while nleft > 0:
                    n = min(512, nleft)
                    r.op("tensor", lambda e, kc=kc, kp=kp, a=a, k0=k0, off=off, n=n, par=par: e.matmul(
                        ps_sc[par][:, off:off + n],
                        qfT3[kp:kp + 64, kc, a * 128:(a + 1) * 128],
                        kfT3[kp:kp + 64, kc, k0 * 128 + off:k0 * 128 + off + n],
                        start=True, stop=True, tile_position=(kp, 0)),
                        inc="tE" if n == nleft else None)
                    off += n
                    nleft -= n
                mm = r.marker("tE")
                r.wait("vector", mm)
                r.wait("vector", msc_guard[par])
                pssc_guard[par] = r.op("vector", lambda e, par=par, W=W: e.scalar_tensor_tensor(
                    msc3[:, par, 0:W], ps_sc[par][:, 0:W], 1.0,
                    mkv[:, 640 - W:640], ALU.mult, ALU.add), inc="vE")
                r.opc("vector", lambda e, par=par, W=W: e.reduce_max(
                    cols[:, 0:1], msc3[:, par, 0:W], axis=AX.X))
                ev_neg = r.op("vector", lambda e: e.tensor_scalar(
                    cols[:, 1:2], cols[:, 0:1], -1.0, None, ALU.mult), inc="vE")
                r.wait("scalar", ev_neg)
                r.wait("scalar", attn_guard[par])
                ev_ex = r.op("scalar", lambda e, par=par, W=W: e.activation(
                    attn3[:, par, 0:W], msc3[:, par, 0:W], AF.Exp,
                    bias=cols[:, 1:2], accum_out=cols[:, 2:3]), inc="aE")
                msc_guard[par] = ev_ex
                r.wait("vector", ev_ex)
                r.opc("vector", lambda e: e.reciprocal(cols[:, 3:4], cols[:, 2:3]))
                ev_nm = r.op("vector", lambda e, par=par, W=W: e.tensor_scalar(
                    attn3[:, par, 0:W], attn3[:, par, 0:W], cols[:, 3:4], None,
                    ALU.mult), inc="vE")
                r.wait("tensor", ev_nm)
                r.wait("tensor", pstp_guard[par])
                for j in range(wmax + 1):
                    r.op("tensor", lambda e, par=par, j=j: e.transpose(
                        ps_tp[par][:, j * 128:(j + 1) * 128],
                        attn3[:, par, j * 128:(j + 1) * 128],
                        identv[:, 0:128]),
                        inc="tE" if j == wmax else None)
                attn_guard[par] = r.marker("tE")
                r.wait("vector", attn_guard[par])
                r.wait("vector", atT_guard[par])
                ev_tT = r.op("vector", lambda e, par=par, W=W: e.tensor_copy(
                    atT3[:, par, 0:W], ps_tp[par][:, 0:W]), inc="vE")
                pstp_guard[par] = ev_tT
                r.wait("tensor", pso_guard[par])
                r.wait("tensor", ev_tT)
                for j in range(wmax + 1):
                    kt = k0 + j
                    r.op("tensor", lambda e, h=h, kt=kt, j=j, par=par, wmax=wmax: e.matmul(
                        ps_o[par][0:64, 0:128], v3[:, kt, 64 * h:64 * h + 64],
                        atT3[:, par, j * 128:(j + 1) * 128],
                        start=(j == 0), stop=(j == wmax)),
                        inc="tE" if j == wmax else None)
                atT_guard[par] = r.marker("tE")
                r.wait("vector", atT_guard[par])
                pso_guard[par] = r.op("vector", lambda e, kp=kp, kc=kc, a=a, par=par: e.tensor_copy(
                    of3[:, kc, a * 128:(a + 1) * 128][kp:kp + 64, :],
                    ps_o[par][0:64, 0:128]), inc="vE")

        of_ev = r.marker("vE")
        for i in range(2):
            self.P.free(f"fs{i}")
            self.P.free(f"ft{i}")
            self.P.free(f"fo{i}")
        if l == 0:
            self.tap("of", self.B["of"][:, 0:4 * S])
        self.free("msc", "attn", "atT", "cols", "qfT", "kfT", "vf")

        ps_d = [self.P.alloc(f"pd{i}", 2) for i in range(2)]
        dguard = [None, None]
        r.wait("tensor", of_ev)
        r.wait("tensor", wo_ev)
        r.wait("vector", z_ev)
        for t in range(NT):
            par = t % 2
            r.wait("tensor", dguard[par])
            for k in range(4):
                for nch in range(2):
                    r.op("tensor", lambda e, k=k, t=t, nch=nch, par=par: e.matmul(
                        ps_d[par][:, nch * 512:(nch + 1) * 512],
                        of3[:, k, t * 128:(t + 1) * 128],
                        wo3[:, k, nch * 512:(nch + 1) * 512],
                        start=(k == 0), stop=(k == 3)),
                        inc="tE" if (k == 3 and nch == 1) else None)
            mm = r.marker("tE")
            r.wait("vector", mm)
            dguard[par] = r.op("vector", lambda e, t=t, par=par: e.tensor_tensor(
                zt[:, t, :], zt[:, t, :], ps_d[par][:, 0:1024], ALU.add), inc="vE")
        for i in range(2):
            self.P.free(f"pd{i}")
        self.free("of", "wobf")
        if l == 0:
            self.tap("z1", z[:, 0:NT * D])
        r.fence()

        lg = self.alloc("lg", D, F32)
        lb = self.alloc("lb", D, F32)
        r.dma_b("sync", "bS0", lg[:, 0:D], dr[f"lg_f{l}"].broadcast_to([128, D]))
        r.dma_b("sync", "bS0", lb[:, 0:D], dr[f"lb_f{l}"].broadcast_to([128, D]))
        lgb_ev = r.marker("bS0")
        sh = self.alloc("sh", 2 * 512, F32)
        sh3 = sh.rearrange("p (b c) -> p b c", b=2)
        h2b = self.alloc("h2b", 2 * D, BF16)
        h2b3 = h2b.rearrange("p (b c) -> p b c", b=2)
        aw = self.alloc("aw", D, F32)
        stc = self.alloc("stc", 8, F32)
        sh_guard = [None, None]
        h2b_guard = [None, None]
        aw_guard = None
        r.wait("gpsimd", lgb_ev)
        for t in range(NT):
            par = t % 2
            r.wait("sync", sh_guard[par])
            bsh = "bS" + str(2 + par)
            if t == 0:
                ev0 = r.op("vector", lambda e, par=par: e.memset(
                    sh3[0:1, par, :], 0.0), inc="vE")
                r.wait("sync", ev0)
            else:
                r.dma_b("sync", bsh, sh3[0:1, par, :], zt[127:128, t - 1, 512:1024])
            r.dma_b("sync", bsh, sh3[1:128, par, :], zt[0:127, t, 512:1024])
            shd = r.marker(bsh)
            r.wait("scalar", shd)
            r.wait("scalar", aw_guard)
            r.op("scalar", lambda e, t=t: e.activation(
                aw[:, 0:512], zt[:, t, 0:512], AF.Copy, accum_out=stc[:, 0:1]))
            r.op("scalar", lambda e, par=par: e.activation(
                aw[:, 0:512], sh3[:, par, :], AF.Copy, accum_out=stc[:, 1:2]))
            r.op("scalar", lambda e, t=t: e.activation(
                aw[:, 0:512], zt[:, t, 0:512], AF.Square, accum_out=stc[:, 2:3]))
            ev = r.op("scalar", lambda e, par=par: e.activation(
                aw[:, 0:512], sh3[:, par, :], AF.Square, accum_out=stc[:, 3:4]),
                inc="aE")
            r.wait("vector", ev)
            r.opc("vector", lambda e: e.tensor_tensor(
                stc[:, 0:1], stc[:, 0:1], stc[:, 1:2], ALU.add))
            r.opc("vector", lambda e: e.tensor_tensor(
                stc[:, 2:3], stc[:, 2:3], stc[:, 3:4], ALU.add))
            r.opc("vector", lambda e: e.tensor_scalar(
                stc[:, 0:1], stc[:, 0:1], 1.0 / D, None, ALU.mult))
            r.opc("vector", lambda e: e.tensor_scalar(
                stc[:, 2:3], stc[:, 2:3], 1.0 / D, None, ALU.mult))
            r.opc("vector", lambda e: e.tensor_tensor(
                stc[:, 4:5], stc[:, 0:1], stc[:, 0:1], ALU.mult))
            r.opc("vector", lambda e: e.tensor_tensor(
                stc[:, 2:3], stc[:, 2:3], stc[:, 4:5], ALU.subtract))
            ev = r.op("vector", lambda e: e.tensor_scalar(
                stc[:, 2:3], stc[:, 2:3], LN_EPS, None, ALU.add), inc="vE")
            r.wait("scalar", ev)
            r.opc("scalar", lambda e: e.activation(stc[:, 2:3], stc[:, 2:3], AF.Ln))
            ev = r.op("scalar", lambda e: e.activation(
                stc[:, 2:3], stc[:, 2:3], AF.Exp, scale=-0.5), inc="aE")
            r.wait("vector", ev)
            r.wait("vector", h2b_guard[par])
            r.op("vector", lambda e, t=t: e.tensor_scalar(
                aw[:, 0:512], zt[:, t, 0:512], stc[:, 0:1], stc[:, 2:3],
                ALU.subtract, ALU.mult))
            ev = r.op("vector", lambda e, par=par: e.tensor_scalar(
                aw[:, 512:1024], sh3[:, par, :], stc[:, 0:1], stc[:, 2:3],
                ALU.subtract, ALU.mult), inc="vE")
            r.wait("gpsimd", ev)
            r.op("gpsimd", lambda e: e.tensor_tensor(
                aw[:, 0:D], aw[:, 0:D], lg[:, 0:D], ALU.mult), inc="gE", inc_n=1)
            ev = r.op("gpsimd", lambda e, par=par: e.tensor_tensor(
                h2b3[:, par, :], aw[:, 0:D], lb[:, 0:D], ALU.add),
                inc="gE", inc_n=1)
            aw_guard = ev
            sh_guard[par] = ev
            r.wait("scalar", ev)
            h2b_guard[par] = r.dma_b(
                "scalar", "bA" + str(par),
                scr["tp"][t * 128:(t + 1) * 128, :], h2b3[:, par, :])
        r.fence(engs=("sync",), sems=("qA", "bA0", "bA1"))
        dmp = r.marker("qA")
        h2T = self.alloc("h2T", KT * S, BF16)
        h2T3 = h2T.rearrange("p (k s) -> p k s", k=KT)
        r.wait("sync", dmp)
        for k in range(KT):
            r.op("sync", lambda e, k=k: e.dma_start(
                h2T3[:, k, :], scr["tp"][:, k * 128:(k + 1) * 128],
                transpose=True), inc="bS1", inc_n=16, is_dma=True)
        self.tap("h2T", self.B["h2T"][:, 0:KT * S])
        r.fence()
        self.free("lg", "lb", "sh", "h2b", "aw", "stc")

        b1a = self.alloc("b1a", HID_T, F32)
        b1g = self.alloc("b1g", HID_T, F32)
        b2 = self.alloc("b2", D, F32)
        r.dma_b("sync", "bS0", b1a[:, 0:HID_T], dr[f"b1a_f{l}"])
        r.dma_b("sync", "bS0", b1g[:, 0:HID_T], dr[f"b1g_f{l}"])
        r.dma_b("sync", "bS0", b2[:, 0:D], dr[f"b2_f{l}"].broadcast_to([128, D]))
        bia_ev = r.marker("bS0")
        w1b = self.alloc("w1b", 16 * 128, BF16)
        w1b3 = w1b.rearrange("p (b c) -> p b c", b=16)
        w2b = self.alloc("w2b", 11 * 1024, BF16)
        w2b3 = w2b.rearrange("p (k n) -> p k n", k=11)
        h4 = self.alloc("h4", HID_T * 1024, BF16)
        h43 = h4.rearrange("p (k s) -> p k s", k=HID_T)
        gel = self.alloc("gel", 1024, F32)
        tmp2 = self.alloc("tmp2", D, F32)
        slot_ev = {}
        si = [0]
        r.wait("vector", bia_ev)
        r.wait("scalar", bia_ev)
        h2T_mark = r.marker("bS1")
        r.wait("tensor", h2T_mark)

        ps_a = self.P.alloc("pa", 2)
        ps_g = self.P.alloc("pg", 2)
        ps_w = [self.P.alloc(f"pw{i}", 2) for i in range(2)]
        pa_guard = None
        pg_guard = None
        tmp2_guard = None
        pw_guard = [None, None]
        h4_guard = {}
        w2slot_guard = {}

        for half in range(2):
            scol = half * 1024
            for kt in range(HID_T):
                rows = 128 if kt < HID_T - 1 else HID - 128 * (HID_T - 1)
                for ag in range(2):
                    wsrc = f"w1a_f{l}" if ag == 0 else f"w1g_f{l}"
                    psAG = ps_a if ag == 0 else ps_g
                    slots = []
                    bs = "bG" + str(ag)
                    for k in range(KT):
                        i = si[0] % 16
                        si[0] += 1
                        r.wait("gpsimd", slot_ev.get(i))
                        r.dma_b("gpsimd", bs, w1b3[:, i, 0:rows],
                                dr[wsrc][k * 128:(k + 1) * 128,
                                         kt * 128:kt * 128 + rows])
                        slots.append((i, None))
                    batch_ev = r.marker(bs)
                    r.wait("tensor", pa_guard if ag == 0 else pg_guard)
                    r.wait("tensor", batch_ev)
                    for k in range(KT):
                        i, ev = slots[k]
                        for nch in range(2):
                            r.op("tensor", lambda e, i=i, k=k, nch=nch, psAG=psAG, scol=scol, rows=rows: e.matmul(
                                psAG[0:rows, nch * 512:(nch + 1) * 512],
                                w1b3[:, i, 0:rows],
                                h2T3[:, k, scol + nch * 512:scol + (nch + 1) * 512],
                                start=(k == 0), stop=(k == KT - 1)),
                                inc="tE" if (k == KT - 1 and nch == 1) else None)
                    mm = r.marker("tE")
                    for (i, _e) in slots:
                        slot_ev[i] = mm
                    if ag == 0:
                        pa_mm = mm
                    else:
                        pg_mm = mm
                r.wait("scalar", pg_mm)
                r.wait("scalar", pa_guard)
                pg_guard = r.op("scalar", lambda e, kt=kt, rows=rows: e.activation(
                    gel[0:rows, 0:1024], ps_g[0:rows, 0:1024], AF.Gelu,
                    bias=b1g[0:rows, kt:kt + 1]), inc="aE")
                r.wait("vector", pg_guard)
                r.wait("vector", pa_mm)
                r.wait("vector", h4_guard.get(kt))
                pa_guard = r.op("vector", lambda e, kt=kt, rows=rows: e.scalar_tensor_tensor(
                    h43[0:rows, kt, 0:1024], ps_a[0:rows, 0:1024],
                    b1a[0:rows, kt:kt + 1], gel[0:rows, 0:1024],
                    ALU.add, ALU.mult), inc="vE")
            h4_done = r.marker("vE")

            for kh in range(2):
                kts = list(range(0, 11)) if kh == 0 else list(range(11, HID_T))
                bs = "bG" + str(2 + kh)
                for j, kt in enumerate(kts):
                    r.wait("gpsimd", w2slot_guard.get(j))
                    r.dma_b("gpsimd", bs, w2b3[:, j, :],
                            dr[f"w2_f{l}"][kt * 128:(kt + 1) * 128, :])
                batch_ev = r.marker(bs)
                r.wait("tensor", h4_done)
                r.wait("tensor", batch_ev)
                for ti in range(8):
                    t = half * 8 + ti
                    par = ti % 2
                    r.wait("tensor", pw_guard[par])
                    for j, kt in enumerate(kts):
                        for nch in range(2):
                            r.op("tensor", lambda e, j=j, kt=kt, ti=ti, nch=nch, par=par: e.matmul(
                                ps_w[par][:, nch * 512:(nch + 1) * 512],
                                h43[:, kt, ti * 128:(ti + 1) * 128],
                                w2b3[:, j, nch * 512:(nch + 1) * 512],
                                start=(j == 0), stop=(j == len(kts) - 1)),
                                inc="tE" if (j == len(kts) - 1 and nch == 1) else None)
                    mm = r.marker("tE")
                    r.wait("vector", mm)
                    if kh == 0:
                        pw_guard[par] = r.op("vector", lambda e, t=t, par=par: e.tensor_tensor(
                            zt[:, t, :], zt[:, t, :], ps_w[par][:, 0:1024], ALU.add),
                            inc="vE")
                    else:
                        r.wait("vector", tmp2_guard)
                        ev = r.op("vector", lambda e, par=par: e.scalar_tensor_tensor(
                            tmp2[:, 0:D], ps_w[par][:, 0:1024], 1.0, b2[:, 0:D],
                            ALU.mult, ALU.add), inc="vE")
                        r.wait("gpsimd", ev)
                        pw_guard[par] = r.op("gpsimd", lambda e, t=t: e.tensor_tensor(
                            zt[:, t, :], zt[:, t, :], tmp2[:, 0:D], ALU.add),
                            inc="gE", inc_n=1)
                        tmp2_guard = pw_guard[par]
                wmark = r.marker("tE")
                for j in range(len(kts)):
                    w2slot_guard[j] = wmark
            hm = r.marker("tE")
            for kt in range(HID_T):
                h4_guard[kt] = hm

        self.P.free("pa")
        self.P.free("pg")
        for i in range(2):
            self.P.free(f"pw{i}")
        r.fence()
        self.free("b1a", "b1g", "b2", "w1b", "w2b", "h4", "gel", "tmp2", "h2T")
        self.tap("z2" if l == 0 else "zf", z[:, 0:NT * D])

    def p3(self):
        r, dr, scr = self.r, self.dr, self.scr
        r.fence()
        zt = self.B["z"].rearrange("p (t d) -> p t d", t=NT)
        xb = self.alloc("xb3", 2 * D, F32)
        xb3 = xb.rearrange("p (b c) -> p b c", b=2)
        sb = self.alloc("sb3", 2 * D, F32)
        sb3 = sb.rearrange("p (b c) -> p b c", b=2)
        gw = self.alloc("gw3", D, F32)
        ow = self.alloc("ow3", D, F32)
        buf_guard = [None, None]
        ow_guard = None
        gw_guard = None
        for t in range(NT):
            par = t % 2
            r.wait("scalar", buf_guard[par])
            e1 = r.dma_b("scalar", "bA" + str(par), xb3[:, par, :],
                         dr["x"][t * 128:(t + 1) * 128, :])
            e2 = r.dma_b("scalar", "bA" + str(par), sb3[:, par, :],
                         scr["s"][t * 128:(t + 1) * 128, :])
            r.wait("scalar", gw_guard)
            ev_g = r.op("scalar", lambda e, t=t: e.activation(
                gw[:, 0:D], zt[:, t, :], AF.Silu), inc="aE")
            r.wait("vector", ev_g)
            r.wait("vector", e2)
            r.wait("vector", ow_guard)
            gw_guard = r.op("vector", lambda e, par=par: e.tensor_tensor(
                ow[:, 0:D], gw[:, 0:D], sb3[:, par, :], ALU.mult), inc="vE")
            ev_o = r.op("vector", lambda e, par=par: e.tensor_tensor(
                ow[:, 0:D], ow[:, 0:D], xb3[:, par, :], ALU.add), inc="vE")
            r.wait("sync", ev_o)
            st_ev = r.dma("sync", dr["out"][t * 128:(t + 1) * 128, :], ow[:, 0:D])
            ow_guard = st_ev
            buf_guard[par] = st_ev

    def fin(self):
        self.r.fence(engs=("sync",))



def kernel(**inputs):
    g = host_layout(inputs)
    consts = host_constants()
    nc = build()
    x = np.asarray(inputs["L0_attn_x"], dtype=np.float32)
    in_maps = []
    for core in range(8):
        m = {"x": np.ascontiguousarray(x[core])}
        m.update(g)
        m.update(consts)
        in_maps.append(m)
    res = run_bass_kernel_spmd(nc, in_maps, core_ids=list(range(8)))
    return np.stack([r["out"] for r in res.results], axis=0)


# version 03
# speedup vs baseline: 1.1613x; 1.1613x over previous
import numpy as np
import ml_dtypes
import concourse.bass as bass
import concourse.mybir as mybir
from concourse.bass_utils import run_bass_kernel_spmd

F32 = mybir.dt.float32
BF16 = mybir.dt.bfloat16
AF = mybir.ActivationFunctionType
ALU = mybir.AluOpType
AX = mybir.AxisListType

S = 2048
NT = 16
D = 1024
KT = 8
HA = 16
HF = 8
HD = 64
INNER = 512
HID = 2730
HID_T = 22
WIN = 512
NEG = -3.0e38
LN_EPS = 1e-5

ENGINES = ("tensor", "vector", "scalar", "sync", "gpsimd")


class Rec:
    def __init__(self, nc):
        self.nc = nc
        self.ops = {e: [] for e in ENGINES}
        self.sems = {}
        self.cnt = {}
        self.last_wait = {e: {} for e in ENGINES}

    def sem(self, name):
        self.sems[name] = self.nc.semaphore(name).__enter__()
        self.cnt[name] = 0

    SELF_SEM = {"vector": "vE", "scalar": "aE", "gpsimd": "gE"}

    def op(self, eng, fn, inc=None, inc_n=1, is_dma=False):
        own = None if is_dma else self.SELF_SEM.get(eng)
        if own is not None:
            self.wait(eng, (own, self.cnt[own]))
            if inc is None:
                inc = own
        if inc is None:
            self.ops[eng].append(fn)
            return None
        self.cnt[inc] += inc_n
        tgt = self.cnt[inc]
        sem = self.sems[inc]
        if own is not None and inc != own:
            self.cnt[own] += 1
            tgt2 = self.cnt[own]
            sem2 = self.sems[own]
            self.ops[eng].append(
                lambda e, f=fn, s=sem, n=inc_n, s2=sem2: f(e).then_inc(s, n).then_inc(s2, 1))
            return (inc, tgt)
        self.ops[eng].append(lambda e, f=fn, s=sem, n=inc_n: f(e).then_inc(s, n))
        return (inc, tgt)

    def dma_b(self, q, bsem, dst, src, **kw):
        return self.op(q, lambda e: e.dma_start(dst, src, **kw), inc=bsem,
                       inc_n=16, is_dma=True)

    def dma(self, q, dst, src, **kw):
        inc = {"sync": "qS", "scalar": "qA", "gpsimd": "qG"}[q]
        return self.op(q, lambda e: e.dma_start(dst, src, **kw), inc=inc,
                       inc_n=16, is_dma=True)

    def dmat(self, q, dst, src):
        inc = {"sync": "qS", "scalar": "qA"}[q]
        return self.op(q, lambda e: e.dma_start(dst, src, transpose=True),
                       inc=inc, inc_n=16, is_dma=True)

    def wait(self, eng, ev):
        if ev is None:
            return
        sname, val = ev
        if sname in ("qS", "qA", "qG"):
            val = self.cnt[sname]
        lw = self.last_wait[eng]
        if lw.get(sname, -1) >= val:
            return
        lw[sname] = val
        sem = self.sems[sname]
        self.ops[eng].append(lambda e, s=sem, v=val: e.wait_ge(s, v))

    def marker(self, sname):
        return (sname, self.cnt[sname])

    def opc(self, eng, fn):
        inc = {"vector": "vE", "scalar": "aE", "gpsimd": "qG"}[eng]
        ev = self.op(eng, fn, inc=inc)
        self.wait(eng, ev)
        return ev

    def fence(self, engs=ENGINES, sems=("qS", "qA", "qG", "tE", "vE", "aE", "gE",
                                        "bG0", "bG1", "bG2", "bG3", "bS0", "bS1",
                                        "bS2", "bS3", "bA0", "bA1")):
        marks = [self.marker(s) for s in sems if self.cnt[s] > 0]
        for e in engs:
            for m in marks:
                self.wait(e, m)

    def replay(self, block):
        for eng in ENGINES:
            ops = self.ops[eng]

            def body(e, ops=ops):
                for f in ops:
                    f(e)

            getattr(block, eng)(body)


class Arena:
    def __init__(self, nc, elems_f32):
        self.t = nc.alloc_sbuf_tensor("arena", [128, elems_f32], F32)
        self.free_list = [(0, elems_f32)]
        self.allocs = {}

    def alloc(self, name, elems, dtype):
        f32e = elems if dtype == F32 else (elems + 1) // 2
        align = 16
        for i, (off, size) in enumerate(self.free_list):
            aoff = (off + align - 1) // align * align
            pad = aoff - off
            if size >= f32e + pad:
                rest = size - f32e - pad
                frags = []
                if pad:
                    frags.append((off, pad))
                if rest:
                    frags.append((aoff + f32e, rest))
                self.free_list[i:i + 1] = frags
                self.allocs[name] = (aoff, f32e)
                ap = self.t[:, aoff:aoff + f32e]
                if dtype == BF16:
                    ap = ap.bitcast(BF16)[:, 0:elems]
                return ap
        raise MemoryError(f"arena: cannot alloc {name} ({f32e}); free={self.free_list}")

    def free(self, name):
        off, elems = self.allocs.pop(name)
        self.free_list.append((off, elems))
        self.free_list.sort()
        merged = []
        for o, sz in self.free_list:
            if merged and merged[-1][0] + merged[-1][1] == o:
                merged[-1] = (merged[-1][0], merged[-1][1] + sz)
            else:
                merged.append([o, sz])
        self.free_list = [tuple(x) for x in merged]


class PsumMgr:
    def __init__(self, nc):
        self.t = nc.alloc_psum_tensor("psum", [128, 4096], F32)
        self.free_banks = set(range(8))
        self.allocs = {}

    def alloc(self, name, banks, dtype=F32):
        for b0 in range(8 - banks + 1):
            if all(b0 + j in self.free_banks for j in range(banks)):
                for j in range(banks):
                    self.free_banks.discard(b0 + j)
                self.allocs[name] = (b0, banks)
                ap = self.t[:, b0 * 512:(b0 + banks) * 512]
                return ap.bitcast(BF16) if dtype == BF16 else ap
        raise MemoryError(f"psum {name}: need {banks}, free={sorted(self.free_banks)}")

    def free(self, name):
        b0, banks = self.allocs.pop(name)
        for j in range(banks):
            self.free_banks.add(b0 + j)



def host_constants():
    c = {}
    c["ident"] = np.eye(128, dtype=ml_dtypes.bfloat16)
    c["onesrow"] = np.ones((1, 64), dtype=ml_dtypes.bfloat16)
    r = np.arange(128)[:, None]
    cc = np.arange(640)[None, :]
    c["mask"] = np.where((cc >= r) & (cc <= r + 512), 0.0, NEG).astype(np.float32)
    half = np.arange(0, HD, 2, dtype=np.float64)
    inv_freq = 1.0 / 10000.0 ** (half / HD)
    t = np.arange(S, dtype=np.float64)
    freqs = t[:, None] * inv_freq[None, :]
    freqs = np.concatenate([freqs, freqs], axis=-1)
    base = (half + 0.4 * HD) / (1.4 * HD)
    power = (t - S // 2) / float(WIN // 2)
    scale = base[None, :] ** power[:, None]
    scale = np.concatenate([scale, scale], axis=-1)
    cos, sin = np.cos(freqs), np.sin(freqs)
    sign = np.concatenate([-np.ones(32), np.ones(32)])[None, :]

    def tok(x):
        return np.ascontiguousarray(
            x.reshape(NT, 128, HD).transpose(1, 0, 2).reshape(128, NT * HD)
        ).astype(np.float32)

    c["tqc"] = tok(cos * scale)
    c["tqs"] = tok(sin * scale * sign)
    c["tkc"] = tok(cos / scale)
    c["tks"] = tok(sin / scale * sign)
    return c


def host_layout(inputs):
    g = {}
    w3 = np.asarray(inputs["L0_attn_w_qkv"]).reshape(D, HA, 3, HD)
    g["wq_a"] = np.ascontiguousarray(w3[:, :, 0, :].reshape(D, D))
    g["wk_a"] = np.ascontiguousarray(w3[:, :, 1, :].reshape(D, D))
    g["wv_a"] = np.ascontiguousarray(w3[:, :, 2, :].reshape(D, D))
    g["wout_a"] = np.ascontiguousarray(np.asarray(inputs["L0_attn_w_out"]))
    g["bout_a"] = np.ascontiguousarray(np.asarray(inputs["L0_attn_b_out"]).reshape(1, D))
    g["gamma_a"] = np.ascontiguousarray(np.asarray(inputs["L0_attn_gamma"]).reshape(1, D))
    g["beta_a"] = np.ascontiguousarray(np.asarray(inputs["L0_attn_beta"]).reshape(1, D))
    for l in range(2):
        wf = np.asarray(inputs["L0_ffn_wqkv"])[l]
        g[f"wq_f{l}"] = np.ascontiguousarray(wf[:, 0:INNER])
        g[f"wk_f{l}"] = np.ascontiguousarray(wf[:, INNER:2 * INNER])
        g[f"wv_f{l}"] = np.ascontiguousarray(wf[:, 2 * INNER:3 * INNER])
        g[f"qs_f{l}"] = np.ascontiguousarray(
            np.asarray(inputs["L0_ffn_q_scale"])[l].reshape(1, HD))
        g[f"ks_f{l}"] = np.ascontiguousarray(
            np.asarray(inputs["L0_ffn_k_scale"])[l].reshape(1, HD))
        g[f"wo_f{l}"] = np.ascontiguousarray(np.asarray(inputs["L0_ffn_wo"])[l])
        g[f"lg_f{l}"] = np.ascontiguousarray(
            np.asarray(inputs["L0_ffn_ln_g"])[l].reshape(1, D))
        g[f"lb_f{l}"] = np.ascontiguousarray(
            np.asarray(inputs["L0_ffn_ln_b"])[l].reshape(1, D))
        w1 = np.asarray(inputs["L0_ffn_w1"])[l]
        g[f"w1a_f{l}"] = np.ascontiguousarray(w1[:, 0:HID])
        g[f"w1g_f{l}"] = np.ascontiguousarray(w1[:, HID:2 * HID])
        b1 = np.asarray(inputs["L0_ffn_b1"])[l]
        b1a = np.zeros((HID_T, 128), np.float32)
        b1g = np.zeros((HID_T, 128), np.float32)
        b1a.reshape(-1)[:HID] = b1[0:HID]
        b1g.reshape(-1)[:HID] = b1[HID:2 * HID]
        g[f"b1a_f{l}"] = np.ascontiguousarray(b1a.T)
        g[f"b1g_f{l}"] = np.ascontiguousarray(b1g.T)
        w2p = np.zeros((HID_T * 128, D), np.float32)
        w2p[:HID] = np.asarray(inputs["L0_ffn_w2"])[l]
        g[f"w2_f{l}"] = w2p
        g[f"b2_f{l}"] = np.ascontiguousarray(
            np.asarray(inputs["L0_ffn_b2"])[l].reshape(1, D))
    return g



def build(debug_taps=(), phases=("p0", "p1", "p2a", "p2b", "p3")):
    nc = bass.Bass("TRN2", target_bir_lowering=False)
    dr = {}
    dr["x"] = nc.dram_tensor("x", [S, D], F32, kind="ExternalInput").ap()
    f32_ins = {
        "wq_a": [D, D], "wk_a": [D, D], "wv_a": [D, D], "wout_a": [D, D],
        "bout_a": [1, D], "gamma_a": [1, D], "beta_a": [1, D],
        "mask": [128, 640],
        "tqc": [128, NT * HD], "tqs": [128, NT * HD],
        "tkc": [128, NT * HD], "tks": [128, NT * HD],
    }
    for l in range(2):
        f32_ins.update({
            f"wq_f{l}": [D, INNER], f"wk_f{l}": [D, INNER], f"wv_f{l}": [D, INNER],
            f"qs_f{l}": [1, HD], f"ks_f{l}": [1, HD],
            f"wo_f{l}": [INNER, D], f"lg_f{l}": [1, D], f"lb_f{l}": [1, D],
            f"w1a_f{l}": [D, HID], f"w1g_f{l}": [D, HID],
            f"b1a_f{l}": [128, HID_T], f"b1g_f{l}": [128, HID_T],
            f"w2_f{l}": [HID_T * 128, D], f"b2_f{l}": [1, D],
        })
    for n, shp in f32_ins.items():
        dr[n] = nc.dram_tensor(n, shp, F32, kind="ExternalInput").ap()
    dr["ident"] = nc.dram_tensor("ident", [128, 128], BF16, kind="ExternalInput").ap()
    dr["onesrow"] = nc.dram_tensor("onesrow", [1, 64], BF16, kind="ExternalInput").ap()
    dr["out"] = nc.dram_tensor("out", [S, D], F32, kind="ExternalOutput").ap()

    scr = {
        "tp": nc.dram_tensor("scr_tp", [S, D], BF16, kind="Internal").ap(),
        "tq": nc.dram_tensor("scr_tq", [S, INNER], BF16, kind="Internal").ap(),
        "tk": nc.dram_tensor("scr_tk", [S, INNER], BF16, kind="Internal").ap(),
        "xn": nc.dram_tensor("scr_xn", [S, D], F32, kind="Internal").ap(),
        "s": nc.dram_tensor("scr_s", [S, D], F32, kind="Internal").ap(),
    }
    tap_shapes = {
        "xn": ([S, D], F32), "sout": ([S, D], F32),
        "qTa": ([128, KT * S], BF16), "kTa": ([128, KT * S], BF16),
        "va": ([128, NT * HA * 65], BF16),
        "qf": ([128, NT * INNER], BF16), "kf": ([128, NT * INNER], BF16),
        "vf": ([128, NT * INNER], BF16), "of": ([128, 4 * S], BF16),
        "z1": ([S, D], F32), "z2": ([S, D], F32), "zf": ([S, D], F32),
        "xnT": ([128, KT * (S + 16)], BF16),
        "vb": ([128, NT * HA * 65], BF16),
        "h2T": ([128, KT * S], BF16),
    }
    taps = {}
    for n in debug_taps:
        shp, dt = tap_shapes[n]
        taps[n] = nc.dram_tensor("tap_" + n, shp, dt, kind="ExternalOutput").ap()

    r = Rec(nc)
    for s_ in ("qS", "qA", "qG", "tE", "vE", "aE", "gE",
               "bG0", "bG1", "bG2", "bG3", "bS0", "bS1", "bS2", "bS3", "bA0", "bA1"):
        r.sem(s_)
    avail = nc.sbuf_bytes_remaining // 4 - 16
    ar = Arena(nc, avail)
    P = PsumMgr(nc)
    b = _B(nc, r, ar, P, dr, scr, taps, set(debug_taps))
    b.prelude()
    if "p0" in phases:
        b.p0()
    if "p1" in phases:
        b.p1()
    if "p2a" in phases:
        b.p2(0)
    if "p2b" in phases:
        b.p2(1)
    if "p3" in phases:
        b.p3()
    b.fin()
    blk = nc.Block().__enter__()
    r.replay(blk)
    blk.__exit__(None, None, None)
    return nc


class _B:
    def __init__(self, nc, r, ar, P, dr, scr, taps, debug):
        self.nc, self.r, self.ar, self.P = nc, r, ar, P
        self.dr, self.scr, self.taps, self.debug = dr, scr, taps, debug
        self.B = {}

    def alloc(self, name, elems, dtype):
        self.B[name] = self.ar.alloc(name, elems, dtype)
        return self.B[name]

    def free(self, *names):
        for n in names:
            self.ar.free(n)
            del self.B[n]

    def tap(self, name, src_ap):
        if name in self.debug:
            self.r.fence(engs=("scalar",))
            self.r.dma("scalar", self.taps[name], src_ap)

    def prelude(self):
        r, dr = self.r, self.dr
        idb = self.alloc("ident", 128, BF16)
        ob = self.alloc("ones64", 64, BF16)
        mk = self.alloc("mask", 640, F32)
        r.dma("sync", idb[:, 0:128], dr["ident"])
        r.dma("sync", ob[64:65, 0:64], dr["onesrow"])
        r.dma("sync", mk[:, 0:640], dr["mask"])
        for n in ("gamma_a", "beta_a", "bout_a"):
            buf = self.alloc(n, D, F32)
            r.dma("sync", buf[:, 0:D], dr[n].broadcast_to([128, D]))

    def p0(self):
        r, dr, scr = self.r, self.dr, self.scr
        x = self.alloc("xw", NT * D, F32)
        xt = x.rearrange("p (t d) -> p t d", t=NT)
        st = self.alloc("st0", 4 * NT, F32)
        st3 = st.rearrange("p (a t) -> p a t", a=4)
        sq = self.alloc("sq0", D, F32)
        for t in range(NT):
            r.dma_b("scalar", "bA0", xt[:, t, :], dr["x"][t * 128:(t + 1) * 128, :])
        r.wait("scalar", r.marker("bA0"))
        for t in range(NT):
            r.op("scalar", lambda e, t=t: e.activation(
                sq[:, 0:D], xt[:, t, :], AF.Square, accum_out=st3[:, 1, t:t + 1]))
            r.op("scalar", lambda e, t=t: e.activation(
                sq[:, 0:D], xt[:, t, :], AF.Copy, accum_out=st3[:, 0, t:t + 1]),
                inc="aE")
        r.wait("vector", r.marker("aE"))
        r.opc("vector", lambda e: e.tensor_scalar(
            st3[:, 0, :], st3[:, 0, :], 1.0 / D, None, ALU.mult))
        r.opc("vector", lambda e: e.tensor_scalar(
            st3[:, 1, :], st3[:, 1, :], 1.0 / D, None, ALU.mult))
        r.opc("vector", lambda e: e.tensor_tensor(
            st3[:, 2, :], st3[:, 0, :], st3[:, 0, :], ALU.mult))
        r.opc("vector", lambda e: e.tensor_tensor(
            st3[:, 1, :], st3[:, 1, :], st3[:, 2, :], ALU.subtract))
        ev = r.op("vector", lambda e: e.tensor_scalar(
            st3[:, 1, :], st3[:, 1, :], LN_EPS, None, ALU.add), inc="vE")
        r.wait("scalar", ev)
        r.opc("scalar", lambda e: e.activation(st3[:, 2, :], st3[:, 1, :], AF.Ln))
        ev = r.op("scalar", lambda e: e.activation(
            st3[:, 2, :], st3[:, 2, :], AF.Exp, scale=-0.5), inc="aE")
        r.wait("vector", ev)
        for t in range(NT):
            ev = r.op("vector", lambda e, t=t: e.tensor_scalar(
                xt[:, t, :], xt[:, t, :], st3[:, 0, t:t + 1], st3[:, 2, t:t + 1],
                ALU.subtract, ALU.mult), inc="vE")
            r.wait("scalar", ev)
            r.wait("gpsimd", ev)
            r.dma("scalar", scr["xn"][t * 128:(t + 1) * 128, :], xt[:, t, :])
            r.dma("gpsimd", scr["tp"][t * 128:(t + 1) * 128, :], xt[:, t, :])
        self.tap("xn", x[:, 0:NT * D])
        dump = r.marker("qG")
        xnT = self.alloc("xnT", KT * (S + 16), BF16)
        xT3 = xnT.rearrange("p (k s) -> p k s", k=KT)
        r.op("vector", lambda e: e.memset(xT3[:, :, 0:16], 0.0), inc="vE")
        r.wait("sync", dump)
        for k in range(KT):
            r.dmat("sync", xT3[:, k, 16:S + 16], scr["tp"][:, k * 128:(k + 1) * 128])
        self.tap("xnT", xnT[:, 0:KT * (S + 16)])
        self.free("xw", "st0", "sq0")

    def p1(self):
        r, dr = self.r, self.dr
        r.fence()
        xT3 = self.B["xnT"].rearrange("p (k s) -> p k s", k=KT)
        qT = self.alloc("qTa", KT * S, BF16)
        kT = self.alloc("kTa", KT * S, BF16)
        va = self.alloc("va", NT * HA * 65, BF16)
        qT3 = qT.rearrange("p (k s) -> p k s", k=KT)
        kT3 = kT.rearrange("p (k s) -> p k s", k=KT)
        va3 = va.rearrange("p (t f) -> p t f", t=NT)
        va4 = va.rearrange("p (t h c) -> p t h c", t=NT, h=HA)
        r.op("vector", lambda e: e.memset(va4[:, :, :, 64:65], 1.0), inc="vE")

        wqk = self.alloc("wqk", 16 * 128, BF16)
        wqk3 = wqk.rearrange("p (b c) -> p b c", b=16)
        wv = self.alloc("wv", 16 * 512, BF16)
        wv3 = wv.rearrange("p (b c) -> p b c", b=16)
        slot_ev = {}
        ps = [self.P.alloc(f"pq{i}", 1) for i in range(4)]
        pev = [None] * 4
        pi = [0]
        r.wait("tensor", r.marker("qS"))

        def qk_part(wname, dstT, scale):
            for m in range(8):
                slots = []
                bs = "bG" + str(m % 2)
                for k in range(KT):
                    i = (m * KT + k) % 16
                    r.wait("gpsimd", slot_ev.get(("qk", i)))
                    r.dma_b("gpsimd", bs, wqk3[:, i, :],
                            dr[wname][k * 128:(k + 1) * 128, m * 128:(m + 1) * 128])
                    slots.append((i, None))
                batch_ev = r.marker(bs)
                for c in range(4):
                    p = pi[0] % 4
                    pi[0] += 1
                    r.wait("tensor", pev[p])
                    r.wait("tensor", batch_ev)
                    for k in range(KT):
                        i, ev = slots[k]
                        r.op("tensor", lambda e, p=p, i=i, k=k, c=c: e.matmul(
                            ps[p][:, 0:512], wqk3[:, i, :],
                            xT3[:, k, 16 + c * 512:16 + (c + 1) * 512],
                            start=(k == 0), stop=(k == KT - 1)),
                            inc="tE" if k == KT - 1 else None)
                    mm = r.marker("tE")
                    r.wait("vector", mm)
                    if scale == 1.0:
                        pev[p] = r.op("vector", lambda e, p=p, m=m, c=c: e.tensor_copy(
                            dstT[:, m, c * 512:(c + 1) * 512], ps[p][:, 0:512]),
                            inc="vE")
                    else:
                        pev[p] = r.op("vector", lambda e, p=p, m=m, c=c: e.tensor_scalar(
                            dstT[:, m, c * 512:(c + 1) * 512], ps[p][:, 0:512],
                            scale, None, ALU.mult), inc="vE")
                for (i, _e) in slots:
                    slot_ev[("qk", i)] = r.marker("tE")

        qk_part("wq_a", qT3, 0.125)
        qk_part("wk_a", kT3, 1.0)

        for nch in range(2):
            slots = []
            bs = "bG" + str(2 + nch)
            for k in range(KT):
                i = nch * 8 + k
                r.dma_b("gpsimd", bs, wv3[:, i, :],
                        dr["wv_a"][k * 128:(k + 1) * 128, nch * 512:(nch + 1) * 512])
                slots.append((i, None))
            batch_ev = r.marker(bs)
            for t in range(NT):
                p = pi[0] % 4
                pi[0] += 1
                r.wait("tensor", pev[p])
                r.wait("tensor", batch_ev)
                for k in range(KT):
                    i, ev = slots[k]
                    r.op("tensor", lambda e, p=p, i=i, k=k, t=t: e.matmul(
                        ps[p][:, 0:512], xT3[:, k, 16 + t * 128:16 + (t + 1) * 128],
                        wv3[:, i, :], start=(k == 0), stop=(k == KT - 1)),
                        inc="tE" if k == KT - 1 else None)
                mm = r.marker("tE")
                r.wait("vector", mm)
                dst = va4[:, t, nch * 8:(nch + 1) * 8, 0:64]
                srcv = ps[p][:, 0:512].rearrange("p (h c) -> p h c", h=8)
                pev[p] = r.op("vector", lambda e, dst=dst, srcv=srcv: e.tensor_copy(
                    dst, srcv), inc="vE")
            for (i, _e) in slots:
                slot_ev[("wv", i)] = r.marker("tE")

        for i in range(4):
            self.P.free(f"pq{i}")
        self.free("wqk", "wv")
        self.tap("qTa", self.B["qTa"][:, 0:KT * S])
        self.tap("kTa", self.B["kTa"][:, 0:KT * S])
        self.tap("va", self.B["va"][:, 0:NT * HA * 65])
        r.fence()
        self.free("xnT")

        wob = self.alloc("wob", KT * D, BF16)
        wo3 = wob.rearrange("p (k n) -> p k n", k=KT)
        for k in range(KT):
            r.dma_b("gpsimd", "bG0", wo3[:, k, :],
                    dr["wout_a"][k * 128:(k + 1) * 128, :])
        wout_ev = r.marker("bG0")

        wst = self.alloc("wst", 3 * 512, BF16)
        ws3 = wst.rearrange("p (b c) -> p b c", b=3)
        oT = self.alloc("oT", 2 * 8 * 512, BF16)
        oT4 = oT.rearrange("p (b k c) -> p b k c", b=2, k=8)
        rr = self.alloc("rr", 512, F32)
        rrb = self.alloc("rrb", 512, BF16)
        ub = self.alloc("ub", 512, BF16)
        xnc = self.alloc("xnc", 2 * 4 * D, F32)
        xc4 = xnc.rearrange("p (b t d) -> p b t d", b=2, t=4)
        yw = self.alloc("yw", D, F32)
        tw = self.alloc("tw", D, F32)
        sw = self.alloc("sw", D, F32)
        sta = self.alloc("sta", 8, F32)

        ps_sc = [self.P.alloc(f"sc{i}", 1) for i in range(3)]
        ps_u = [self.P.alloc(f"u{i}", 1) for i in range(2)]
        ps_bc = self.P.alloc("bc", 1)
        ps_at = self.P.alloc("at", 2)

        sc_guard = [None, None, None]
        ws_guard = [None, None, None]
        u_guard = [None, None]
        oT_guard = [None, None]
        bc_guard = None
        rr_guard = None
        at_guard = None
        yw_guard = None
        tw_guard = None
        sw_guard = None

        ones64v = self.B["ones64"]
        boutv = self.B["bout_a"]
        gammav = self.B["gamma_a"]
        betav = self.B["beta_a"]
        xnc_guard = [None, None]
        for c in range(4):
            ob = c % 2
            r.wait("vector", oT_guard[ob])
            r.wait("scalar", xnc_guard[ob])
            for t in range(4):
                r.dma_b("scalar", "bA" + str(ob), xc4[:, ob, t, :],
                        self.scr["xn"][(4 * c + t) * 128:(4 * c + t + 1) * 128, :])
            xn_ev = r.marker("bA" + str(ob))
            for h in range(HA):
                kc, kp = h // 2, (h % 2) * 64
                up = h % 2
                r.wait("tensor", u_guard[up])
                ev_evs = [None] * NT

                def emit_scores(t, sp):
                    r.wait("tensor", sc_guard[sp])
                    ev_mm = r.op("tensor", lambda e, kc=kc, kp=kp, t=t, sp=sp, c=c: e.matmul(
                        ps_sc[sp][:, 0:512],
                        kT3[kp:kp + 64, kc, t * 128:(t + 1) * 128],
                        qT3[kp:kp + 64, kc, c * 512:(c + 1) * 512],
                        start=True, stop=True, tile_position=(kp, 0)), inc="tE")
                    eng = "vector" if t % 2 == 0 else "scalar"
                    r.wait(eng, ev_mm)
                    r.wait(eng, ws_guard[sp])
                    if eng == "vector":
                        ev_evs[t] = r.op("vector", lambda e, sp=sp: e.tensor_scalar(
                            ws3[:, sp, :], ps_sc[sp][:, 0:512], 0.0, None,
                            ALU.max), inc="vE")
                    else:
                        ev_evs[t] = r.op("scalar", lambda e, sp=sp: e.activation(
                            ws3[:, sp, :], ps_sc[sp][:, 0:512], AF.Relu), inc="aE")

                def emit_av(t, sp):
                    r.wait("tensor", ev_evs[t])
                    r.op("tensor", lambda e, h=h, t=t, sp=sp, up=up: e.matmul(
                        ps_u[up][0:65, 0:512], va3[:, t, 65 * h:65 * h + 65],
                        ws3[:, sp, :], start=(t == 0), stop=(t == NT - 1)),
                        inc="tE")
                    sc_guard[sp] = r.marker("tE")
                    ws_guard[sp] = sc_guard[sp]

                emit_scores(0, 0)
                emit_scores(1, 1)
                for t in range(NT):
                    if t + 2 < NT:
                        emit_scores(t + 2, (t + 2) % 3)
                    emit_av(t, t % 3)
                u_done = r.marker("tE")
                r.wait("vector", u_done)
                r.wait("vector", rr_guard)
                r.op("vector", lambda e, up=up: e.tensor_scalar(
                    rr[64:65, 0:512], ps_u[up][64:65, 0:512], 1e-6, None, ALU.add))
                with_lp = self.nc.allow_low_precision
                def _recip(e):
                    with with_lp("bf16 normalizer is a matmul operand"):
                        return e.reciprocal(rrb[64:65, 0:512], rr[64:65, 0:512])
                ev_r = r.op("vector", _recip, inc="vE")
                u_guard[up] = r.op("vector", lambda e, up=up: e.tensor_copy(
                    ub[0:64, 0:512], ps_u[up][0:64, 0:512]), inc="vE")
                r.wait("tensor", ev_r)
                r.wait("tensor", bc_guard)
                ev_bc = r.op("tensor", lambda e: e.matmul(
                    ps_bc[0:64, 0:512], ones64v[64:65, 0:64],
                    rrb[64:65, 0:512], start=True, stop=True), inc="tE")
                rr_guard = ev_bc
                r.wait("vector", ev_bc)
                dsto = oT4[:, ob, h // 2, :][kp:kp + 64, :]
                bc_guard = r.op("vector", lambda e, dsto=dsto: e.tensor_tensor(
                    dsto, ub[0:64, 0:512], ps_bc[0:64, 0:512], ALU.mult), inc="vE")
            oT_ready = r.marker("vE")

            r.wait("tensor", wout_ev)
            r.wait("tensor", oT_ready)
            for t in range(4):
                stile = 4 * c + t
                r.wait("tensor", at_guard)
                for k in range(KT):
                    for nch in range(2):
                        r.op("tensor", lambda e, ob=ob, k=k, t=t, nch=nch: e.matmul(
                            ps_at[:, nch * 512:(nch + 1) * 512],
                            oT4[:, ob, k, t * 128:(t + 1) * 128],
                            wo3[:, k, nch * 512:(nch + 1) * 512],
                            start=(k == 0), stop=(k == KT - 1)),
                            inc="tE" if (k == KT - 1 and nch == 1) else None)
                mm = r.marker("tE")
                r.wait("vector", mm)
                r.wait("vector", xn_ev)
                r.wait("vector", yw_guard)
                at_guard = r.op("vector", lambda e: e.scalar_tensor_tensor(
                    yw[:, 0:D], ps_at[:, 0:1024], 1.0, boutv[:, 0:D],
                    ALU.mult, ALU.add), inc="vE")
                r.wait("gpsimd", at_guard)
                ev_y = r.op("gpsimd", lambda e, ob=ob, t=t: e.tensor_tensor(
                    yw[:, 0:D], yw[:, 0:D], xc4[:, ob, t, :], ALU.add),
                    inc="gE", inc_n=1)
                xnc_guard[ob] = ev_y
                r.wait("scalar", ev_y)
                r.wait("scalar", tw_guard)
                r.op("scalar", lambda e: e.activation(
                    tw[:, 0:D], yw[:, 0:D], AF.Square, accum_out=sta[:, 1:2]))
                ev = r.op("scalar", lambda e: e.activation(
                    tw[:, 0:D], yw[:, 0:D], AF.Copy, accum_out=sta[:, 0:1]),
                    inc="aE")
                r.wait("vector", ev)
                r.opc("vector", lambda e: e.tensor_scalar(
                    sta[:, 0:1], sta[:, 0:1], 1.0 / D, None, ALU.mult))
                r.opc("vector", lambda e: e.tensor_scalar(
                    sta[:, 2:3], sta[:, 1:2], 1.0 / D, None, ALU.mult))
                r.opc("vector", lambda e: e.tensor_tensor(
                    sta[:, 3:4], sta[:, 0:1], sta[:, 0:1], ALU.mult))
                r.opc("vector", lambda e: e.tensor_tensor(
                    sta[:, 2:3], sta[:, 2:3], sta[:, 3:4], ALU.subtract))
                ev = r.op("vector", lambda e: e.tensor_scalar(
                    sta[:, 2:3], sta[:, 2:3], LN_EPS, None, ALU.add), inc="vE")
                r.wait("scalar", ev)
                r.opc("scalar", lambda e: e.activation(sta[:, 2:3], sta[:, 2:3], AF.Ln))
                ev = r.op("scalar", lambda e: e.activation(
                    sta[:, 2:3], sta[:, 2:3], AF.Exp, scale=-0.5), inc="aE")
                r.wait("vector", ev)
                r.wait("vector", tw_guard)
                yw_guard = r.op("vector", lambda e: e.tensor_scalar(
                    tw[:, 0:D], yw[:, 0:D], sta[:, 0:1], sta[:, 2:3],
                    ALU.subtract, ALU.mult), inc="vE")
                r.wait("gpsimd", yw_guard)
                r.wait("gpsimd", sw_guard)
                tw_guard = r.op("gpsimd", lambda e: e.tensor_tensor(
                    sw[:, 0:D], tw[:, 0:D], gammav[:, 0:D], ALU.mult),
                    inc="gE", inc_n=1)
                ev = r.op("gpsimd", lambda e: e.tensor_tensor(
                    sw[:, 0:D], sw[:, 0:D], betav[:, 0:D], ALU.add),
                    inc="gE", inc_n=1)
                r.wait("sync", ev)
                sw_guard = r.dma("sync", self.scr["s"][stile * 128:(stile + 1) * 128, :],
                                 sw[:, 0:D])
            oT_guard[ob] = r.marker("tE")

        self.tap("vb", self.B["va"][:, 0:NT * HA * 65])
        if "sout" in self.debug:
            r.fence(engs=("scalar",))
            r.dma("scalar", self.taps["sout"], self.scr["s"])
        r.fence()
        for i in range(3):
            self.P.free(f"sc{i}")
        for i in range(2):
            self.P.free(f"u{i}")
        self.P.free("bc")
        self.P.free("at")
        self.free("qTa", "kTa", "va", "wob", "wst", "oT", "rr", "rrb", "ub",
                  "xnc", "yw", "tw", "sw", "sta")
        self.free("gamma_a", "beta_a", "bout_a")

    def p2(self, l):
        r, dr, scr = self.r, self.dr, self.scr
        r.fence()
        if l == 0:
            self.alloc("z", NT * D, F32)
        z = self.B["z"]
        zt = z.rearrange("p (t d) -> p t d", t=NT)
        if l == 0:
            for t in range(NT):
                r.dma("scalar", zt[:, t, :], scr["xn"][t * 128:(t + 1) * 128, :])
        else:
            for t in range(NT):
                r.dma("gpsimd", scr["tp"][t * 128:(t + 1) * 128, :], zt[:, t, :])
        zT = self.alloc("zT", KT * (S + 16), BF16)
        zT3 = zT.rearrange("p (k s) -> p k s", k=KT)
        zcol_ev = r.op("vector", lambda e: e.memset(zT3[:, :, 0:16], 0.0), inc="vE")
        r.wait("sync", r.marker("qG"))
        for k in range(KT):
            r.op("sync", lambda e, k=k: e.dma_start(
                zT3[:, k, 16:S + 16], scr["tp"][:, k * 128:(k + 1) * 128],
                transpose=True), inc="bS1", inc_n=16, is_dma=True)
        zT_ev = r.marker("bS1")
        z_ev = r.marker("qA")

        qs = self.alloc("qs8", HD, F32)
        ks = self.alloc("ks8", HD, F32)
        r.dma_b("sync", "bS0", qs[:, 0:HD], dr[f"qs_f{l}"].broadcast_to([128, HD]))
        r.dma_b("sync", "bS0", ks[:, 0:HD], dr[f"ks_f{l}"].broadcast_to([128, HD]))
        tabs = {}
        for tn in ("tqc", "tqs", "tkc", "tks"):
            tabs[tn] = self.alloc(tn, NT * HD, F32)
            r.dma_b("scalar", "bA0", tabs[tn][:, 0:NT * HD], dr[tn])
        tab3 = {tn: tabs[tn].rearrange("p (t d) -> p t d", t=NT) for tn in tabs}
        small_ev = r.marker("bS0")
        tab_ev = r.marker("bA0")
        r.wait("vector", small_ev)
        r.op("vector", lambda e: e.tensor_scalar(
            qs[:, 0:HD], qs[:, 0:HD], 8.0, None, ALU.mult), inc="vE")

        wb = {}
        for wn in ("wq", "wk", "wv"):
            buf = self.alloc(wn + "f", KT * 512, BF16)
            w3 = buf.rearrange("p (k c) -> p k c", k=KT)
            for k in range(KT):
                r.dma_b("gpsimd", "bG0", w3[:, k, :],
                        dr[f"{wn}_f{l}"][k * 128:(k + 1) * 128, :])
            wb[wn] = w3
        wf_ev = r.marker("bG0")

        qtok = self.alloc("qtok", NT * INNER, BF16)
        ktok = self.alloc("ktok", NT * INNER, BF16)
        vf = self.alloc("vf", NT * INNER, BF16)
        q3 = qtok.rearrange("p (t c) -> p t c", t=NT)
        k3 = ktok.rearrange("p (t c) -> p t c", t=NT)
        v3 = vf.rearrange("p (t c) -> p t c", t=NT)

        qw = self.alloc("qw", 512, F32)
        qq = self.alloc("qq", 512, F32)
        q1 = self.alloc("q1", 512, F32)
        t1 = self.alloc("t1", 512, F32)
        t2 = self.alloc("t2", 512, F32)
        nrm = self.alloc("nrm", 16, F32)

        ps = [self.P.alloc(f"pf{i}", 1) for i in range(4)]
        pev = [None] * 4
        pi = [0]
        r.wait("tensor", zT_ev)
        r.wait("tensor", zcol_ev)
        r.wait("tensor", wf_ev)
        r.wait("vector", tab_ev)
        chain = None

        for t in range(NT):
            for wn, dtok in (("wq", q3), ("wk", k3), ("wv", v3)):
                p = pi[0] % 4
                pi[0] += 1
                r.wait("tensor", pev[p])
                for k in range(KT):
                    coff = 16 if k < 4 else 15
                    r.op("tensor", lambda e, p=p, wn=wn, k=k, t=t, coff=coff: e.matmul(
                        ps[p][:, 0:512],
                        zT3[:, k, coff + t * 128:coff + t * 128 + 128],
                        wb[wn][:, k, :], start=(k == 0), stop=(k == KT - 1)),
                        inc="tE" if k == KT - 1 else None)
                mm = r.marker("tE")
                if wn == "wv":
                    r.wait("vector", mm)
                    pev[p] = r.op("vector", lambda e, p=p, t=t: e.tensor_copy(
                        v3[:, t, :], ps[p][:, 0:512]), inc="vE")
                    continue
                is_q = wn == "wq"
                sc = qs if is_q else ks
                ctn, stn = ("tqc", "tqs") if is_q else ("tkc", "tks")
                r.wait("vector", mm)
                r.wait("vector", chain)
                pev[p] = r.op("vector", lambda e, p=p: e.tensor_copy(
                    qw[:, 0:512], ps[p][:, 0:512]), inc="vE")
                r.wait("scalar", pev[p])
                ev1 = r.op("scalar", lambda e: e.activation(
                    qq[:, 0:512], qw[:, 0:512], AF.Square), inc="aE")
                r.wait("vector", ev1)
                r.opc("vector", lambda e: e.reduce_sum(
                    nrm[:, 0:8], qq[:, 0:512].rearrange("p (h d) -> p h d", h=8),
                    axis=AX.X))
                ev2 = r.op("vector", lambda e: e.tensor_scalar(
                    nrm[:, 0:8], nrm[:, 0:8], 1e-24, None, ALU.max), inc="vE")
                r.wait("scalar", ev2)
                r.opc("scalar", lambda e: e.activation(nrm[:, 8:16], nrm[:, 0:8], AF.Ln))
                ev3 = r.op("scalar", lambda e: e.activation(
                    nrm[:, 8:16], nrm[:, 8:16], AF.Exp, scale=-0.5), inc="aE")
                r.wait("vector", ev3)
                r.op("vector", lambda e: e.tensor_tensor(
                    q1[:, 0:512].rearrange("p (h d) -> p h d", h=8),
                    qw[:, 0:512].rearrange("p (h d) -> p h d", h=8),
                    nrm[:, 8:16].rearrange("p (h o) -> p h o", o=1
                                           ).broadcast_to([128, 8, HD]),
                    ALU.mult))
                r.op("vector", lambda e, sc=sc: e.tensor_tensor(
                    qw[:, 0:512].rearrange("p (h d) -> p h d", h=8),
                    q1[:, 0:512].rearrange("p (h d) -> p h d", h=8),
                    sc[:, 0:HD].rearrange("p (o d) -> p o d", o=1
                                          ).broadcast_to([128, 8, HD]),
                    ALU.mult))
                r.op("vector", lambda e, ctn=ctn, t=t: e.tensor_tensor(
                    t1[:, 0:512].rearrange("p (h d) -> p h d", h=8),
                    qw[:, 0:512].rearrange("p (h d) -> p h d", h=8),
                    tab3[ctn][:, t, :].rearrange("p (o d) -> p o d", o=1
                                                 ).broadcast_to([128, 8, HD]),
                    ALU.mult))
                q14 = qw[:, 0:512].rearrange("p (h r d) -> p h r d", h=8, r=2)
                t24 = t2[:, 0:512].rearrange("p (h r d) -> p h r d", h=8, r=2)
                st4 = tab3[stn][:, t, :].rearrange("p (o r d) -> p o r d", o=1, r=2)
                r.op("vector", lambda e, q14=q14, t24=t24, st4=st4: e.tensor_tensor(
                    t24[:, :, 0:1, :], q14[:, :, 1:2, :],
                    st4[:, :, 0:1, :].broadcast_to([128, 8, 1, 32]), ALU.mult))
                r.op("vector", lambda e, q14=q14, t24=t24, st4=st4: e.tensor_tensor(
                    t24[:, :, 1:2, :], q14[:, :, 0:1, :],
                    st4[:, :, 1:2, :].broadcast_to([128, 8, 1, 32]), ALU.mult))
                chain = r.op("vector", lambda e, dtok=dtok, t=t: e.tensor_tensor(
                    dtok[:, t, :], t1[:, 0:512], t2[:, 0:512], ALU.add), inc="vE")

        for i in range(4):
            self.P.free(f"pf{i}")
        self.free("wqf", "wkf", "wvf", "qw", "qq", "q1", "t1", "t2",
                  "tqc", "tqs", "tkc", "tks", "qs8", "ks8")
        self.tap("qf", self.B["qtok"][:, 0:NT * INNER])
        self.tap("kf", self.B["ktok"][:, 0:NT * INNER])
        self.tap("vf", self.B["vf"][:, 0:NT * INNER])
        r.fence()
        self.free("zT")

        for t in range(NT):
            r.dma("scalar", scr["tq"][t * 128:(t + 1) * 128, :], q3[:, t, :])
            r.dma("scalar", scr["tk"][t * 128:(t + 1) * 128, :], k3[:, t, :])
        dmp = r.marker("qA")
        qfT = self.alloc("qfT", 4 * S, BF16)
        kfT = self.alloc("kfT", 4 * S, BF16)
        qfT3 = qfT.rearrange("p (k s) -> p k s", k=4)
        kfT3 = kfT.rearrange("p (k s) -> p k s", k=4)
        r.wait("sync", dmp)
        for k in range(4):
            r.op("sync", lambda e, k=k: e.dma_start(
                qfT3[:, k, :], scr["tq"][:, k * 128:(k + 1) * 128],
                transpose=True), inc="bS1", inc_n=16, is_dma=True)
            r.op("sync", lambda e, k=k: e.dma_start(
                kfT3[:, k, :], scr["tk"][:, k * 128:(k + 1) * 128],
                transpose=True), inc="bS1", inc_n=16, is_dma=True)
        qkT_ev = r.marker("bS1")
        r.fence(engs=("vector", "scalar", "gpsimd", "tensor"), sems=("qA",))
        self.free("qtok", "ktok")

        wob = self.alloc("wobf", 4 * D, BF16)
        wo3 = wob.rearrange("p (k n) -> p k n", k=4)
        for k in range(4):
            r.dma_b("gpsimd", "bG1", wo3[:, k, :],
                    dr[f"wo_f{l}"][k * 128:(k + 1) * 128, :])
        wo_ev = r.marker("bG1")

        of = self.alloc("of", 4 * S, BF16)
        of3 = of.rearrange("p (k s) -> p k s", k=4)
        msc = self.alloc("msc", 2 * 640, F32)
        msc3 = msc.rearrange("p (b c) -> p b c", b=2)
        attn = self.alloc("attn", 2 * 640, BF16)
        attn3 = attn.rearrange("p (b c) -> p b c", b=2)
        atT = self.alloc("atT", 2 * 640, BF16)
        atT3 = atT.rearrange("p (b c) -> p b c", b=2)
        colsb = self.alloc("cols", 8, F32)
        cols = colsb.rearrange("p (b c) -> p b c", b=2)

        ps_sc = [self.P.alloc(f"fs{i}", 2) for i in range(2)]
        ps_tp = [self.P.alloc(f"ft{i}", 1, BF16) for i in range(2)]
        ps_o = [self.P.alloc(f"fo{i}", 1) for i in range(2)]

        pssc_guard = [None, None]
        msc_guard = [None, None]
        attn_guard = [None, None]
        pstp_guard = [None, None]
        atT_guard = [None, None]
        pso_guard = [None, None]

        r.wait("tensor", qkT_ev)
        mkv = self.B["mask"]
        identv = self.B["ident"]

        tiles = [(h, a) for h in range(HF) for a in range(NT)]
        nm_ev = [None] * len(tiles)
        tT_ev = [None] * len(tiles)

        def s_scores(i):
            h, a = tiles[i]
            kc, kp = h // 2, (h % 2) * 64
            wmax = min(a, 4)
            W = (wmax + 1) * 128
            k0 = a - wmax
            par = i % 2
            r.wait("tensor", pssc_guard[par])
            nleft, off = W, 0
            while nleft > 0:
                n = min(512, nleft)
                r.op("tensor", lambda e, kc=kc, kp=kp, a=a, k0=k0, off=off, n=n, par=par: e.matmul(
                    ps_sc[par][:, off:off + n],
                    qfT3[kp:kp + 64, kc, a * 128:(a + 1) * 128],
                    kfT3[kp:kp + 64, kc, k0 * 128 + off:k0 * 128 + off + n],
                    start=True, stop=True, tile_position=(kp, 0)),
                    inc="tE" if n == nleft else None)
                off += n
                nleft -= n
            mm = r.marker("tE")
            r.wait("vector", mm)
            r.wait("vector", msc_guard[par])
            pssc_guard[par] = r.op("vector", lambda e, par=par, W=W: e.scalar_tensor_tensor(
                msc3[:, par, 0:W], ps_sc[par][:, 0:W], 1.0,
                mkv[:, 640 - W:640], ALU.mult, ALU.add), inc="vE")
            r.opc("vector", lambda e, par=par, W=W: e.reduce_max(
                cols[:, par, 0:1], msc3[:, par, 0:W], axis=AX.X))
            ev_neg = r.op("vector", lambda e, par=par: e.tensor_scalar(
                cols[:, par, 1:2], cols[:, par, 0:1], -1.0, None, ALU.mult), inc="vE")
            r.wait("scalar", ev_neg)
            r.wait("scalar", attn_guard[par])
            ev_ex = r.op("scalar", lambda e, par=par, W=W: e.activation(
                attn3[:, par, 0:W], msc3[:, par, 0:W], AF.Exp,
                bias=cols[:, par, 1:2], accum_out=cols[:, par, 2:3]), inc="aE")
            msc_guard[par] = ev_ex
            r.wait("vector", ev_ex)
            r.opc("vector", lambda e, par=par: e.reciprocal(
                cols[:, par, 3:4], cols[:, par, 2:3]))
            nm_ev[i] = r.op("vector", lambda e, par=par, W=W: e.tensor_scalar(
                attn3[:, par, 0:W], attn3[:, par, 0:W], cols[:, par, 3:4], None,
                ALU.mult), inc="vE")

        def s_transp(i):
            h, a = tiles[i]
            wmax = min(a, 4)
            W = (wmax + 1) * 128
            par = i % 2
            r.wait("tensor", nm_ev[i])
            r.wait("tensor", pstp_guard[par])
            for j in range(wmax + 1):
                r.op("tensor", lambda e, par=par, j=j: e.transpose(
                    ps_tp[par][:, j * 128:(j + 1) * 128],
                    attn3[:, par, j * 128:(j + 1) * 128],
                    identv[:, 0:128]),
                    inc="tE" if j == wmax else None)
            attn_guard[par] = r.marker("tE")
            r.wait("vector", attn_guard[par])
            r.wait("vector", atT_guard[par])
            tT_ev[i] = r.op("vector", lambda e, par=par, W=W: e.tensor_copy(
                atT3[:, par, 0:W], ps_tp[par][:, 0:W]), inc="vE")
            pstp_guard[par] = tT_ev[i]

        def s_av(i):
            h, a = tiles[i]
            kc, kp = h // 2, (h % 2) * 64
            wmax = min(a, 4)
            k0 = a - wmax
            par = i % 2
            r.wait("tensor", pso_guard[par])
            r.wait("tensor", tT_ev[i])
            for j in range(wmax + 1):
                kt = k0 + j
                r.op("tensor", lambda e, h=h, kt=kt, j=j, par=par, wmax=wmax: e.matmul(
                    ps_o[par][0:64, 0:128], v3[:, kt, 64 * h:64 * h + 64],
                    atT3[:, par, j * 128:(j + 1) * 128],
                    start=(j == 0), stop=(j == wmax)),
                    inc="tE" if j == wmax else None)
            atT_guard[par] = r.marker("tE")
            r.wait("vector", atT_guard[par])
            pso_guard[par] = r.op("vector", lambda e, kp=kp, kc=kc, a=a, par=par: e.tensor_copy(
                of3[:, kc, a * 128:(a + 1) * 128][kp:kp + 64, :],
                ps_o[par][0:64, 0:128]), inc="vE")

        s_scores(0)
        for i in range(len(tiles)):
            if i + 1 < len(tiles):
                s_scores(i + 1)
            s_transp(i)
            s_av(i)

        of_ev = r.marker("vE")
        for i in range(2):
            self.P.free(f"fs{i}")
            self.P.free(f"ft{i}")
            self.P.free(f"fo{i}")
        if l == 0:
            self.tap("of", self.B["of"][:, 0:4 * S])
        self.free("msc", "attn", "atT", "cols", "qfT", "kfT", "vf")

        ps_d = [self.P.alloc(f"pd{i}", 2) for i in range(2)]
        dguard = [None, None]
        r.wait("tensor", of_ev)
        r.wait("tensor", wo_ev)
        r.wait("vector", z_ev)
        for t in range(NT):
            par = t % 2
            r.wait("tensor", dguard[par])
            for k in range(4):
                for nch in range(2):
                    r.op("tensor", lambda e, k=k, t=t, nch=nch, par=par: e.matmul(
                        ps_d[par][:, nch * 512:(nch + 1) * 512],
                        of3[:, k, t * 128:(t + 1) * 128],
                        wo3[:, k, nch * 512:(nch + 1) * 512],
                        start=(k == 0), stop=(k == 3)),
                        inc="tE" if (k == 3 and nch == 1) else None)
            mm = r.marker("tE")
            r.wait("vector", mm)
            dguard[par] = r.op("vector", lambda e, t=t, par=par: e.tensor_tensor(
                zt[:, t, :], zt[:, t, :], ps_d[par][:, 0:1024], ALU.add), inc="vE")
        for i in range(2):
            self.P.free(f"pd{i}")
        self.free("of", "wobf")
        if l == 0:
            self.tap("z1", z[:, 0:NT * D])
        r.fence()

        lg = self.alloc("lg", D, F32)
        lb = self.alloc("lb", D, F32)
        r.dma_b("sync", "bS0", lg[:, 0:D], dr[f"lg_f{l}"].broadcast_to([128, D]))
        r.dma_b("sync", "bS0", lb[:, 0:D], dr[f"lb_f{l}"].broadcast_to([128, D]))
        lgb_ev = r.marker("bS0")
        sh = self.alloc("sh", 2 * 512, F32)
        sh3 = sh.rearrange("p (b c) -> p b c", b=2)
        h2b = self.alloc("h2b", 2 * D, BF16)
        h2b3 = h2b.rearrange("p (b c) -> p b c", b=2)
        aw = self.alloc("aw", D, F32)
        stc = self.alloc("stc", 8, F32)
        sh_guard = [None, None]
        h2b_guard = [None, None]
        aw_guard = None
        r.wait("gpsimd", lgb_ev)
        for t in range(NT):
            par = t % 2
            r.wait("sync", sh_guard[par])
            bsh = "bS" + str(2 + par)
            if t == 0:
                ev0 = r.op("vector", lambda e, par=par: e.memset(
                    sh3[0:1, par, :], 0.0), inc="vE")
                r.wait("sync", ev0)
            else:
                r.dma_b("sync", bsh, sh3[0:1, par, :], zt[127:128, t - 1, 512:1024])
            r.dma_b("sync", bsh, sh3[1:128, par, :], zt[0:127, t, 512:1024])
            shd = r.marker(bsh)
            r.wait("scalar", shd)
            r.wait("scalar", aw_guard)
            r.op("scalar", lambda e, t=t: e.activation(
                aw[:, 0:512], zt[:, t, 0:512], AF.Copy, accum_out=stc[:, 0:1]))
            r.op("scalar", lambda e, par=par: e.activation(
                aw[:, 0:512], sh3[:, par, :], AF.Copy, accum_out=stc[:, 1:2]))
            r.op("scalar", lambda e, t=t: e.activation(
                aw[:, 0:512], zt[:, t, 0:512], AF.Square, accum_out=stc[:, 2:3]))
            ev = r.op("scalar", lambda e, par=par: e.activation(
                aw[:, 0:512], sh3[:, par, :], AF.Square, accum_out=stc[:, 3:4]),
                inc="aE")
            r.wait("vector", ev)
            r.opc("vector", lambda e: e.tensor_tensor(
                stc[:, 0:1], stc[:, 0:1], stc[:, 1:2], ALU.add))
            r.opc("vector", lambda e: e.tensor_tensor(
                stc[:, 2:3], stc[:, 2:3], stc[:, 3:4], ALU.add))
            r.opc("vector", lambda e: e.tensor_scalar(
                stc[:, 0:1], stc[:, 0:1], 1.0 / D, None, ALU.mult))
            r.opc("vector", lambda e: e.tensor_scalar(
                stc[:, 2:3], stc[:, 2:3], 1.0 / D, None, ALU.mult))
            r.opc("vector", lambda e: e.tensor_tensor(
                stc[:, 4:5], stc[:, 0:1], stc[:, 0:1], ALU.mult))
            r.opc("vector", lambda e: e.tensor_tensor(
                stc[:, 2:3], stc[:, 2:3], stc[:, 4:5], ALU.subtract))
            ev = r.op("vector", lambda e: e.tensor_scalar(
                stc[:, 2:3], stc[:, 2:3], LN_EPS, None, ALU.add), inc="vE")
            r.wait("scalar", ev)
            r.opc("scalar", lambda e: e.activation(stc[:, 2:3], stc[:, 2:3], AF.Ln))
            ev = r.op("scalar", lambda e: e.activation(
                stc[:, 2:3], stc[:, 2:3], AF.Exp, scale=-0.5), inc="aE")
            r.wait("vector", ev)
            r.wait("vector", h2b_guard[par])
            r.op("vector", lambda e, t=t: e.tensor_scalar(
                aw[:, 0:512], zt[:, t, 0:512], stc[:, 0:1], stc[:, 2:3],
                ALU.subtract, ALU.mult))
            ev = r.op("vector", lambda e, par=par: e.tensor_scalar(
                aw[:, 512:1024], sh3[:, par, :], stc[:, 0:1], stc[:, 2:3],
                ALU.subtract, ALU.mult), inc="vE")
            r.wait("gpsimd", ev)
            r.op("gpsimd", lambda e: e.tensor_tensor(
                aw[:, 0:D], aw[:, 0:D], lg[:, 0:D], ALU.mult), inc="gE", inc_n=1)
            ev = r.op("gpsimd", lambda e, par=par: e.tensor_tensor(
                h2b3[:, par, :], aw[:, 0:D], lb[:, 0:D], ALU.add),
                inc="gE", inc_n=1)
            aw_guard = ev
            sh_guard[par] = ev
            r.wait("scalar", ev)
            h2b_guard[par] = r.dma_b(
                "scalar", "bA" + str(par),
                scr["tp"][t * 128:(t + 1) * 128, :], h2b3[:, par, :])
        r.fence(engs=("sync",), sems=("qA", "bA0", "bA1"))
        dmp = r.marker("qA")
        h2T = self.alloc("h2T", KT * S, BF16)
        h2T3 = h2T.rearrange("p (k s) -> p k s", k=KT)
        r.wait("sync", dmp)
        for k in range(KT):
            r.op("sync", lambda e, k=k: e.dma_start(
                h2T3[:, k, :], scr["tp"][:, k * 128:(k + 1) * 128],
                transpose=True), inc="bS1", inc_n=16, is_dma=True)
        self.tap("h2T", self.B["h2T"][:, 0:KT * S])
        r.fence()
        self.free("lg", "lb", "sh", "h2b", "aw", "stc")

        b1a = self.alloc("b1a", HID_T, F32)
        b1g = self.alloc("b1g", HID_T, F32)
        b2 = self.alloc("b2", D, F32)
        r.dma_b("sync", "bS0", b1a[:, 0:HID_T], dr[f"b1a_f{l}"])
        r.dma_b("sync", "bS0", b1g[:, 0:HID_T], dr[f"b1g_f{l}"])
        r.dma_b("sync", "bS0", b2[:, 0:D], dr[f"b2_f{l}"].broadcast_to([128, D]))
        bia_ev = r.marker("bS0")
        w1b = self.alloc("w1b", 16 * 128, BF16)
        w1b3 = w1b.rearrange("p (b c) -> p b c", b=16)
        w2b = self.alloc("w2b", 11 * 1024, BF16)
        w2b3 = w2b.rearrange("p (k n) -> p k n", k=11)
        h4 = self.alloc("h4", HID_T * 1024, BF16)
        h43 = h4.rearrange("p (k s) -> p k s", k=HID_T)
        gel = self.alloc("gel", 1024, F32)
        tmp2 = self.alloc("tmp2", D, F32)
        slot_ev = {}
        si = [0]
        r.wait("vector", bia_ev)
        r.wait("scalar", bia_ev)
        h2T_mark = r.marker("bS1")
        r.wait("tensor", h2T_mark)

        ps_a = self.P.alloc("pa", 2)
        ps_g = self.P.alloc("pg", 2)
        ps_w = [self.P.alloc(f"pw{i}", 2) for i in range(2)]
        pa_guard = None
        pg_guard = None
        tmp2_guard = None
        pw_guard = [None, None]
        h4_guard = {}
        w2slot_guard = {}

        for half in range(2):
            scol = half * 1024
            for kt in range(HID_T):
                rows = 128 if kt < HID_T - 1 else HID - 128 * (HID_T - 1)
                for ag in range(2):
                    wsrc = f"w1a_f{l}" if ag == 0 else f"w1g_f{l}"
                    psAG = ps_a if ag == 0 else ps_g
                    slots = []
                    bs = "bG" + str(ag)
                    for k in range(KT):
                        i = si[0] % 16
                        si[0] += 1
                        r.wait("gpsimd", slot_ev.get(i))
                        r.dma_b("gpsimd", bs, w1b3[:, i, 0:rows],
                                dr[wsrc][k * 128:(k + 1) * 128,
                                         kt * 128:kt * 128 + rows])
                        slots.append((i, None))
                    batch_ev = r.marker(bs)
                    r.wait("tensor", pa_guard if ag == 0 else pg_guard)
                    r.wait("tensor", batch_ev)
                    for k in range(KT):
                        i, ev = slots[k]
                        for nch in range(2):
                            r.op("tensor", lambda e, i=i, k=k, nch=nch, psAG=psAG, scol=scol, rows=rows: e.matmul(
                                psAG[0:rows, nch * 512:(nch + 1) * 512],
                                w1b3[:, i, 0:rows],
                                h2T3[:, k, scol + nch * 512:scol + (nch + 1) * 512],
                                start=(k == 0), stop=(k == KT - 1)),
                                inc="tE" if (k == KT - 1 and nch == 1) else None)
                    mm = r.marker("tE")
                    for (i, _e) in slots:
                        slot_ev[i] = mm
                    if ag == 0:
                        pa_mm = mm
                    else:
                        pg_mm = mm
                r.wait("scalar", pg_mm)
                r.wait("scalar", pa_guard)
                pg_guard = r.op("scalar", lambda e, kt=kt, rows=rows: e.activation(
                    gel[0:rows, 0:1024], ps_g[0:rows, 0:1024], AF.Gelu,
                    bias=b1g[0:rows, kt:kt + 1]), inc="aE")
                r.wait("vector", pg_guard)
                r.wait("vector", pa_mm)
                r.wait("vector", h4_guard.get(kt))
                pa_guard = r.op("vector", lambda e, kt=kt, rows=rows: e.scalar_tensor_tensor(
                    h43[0:rows, kt, 0:1024], ps_a[0:rows, 0:1024],
                    b1a[0:rows, kt:kt + 1], gel[0:rows, 0:1024],
                    ALU.add, ALU.mult), inc="vE")
            h4_done = r.marker("vE")

            for kh in range(2):
                kts = list(range(0, 11)) if kh == 0 else list(range(11, HID_T))
                bs = "bG" + str(2 + kh)
                for j, kt in enumerate(kts):
                    r.wait("gpsimd", w2slot_guard.get(j))
                    r.dma_b("gpsimd", bs, w2b3[:, j, :],
                            dr[f"w2_f{l}"][kt * 128:(kt + 1) * 128, :])
                batch_ev = r.marker(bs)
                r.wait("tensor", h4_done)
                r.wait("tensor", batch_ev)
                for ti in range(8):
                    t = half * 8 + ti
                    par = ti % 2
                    r.wait("tensor", pw_guard[par])
                    for j, kt in enumerate(kts):
                        for nch in range(2):
                            r.op("tensor", lambda e, j=j, kt=kt, ti=ti, nch=nch, par=par: e.matmul(
                                ps_w[par][:, nch * 512:(nch + 1) * 512],
                                h43[:, kt, ti * 128:(ti + 1) * 128],
                                w2b3[:, j, nch * 512:(nch + 1) * 512],
                                start=(j == 0), stop=(j == len(kts) - 1)),
                                inc="tE" if (j == len(kts) - 1 and nch == 1) else None)
                    mm = r.marker("tE")
                    r.wait("vector", mm)
                    if kh == 0:
                        pw_guard[par] = r.op("vector", lambda e, t=t, par=par: e.tensor_tensor(
                            zt[:, t, :], zt[:, t, :], ps_w[par][:, 0:1024], ALU.add),
                            inc="vE")
                    else:
                        r.wait("vector", tmp2_guard)
                        ev = r.op("vector", lambda e, par=par: e.scalar_tensor_tensor(
                            tmp2[:, 0:D], ps_w[par][:, 0:1024], 1.0, b2[:, 0:D],
                            ALU.mult, ALU.add), inc="vE")
                        r.wait("gpsimd", ev)
                        pw_guard[par] = r.op("gpsimd", lambda e, t=t: e.tensor_tensor(
                            zt[:, t, :], zt[:, t, :], tmp2[:, 0:D], ALU.add),
                            inc="gE", inc_n=1)
                        tmp2_guard = pw_guard[par]
                wmark = r.marker("tE")
                for j in range(len(kts)):
                    w2slot_guard[j] = wmark
            hm = r.marker("tE")
            for kt in range(HID_T):
                h4_guard[kt] = hm

        self.P.free("pa")
        self.P.free("pg")
        for i in range(2):
            self.P.free(f"pw{i}")
        r.fence()
        self.free("b1a", "b1g", "b2", "w1b", "w2b", "h4", "gel", "tmp2", "h2T")
        self.tap("z2" if l == 0 else "zf", z[:, 0:NT * D])

    def p3(self):
        r, dr, scr = self.r, self.dr, self.scr
        r.fence()
        zt = self.B["z"].rearrange("p (t d) -> p t d", t=NT)
        xb = self.alloc("xb3", 2 * D, F32)
        xb3 = xb.rearrange("p (b c) -> p b c", b=2)
        sb = self.alloc("sb3", 2 * D, F32)
        sb3 = sb.rearrange("p (b c) -> p b c", b=2)
        gw = self.alloc("gw3", D, F32)
        ow = self.alloc("ow3", D, F32)
        buf_guard = [None, None]
        ow_guard = None
        gw_guard = None
        for t in range(NT):
            par = t % 2
            r.wait("scalar", buf_guard[par])
            e1 = r.dma_b("scalar", "bA" + str(par), xb3[:, par, :],
                         dr["x"][t * 128:(t + 1) * 128, :])
            e2 = r.dma_b("scalar", "bA" + str(par), sb3[:, par, :],
                         scr["s"][t * 128:(t + 1) * 128, :])
            r.wait("scalar", gw_guard)
            ev_g = r.op("scalar", lambda e, t=t: e.activation(
                gw[:, 0:D], zt[:, t, :], AF.Silu), inc="aE")
            r.wait("vector", ev_g)
            r.wait("vector", e2)
            r.wait("vector", ow_guard)
            gw_guard = r.op("vector", lambda e, par=par: e.tensor_tensor(
                ow[:, 0:D], gw[:, 0:D], sb3[:, par, :], ALU.mult), inc="vE")
            ev_o = r.op("vector", lambda e, par=par: e.tensor_tensor(
                ow[:, 0:D], ow[:, 0:D], xb3[:, par, :], ALU.add), inc="vE")
            r.wait("sync", ev_o)
            st_ev = r.dma("sync", dr["out"][t * 128:(t + 1) * 128, :], ow[:, 0:D])
            ow_guard = st_ev
            buf_guard[par] = st_ev

    def fin(self):
        self.r.fence(engs=("sync",))



def kernel(**inputs):
    g = host_layout(inputs)
    consts = host_constants()
    nc = build()
    x = np.asarray(inputs["L0_attn_x"], dtype=np.float32)
    in_maps = []
    for core in range(8):
        m = {"x": np.ascontiguousarray(x[core])}
        m.update(g)
        m.update(consts)
        in_maps.append(m)
    res = run_bass_kernel_spmd(nc, in_maps, core_ids=list(range(8)))
    return np.stack([r["out"] for r in res.results], axis=0)


# version 04
# speedup vs baseline: 1.1661x; 1.0041x over previous
import numpy as np
import ml_dtypes
import concourse.bass as bass
import concourse.mybir as mybir
from concourse.bass_utils import run_bass_kernel_spmd

F32 = mybir.dt.float32
BF16 = mybir.dt.bfloat16
AF = mybir.ActivationFunctionType
ALU = mybir.AluOpType
AX = mybir.AxisListType

S = 2048
NT = 16
D = 1024
KT = 8
HA = 16
HF = 8
HD = 64
INNER = 512
HID = 2730
HID_T = 22
WIN = 512
NEG = -3.0e38
LN_EPS = 1e-5

ENGINES = ("tensor", "vector", "scalar", "sync", "gpsimd")


class Rec:
    def __init__(self, nc):
        self.nc = nc
        self.ops = {e: [] for e in ENGINES}
        self.sems = {}
        self.cnt = {}
        self.last_wait = {e: {} for e in ENGINES}

    def sem(self, name):
        self.sems[name] = self.nc.semaphore(name).__enter__()
        self.cnt[name] = 0

    SELF_SEM = {"vector": "vE", "scalar": "aE", "gpsimd": "gE"}

    def op(self, eng, fn, inc=None, inc_n=1, is_dma=False):
        own = None if is_dma else self.SELF_SEM.get(eng)
        if own is not None:
            self.wait(eng, (own, self.cnt[own]))
            if inc is None:
                inc = own
        if inc is None:
            self.ops[eng].append(fn)
            return None
        self.cnt[inc] += inc_n
        tgt = self.cnt[inc]
        sem = self.sems[inc]
        if own is not None and inc != own:
            self.cnt[own] += 1
            tgt2 = self.cnt[own]
            sem2 = self.sems[own]
            self.ops[eng].append(
                lambda e, f=fn, s=sem, n=inc_n, s2=sem2: f(e).then_inc(s, n).then_inc(s2, 1))
            return (inc, tgt)
        self.ops[eng].append(lambda e, f=fn, s=sem, n=inc_n: f(e).then_inc(s, n))
        return (inc, tgt)

    def dma_b(self, q, bsem, dst, src, **kw):
        return self.op(q, lambda e: e.dma_start(dst, src, **kw), inc=bsem,
                       inc_n=16, is_dma=True)

    def dma(self, q, dst, src, **kw):
        inc = {"sync": "qS", "scalar": "qA", "gpsimd": "qG"}[q]
        return self.op(q, lambda e: e.dma_start(dst, src, **kw), inc=inc,
                       inc_n=16, is_dma=True)

    def dmat(self, q, dst, src):
        inc = {"sync": "qS", "scalar": "qA"}[q]
        return self.op(q, lambda e: e.dma_start(dst, src, transpose=True),
                       inc=inc, inc_n=16, is_dma=True)

    def wait(self, eng, ev):
        if ev is None:
            return
        sname, val = ev
        if sname in ("qS", "qA", "qG"):
            val = self.cnt[sname]
        lw = self.last_wait[eng]
        if lw.get(sname, -1) >= val:
            return
        lw[sname] = val
        sem = self.sems[sname]
        self.ops[eng].append(lambda e, s=sem, v=val: e.wait_ge(s, v))

    def marker(self, sname):
        return (sname, self.cnt[sname])

    def opc(self, eng, fn):
        inc = {"vector": "vE", "scalar": "aE", "gpsimd": "qG"}[eng]
        ev = self.op(eng, fn, inc=inc)
        self.wait(eng, ev)
        return ev

    def fence(self, engs=ENGINES, sems=("qS", "qA", "qG", "tE", "vE", "aE", "gE",
                                        "bG0", "bG1", "bG2", "bG3", "bS0", "bS1",
                                        "bS2", "bS3", "bA0", "bA1")):
        marks = [self.marker(s) for s in sems if self.cnt[s] > 0]
        for e in engs:
            for m in marks:
                self.wait(e, m)

    def replay(self, block):
        for eng in ENGINES:
            ops = self.ops[eng]

            def body(e, ops=ops):
                for f in ops:
                    f(e)

            getattr(block, eng)(body)


class Arena:
    def __init__(self, nc, elems_f32):
        self.t = nc.alloc_sbuf_tensor("arena", [128, elems_f32], F32)
        self.free_list = [(0, elems_f32)]
        self.allocs = {}

    def alloc(self, name, elems, dtype):
        f32e = elems if dtype == F32 else (elems + 1) // 2
        align = 16
        for i, (off, size) in enumerate(self.free_list):
            aoff = (off + align - 1) // align * align
            pad = aoff - off
            if size >= f32e + pad:
                rest = size - f32e - pad
                frags = []
                if pad:
                    frags.append((off, pad))
                if rest:
                    frags.append((aoff + f32e, rest))
                self.free_list[i:i + 1] = frags
                self.allocs[name] = (aoff, f32e)
                ap = self.t[:, aoff:aoff + f32e]
                if dtype == BF16:
                    ap = ap.bitcast(BF16)[:, 0:elems]
                return ap
        raise MemoryError(f"arena: cannot alloc {name} ({f32e}); free={self.free_list}")

    def free(self, name):
        off, elems = self.allocs.pop(name)
        self.free_list.append((off, elems))
        self.free_list.sort()
        merged = []
        for o, sz in self.free_list:
            if merged and merged[-1][0] + merged[-1][1] == o:
                merged[-1] = (merged[-1][0], merged[-1][1] + sz)
            else:
                merged.append([o, sz])
        self.free_list = [tuple(x) for x in merged]


class PsumMgr:
    def __init__(self, nc):
        self.t = nc.alloc_psum_tensor("psum", [128, 4096], F32)
        self.free_banks = set(range(8))
        self.allocs = {}

    def alloc(self, name, banks, dtype=F32):
        for b0 in range(8 - banks + 1):
            if all(b0 + j in self.free_banks for j in range(banks)):
                for j in range(banks):
                    self.free_banks.discard(b0 + j)
                self.allocs[name] = (b0, banks)
                ap = self.t[:, b0 * 512:(b0 + banks) * 512]
                return ap.bitcast(BF16) if dtype == BF16 else ap
        raise MemoryError(f"psum {name}: need {banks}, free={sorted(self.free_banks)}")

    def free(self, name):
        b0, banks = self.allocs.pop(name)
        for j in range(banks):
            self.free_banks.add(b0 + j)



def host_constants():
    c = {}
    c["ident"] = np.eye(128, dtype=ml_dtypes.bfloat16)
    c["onesrow"] = np.ones((1, 64), dtype=ml_dtypes.bfloat16)
    r = np.arange(128)[:, None]
    cc = np.arange(640)[None, :]
    c["mask"] = np.where((cc >= r) & (cc <= r + 512), 0.0, NEG).astype(np.float32)
    half = np.arange(0, HD, 2, dtype=np.float64)
    inv_freq = 1.0 / 10000.0 ** (half / HD)
    t = np.arange(S, dtype=np.float64)
    freqs = t[:, None] * inv_freq[None, :]
    freqs = np.concatenate([freqs, freqs], axis=-1)
    base = (half + 0.4 * HD) / (1.4 * HD)
    power = (t - S // 2) / float(WIN // 2)
    scale = base[None, :] ** power[:, None]
    scale = np.concatenate([scale, scale], axis=-1)
    cos, sin = np.cos(freqs), np.sin(freqs)
    sign = np.concatenate([-np.ones(32), np.ones(32)])[None, :]

    def tok(x):
        return np.ascontiguousarray(
            x.reshape(NT, 128, HD).transpose(1, 0, 2).reshape(128, NT * HD)
        ).astype(np.float32)

    c["tqc"] = tok(cos * scale)
    c["tqs"] = tok(sin * scale * sign)
    c["tkc"] = tok(cos / scale)
    c["tks"] = tok(sin / scale * sign)
    return c


def host_layout(inputs):
    g = {}
    w3 = np.asarray(inputs["L0_attn_w_qkv"]).reshape(D, HA, 3, HD)
    g["wq_a"] = np.ascontiguousarray(w3[:, :, 0, :].reshape(D, D))
    g["wk_a"] = np.ascontiguousarray(w3[:, :, 1, :].reshape(D, D))
    g["wv_a"] = np.ascontiguousarray(w3[:, :, 2, :].reshape(D, D))
    g["wout_a"] = np.ascontiguousarray(np.asarray(inputs["L0_attn_w_out"]))
    g["bout_a"] = np.ascontiguousarray(np.asarray(inputs["L0_attn_b_out"]).reshape(1, D))
    g["gamma_a"] = np.ascontiguousarray(np.asarray(inputs["L0_attn_gamma"]).reshape(1, D))
    g["beta_a"] = np.ascontiguousarray(np.asarray(inputs["L0_attn_beta"]).reshape(1, D))
    for l in range(2):
        wf = np.asarray(inputs["L0_ffn_wqkv"])[l]
        g[f"wq_f{l}"] = np.ascontiguousarray(wf[:, 0:INNER])
        g[f"wk_f{l}"] = np.ascontiguousarray(wf[:, INNER:2 * INNER])
        g[f"wv_f{l}"] = np.ascontiguousarray(wf[:, 2 * INNER:3 * INNER])
        g[f"qs_f{l}"] = np.ascontiguousarray(
            np.asarray(inputs["L0_ffn_q_scale"])[l].reshape(1, HD))
        g[f"ks_f{l}"] = np.ascontiguousarray(
            np.asarray(inputs["L0_ffn_k_scale"])[l].reshape(1, HD))
        g[f"wo_f{l}"] = np.ascontiguousarray(np.asarray(inputs["L0_ffn_wo"])[l])
        g[f"lg_f{l}"] = np.ascontiguousarray(
            np.asarray(inputs["L0_ffn_ln_g"])[l].reshape(1, D))
        g[f"lb_f{l}"] = np.ascontiguousarray(
            np.asarray(inputs["L0_ffn_ln_b"])[l].reshape(1, D))
        w1 = np.asarray(inputs["L0_ffn_w1"])[l]
        g[f"w1a_f{l}"] = np.ascontiguousarray(w1[:, 0:HID])
        g[f"w1g_f{l}"] = np.ascontiguousarray(w1[:, HID:2 * HID])
        b1 = np.asarray(inputs["L0_ffn_b1"])[l]
        b1a = np.zeros((HID_T, 128), np.float32)
        b1g = np.zeros((HID_T, 128), np.float32)
        b1a.reshape(-1)[:HID] = b1[0:HID]
        b1g.reshape(-1)[:HID] = b1[HID:2 * HID]
        g[f"b1a_f{l}"] = np.ascontiguousarray(b1a.T)
        g[f"b1g_f{l}"] = np.ascontiguousarray(b1g.T)
        w2p = np.zeros((HID_T * 128, D), np.float32)
        w2p[:HID] = np.asarray(inputs["L0_ffn_w2"])[l]
        g[f"w2_f{l}"] = w2p
        g[f"b2_f{l}"] = np.ascontiguousarray(
            np.asarray(inputs["L0_ffn_b2"])[l].reshape(1, D))
    return g



def build(debug_taps=(), phases=("p0", "p1", "p2a", "p2b", "p3")):
    nc = bass.Bass("TRN2", target_bir_lowering=False)
    dr = {}
    dr["x"] = nc.dram_tensor("x", [S, D], F32, kind="ExternalInput").ap()
    f32_ins = {
        "wq_a": [D, D], "wk_a": [D, D], "wv_a": [D, D], "wout_a": [D, D],
        "bout_a": [1, D], "gamma_a": [1, D], "beta_a": [1, D],
        "mask": [128, 640],
        "tqc": [128, NT * HD], "tqs": [128, NT * HD],
        "tkc": [128, NT * HD], "tks": [128, NT * HD],
    }
    for l in range(2):
        f32_ins.update({
            f"wq_f{l}": [D, INNER], f"wk_f{l}": [D, INNER], f"wv_f{l}": [D, INNER],
            f"qs_f{l}": [1, HD], f"ks_f{l}": [1, HD],
            f"wo_f{l}": [INNER, D], f"lg_f{l}": [1, D], f"lb_f{l}": [1, D],
            f"w1a_f{l}": [D, HID], f"w1g_f{l}": [D, HID],
            f"b1a_f{l}": [128, HID_T], f"b1g_f{l}": [128, HID_T],
            f"w2_f{l}": [HID_T * 128, D], f"b2_f{l}": [1, D],
        })
    for n, shp in f32_ins.items():
        dr[n] = nc.dram_tensor(n, shp, F32, kind="ExternalInput").ap()
    dr["ident"] = nc.dram_tensor("ident", [128, 128], BF16, kind="ExternalInput").ap()
    dr["onesrow"] = nc.dram_tensor("onesrow", [1, 64], BF16, kind="ExternalInput").ap()
    dr["out"] = nc.dram_tensor("out", [S, D], F32, kind="ExternalOutput").ap()

    scr = {
        "tp": nc.dram_tensor("scr_tp", [S, D], BF16, kind="Internal").ap(),
        "tq": nc.dram_tensor("scr_tq", [S, INNER], BF16, kind="Internal").ap(),
        "tk": nc.dram_tensor("scr_tk", [S, INNER], BF16, kind="Internal").ap(),
        "xn": nc.dram_tensor("scr_xn", [S, D], F32, kind="Internal").ap(),
        "s": nc.dram_tensor("scr_s", [S, D], F32, kind="Internal").ap(),
    }
    tap_shapes = {
        "xn": ([S, D], F32), "sout": ([S, D], F32),
        "qTa": ([128, KT * S], BF16), "kTa": ([128, KT * S], BF16),
        "va": ([128, NT * HA * 65], BF16),
        "qf": ([128, NT * INNER], BF16), "kf": ([128, NT * INNER], BF16),
        "vf": ([128, NT * INNER], BF16), "of": ([128, 4 * S], BF16),
        "z1": ([S, D], F32), "z2": ([S, D], F32), "zf": ([S, D], F32),
        "xnT": ([128, KT * (S + 16)], BF16),
        "vb": ([128, NT * HA * 65], BF16),
        "h2T": ([128, KT * S], BF16),
    }
    taps = {}
    for n in debug_taps:
        shp, dt = tap_shapes[n]
        taps[n] = nc.dram_tensor("tap_" + n, shp, dt, kind="ExternalOutput").ap()

    r = Rec(nc)
    for s_ in ("qS", "qA", "qG", "tE", "vE", "aE", "gE",
               "bG0", "bG1", "bG2", "bG3", "bS0", "bS1", "bS2", "bS3", "bA0", "bA1"):
        r.sem(s_)
    avail = nc.sbuf_bytes_remaining // 4 - 16
    ar = Arena(nc, avail)
    P = PsumMgr(nc)
    b = _B(nc, r, ar, P, dr, scr, taps, set(debug_taps))
    b.prelude()
    if "p0" in phases:
        b.p0()
    if "p1" in phases:
        b.p1()
    if "p2a" in phases:
        b.p2(0)
    if "p2b" in phases:
        b.p2(1)
    if "p3" in phases:
        b.p3()
    b.fin()
    blk = nc.Block().__enter__()
    r.replay(blk)
    blk.__exit__(None, None, None)
    return nc


class _B:
    def __init__(self, nc, r, ar, P, dr, scr, taps, debug):
        self.nc, self.r, self.ar, self.P = nc, r, ar, P
        self.dr, self.scr, self.taps, self.debug = dr, scr, taps, debug
        self.B = {}

    def alloc(self, name, elems, dtype):
        self.B[name] = self.ar.alloc(name, elems, dtype)
        return self.B[name]

    def free(self, *names):
        for n in names:
            self.ar.free(n)
            del self.B[n]

    def tap(self, name, src_ap):
        if name in self.debug:
            self.r.fence(engs=("scalar",))
            self.r.dma("scalar", self.taps[name], src_ap)

    def prelude(self):
        r, dr = self.r, self.dr
        idb = self.alloc("ident", 128, BF16)
        ob = self.alloc("ones64", 64, BF16)
        mk = self.alloc("mask", 640, F32)
        r.dma("sync", idb[:, 0:128], dr["ident"])
        r.dma("sync", ob[64:65, 0:64], dr["onesrow"])
        r.dma("sync", mk[:, 0:640], dr["mask"])
        for n in ("gamma_a", "beta_a", "bout_a"):
            buf = self.alloc(n, D, F32)
            r.dma("sync", buf[:, 0:D], dr[n].broadcast_to([128, D]))

    def p0(self):
        r, dr, scr = self.r, self.dr, self.scr
        x = self.alloc("xw", NT * D, F32)
        xt = x.rearrange("p (t d) -> p t d", t=NT)
        st = self.alloc("st0", 4 * NT, F32)
        st3 = st.rearrange("p (a t) -> p a t", a=4)
        sq = self.alloc("sq0", D, F32)
        for t in range(NT):
            r.dma_b("scalar", "bA0", xt[:, t, :], dr["x"][t * 128:(t + 1) * 128, :])
        r.wait("scalar", r.marker("bA0"))
        for t in range(NT):
            r.op("scalar", lambda e, t=t: e.activation(
                sq[:, 0:D], xt[:, t, :], AF.Square, accum_out=st3[:, 1, t:t + 1]))
            r.op("scalar", lambda e, t=t: e.activation(
                sq[:, 0:D], xt[:, t, :], AF.Copy, accum_out=st3[:, 0, t:t + 1]),
                inc="aE")
        r.wait("vector", r.marker("aE"))
        r.opc("vector", lambda e: e.tensor_scalar(
            st3[:, 0, :], st3[:, 0, :], 1.0 / D, None, ALU.mult))
        r.opc("vector", lambda e: e.tensor_scalar(
            st3[:, 1, :], st3[:, 1, :], 1.0 / D, None, ALU.mult))
        r.opc("vector", lambda e: e.tensor_tensor(
            st3[:, 2, :], st3[:, 0, :], st3[:, 0, :], ALU.mult))
        r.opc("vector", lambda e: e.tensor_tensor(
            st3[:, 1, :], st3[:, 1, :], st3[:, 2, :], ALU.subtract))
        ev = r.op("vector", lambda e: e.tensor_scalar(
            st3[:, 1, :], st3[:, 1, :], LN_EPS, None, ALU.add), inc="vE")
        r.wait("scalar", ev)
        r.opc("scalar", lambda e: e.activation(st3[:, 2, :], st3[:, 1, :], AF.Ln))
        ev = r.op("scalar", lambda e: e.activation(
            st3[:, 2, :], st3[:, 2, :], AF.Exp, scale=-0.5), inc="aE")
        r.wait("vector", ev)
        for t in range(NT):
            ev = r.op("vector", lambda e, t=t: e.tensor_scalar(
                xt[:, t, :], xt[:, t, :], st3[:, 0, t:t + 1], st3[:, 2, t:t + 1],
                ALU.subtract, ALU.mult), inc="vE")
            r.wait("scalar", ev)
            r.wait("gpsimd", ev)
            r.dma("scalar", scr["xn"][t * 128:(t + 1) * 128, :], xt[:, t, :])
            r.dma("gpsimd", scr["tp"][t * 128:(t + 1) * 128, :], xt[:, t, :])
        self.tap("xn", x[:, 0:NT * D])
        dump = r.marker("qG")
        xnT = self.alloc("xnT", KT * (S + 16), BF16)
        xT3 = xnT.rearrange("p (k s) -> p k s", k=KT)
        r.op("vector", lambda e: e.memset(xT3[:, :, 0:16], 0.0), inc="vE")
        r.wait("sync", dump)
        for k in range(KT):
            r.dmat("sync", xT3[:, k, 16:S + 16], scr["tp"][:, k * 128:(k + 1) * 128])
        self.tap("xnT", xnT[:, 0:KT * (S + 16)])
        self.free("xw", "st0", "sq0")

    def p1(self):
        r, dr = self.r, self.dr
        r.fence()
        xT3 = self.B["xnT"].rearrange("p (k s) -> p k s", k=KT)
        qT = self.alloc("qTa", KT * S, BF16)
        kT = self.alloc("kTa", KT * S, BF16)
        va = self.alloc("va", NT * HA * 65, BF16)
        qT3 = qT.rearrange("p (k s) -> p k s", k=KT)
        kT3 = kT.rearrange("p (k s) -> p k s", k=KT)
        va3 = va.rearrange("p (t f) -> p t f", t=NT)
        va4 = va.rearrange("p (t h c) -> p t h c", t=NT, h=HA)
        r.op("vector", lambda e: e.memset(va4[:, :, :, 64:65], 1.0), inc="vE")

        wqk = self.alloc("wqk", 16 * 128, BF16)
        wqk3 = wqk.rearrange("p (b c) -> p b c", b=16)
        wv = self.alloc("wv", 16 * 512, BF16)
        wv3 = wv.rearrange("p (b c) -> p b c", b=16)
        slot_ev = {}
        ps = [self.P.alloc(f"pq{i}", 1) for i in range(4)]
        pev = [None] * 4
        pi = [0]
        r.wait("tensor", r.marker("qS"))

        def qk_part(wname, dstT, scale):
            for m in range(8):
                slots = []
                bs = "bG" + str(m % 2)
                for k in range(KT):
                    i = (m * KT + k) % 16
                    r.wait("gpsimd", slot_ev.get(("qk", i)))
                    r.dma_b("gpsimd", bs, wqk3[:, i, :],
                            dr[wname][k * 128:(k + 1) * 128, m * 128:(m + 1) * 128])
                    slots.append((i, None))
                batch_ev = r.marker(bs)
                for c in range(4):
                    p = pi[0] % 4
                    pi[0] += 1
                    r.wait("tensor", pev[p])
                    r.wait("tensor", batch_ev)
                    for k in range(KT):
                        i, ev = slots[k]
                        r.op("tensor", lambda e, p=p, i=i, k=k, c=c: e.matmul(
                            ps[p][:, 0:512], wqk3[:, i, :],
                            xT3[:, k, 16 + c * 512:16 + (c + 1) * 512],
                            start=(k == 0), stop=(k == KT - 1)),
                            inc="tE" if k == KT - 1 else None)
                    mm = r.marker("tE")
                    r.wait("vector", mm)
                    if scale == 1.0:
                        pev[p] = r.op("vector", lambda e, p=p, m=m, c=c: e.tensor_copy(
                            dstT[:, m, c * 512:(c + 1) * 512], ps[p][:, 0:512]),
                            inc="vE")
                    else:
                        pev[p] = r.op("vector", lambda e, p=p, m=m, c=c: e.tensor_scalar(
                            dstT[:, m, c * 512:(c + 1) * 512], ps[p][:, 0:512],
                            scale, None, ALU.mult), inc="vE")
                for (i, _e) in slots:
                    slot_ev[("qk", i)] = r.marker("tE")

        qk_part("wq_a", qT3, 0.125)
        qk_part("wk_a", kT3, 1.0)

        for nch in range(2):
            slots = []
            bs = "bG" + str(2 + nch)
            for k in range(KT):
                i = nch * 8 + k
                r.dma_b("gpsimd", bs, wv3[:, i, :],
                        dr["wv_a"][k * 128:(k + 1) * 128, nch * 512:(nch + 1) * 512])
                slots.append((i, None))
            batch_ev = r.marker(bs)
            for t in range(NT):
                p = pi[0] % 4
                pi[0] += 1
                r.wait("tensor", pev[p])
                r.wait("tensor", batch_ev)
                for k in range(KT):
                    i, ev = slots[k]
                    r.op("tensor", lambda e, p=p, i=i, k=k, t=t: e.matmul(
                        ps[p][:, 0:512], xT3[:, k, 16 + t * 128:16 + (t + 1) * 128],
                        wv3[:, i, :], start=(k == 0), stop=(k == KT - 1)),
                        inc="tE" if k == KT - 1 else None)
                mm = r.marker("tE")
                r.wait("vector", mm)
                dst = va4[:, t, nch * 8:(nch + 1) * 8, 0:64]
                srcv = ps[p][:, 0:512].rearrange("p (h c) -> p h c", h=8)
                pev[p] = r.op("vector", lambda e, dst=dst, srcv=srcv: e.tensor_copy(
                    dst, srcv), inc="vE")
            for (i, _e) in slots:
                slot_ev[("wv", i)] = r.marker("tE")

        for i in range(4):
            self.P.free(f"pq{i}")
        self.free("wqk", "wv")
        self.tap("qTa", self.B["qTa"][:, 0:KT * S])
        self.tap("kTa", self.B["kTa"][:, 0:KT * S])
        self.tap("va", self.B["va"][:, 0:NT * HA * 65])
        r.fence()
        self.free("xnT")

        wob = self.alloc("wob", KT * D, BF16)
        wo3 = wob.rearrange("p (k n) -> p k n", k=KT)
        for k in range(KT):
            r.dma_b("gpsimd", "bG0", wo3[:, k, :],
                    dr["wout_a"][k * 128:(k + 1) * 128, :])
        wout_ev = r.marker("bG0")

        wst = self.alloc("wst", 3 * 512, BF16)
        ws3 = wst.rearrange("p (b c) -> p b c", b=3)
        oT = self.alloc("oT", 2 * 8 * 512, BF16)
        oT4 = oT.rearrange("p (b k c) -> p b k c", b=2, k=8)
        rr = self.alloc("rr", 512, F32)
        rrb = self.alloc("rrb", 512, BF16)
        ub = self.alloc("ub", 512, BF16)
        xnc = self.alloc("xnc", 2 * 4 * D, F32)
        xc4 = xnc.rearrange("p (b t d) -> p b t d", b=2, t=4)
        yw = self.alloc("yw", D, F32)
        tw = self.alloc("tw", D, F32)
        sw = self.alloc("sw", D, F32)
        sta = self.alloc("sta", 8, F32)

        ps_sc = [self.P.alloc(f"sc{i}", 1) for i in range(3)]
        ps_u = [self.P.alloc(f"u{i}", 1) for i in range(2)]
        ps_bc = self.P.alloc("bc", 1)
        ps_at = self.P.alloc("at", 2)

        sc_guard = [None, None, None]
        ws_guard = [None, None, None]
        u_guard = [None, None]
        oT_guard = [None, None]
        bc_guard = None
        rr_guard = None
        at_guard = None
        yw_guard = None
        tw_guard = None
        sw_guard = None

        ones64v = self.B["ones64"]
        boutv = self.B["bout_a"]
        gammav = self.B["gamma_a"]
        betav = self.B["beta_a"]
        xnc_guard = [None, None]
        for c in range(4):
            ob = c % 2
            r.wait("vector", oT_guard[ob])
            r.wait("scalar", xnc_guard[ob])
            for t in range(4):
                r.dma_b("scalar", "bA" + str(ob), xc4[:, ob, t, :],
                        self.scr["xn"][(4 * c + t) * 128:(4 * c + t + 1) * 128, :])
            xn_ev = r.marker("bA" + str(ob))
            for h in range(HA):
                kc, kp = h // 2, (h % 2) * 64
                up = h % 2
                r.wait("tensor", u_guard[up])
                ev_evs = [None] * NT

                def emit_scores(t, sp):
                    r.wait("tensor", sc_guard[sp])
                    ev_mm = r.op("tensor", lambda e, kc=kc, kp=kp, t=t, sp=sp, c=c: e.matmul(
                        ps_sc[sp][:, 0:512],
                        kT3[kp:kp + 64, kc, t * 128:(t + 1) * 128],
                        qT3[kp:kp + 64, kc, c * 512:(c + 1) * 512],
                        start=True, stop=True, tile_position=(kp, 0)), inc="tE")
                    eng = "vector" if t % 2 == 0 else "scalar"
                    r.wait(eng, ev_mm)
                    r.wait(eng, ws_guard[sp])
                    if eng == "vector":
                        ev_evs[t] = r.op("vector", lambda e, sp=sp: e.tensor_scalar(
                            ws3[:, sp, :], ps_sc[sp][:, 0:512], 0.0, None,
                            ALU.max), inc="vE")
                    else:
                        ev_evs[t] = r.op("scalar", lambda e, sp=sp: e.activation(
                            ws3[:, sp, :], ps_sc[sp][:, 0:512], AF.Relu), inc="aE")

                def emit_av(t, sp):
                    r.wait("tensor", ev_evs[t])
                    r.op("tensor", lambda e, h=h, t=t, sp=sp, up=up: e.matmul(
                        ps_u[up][0:65, 0:512], va3[:, t, 65 * h:65 * h + 65],
                        ws3[:, sp, :], start=(t == 0), stop=(t == NT - 1)),
                        inc="tE")
                    sc_guard[sp] = r.marker("tE")
                    ws_guard[sp] = sc_guard[sp]

                emit_scores(0, 0)
                emit_scores(1, 1)
                for t in range(NT):
                    if t + 2 < NT:
                        emit_scores(t + 2, (t + 2) % 3)
                    emit_av(t, t % 3)
                u_done = r.marker("tE")
                r.wait("vector", u_done)
                r.wait("vector", rr_guard)
                r.op("vector", lambda e, up=up: e.tensor_scalar(
                    rr[64:65, 0:512], ps_u[up][64:65, 0:512], 1e-6, None, ALU.add))
                with_lp = self.nc.allow_low_precision
                def _recip(e):
                    with with_lp("bf16 normalizer is a matmul operand"):
                        return e.reciprocal(rrb[64:65, 0:512], rr[64:65, 0:512])
                ev_r = r.op("vector", _recip, inc="vE")
                u_guard[up] = r.op("vector", lambda e, up=up: e.tensor_copy(
                    ub[0:64, 0:512], ps_u[up][0:64, 0:512]), inc="vE")
                r.wait("tensor", ev_r)
                r.wait("tensor", bc_guard)
                ev_bc = r.op("tensor", lambda e: e.matmul(
                    ps_bc[0:64, 0:512], ones64v[64:65, 0:64],
                    rrb[64:65, 0:512], start=True, stop=True), inc="tE")
                rr_guard = ev_bc
                r.wait("vector", ev_bc)
                dsto = oT4[:, ob, h // 2, :][kp:kp + 64, :]
                bc_guard = r.op("vector", lambda e, dsto=dsto: e.tensor_tensor(
                    dsto, ub[0:64, 0:512], ps_bc[0:64, 0:512], ALU.mult), inc="vE")
            oT_ready = r.marker("vE")

            r.wait("tensor", wout_ev)
            r.wait("tensor", oT_ready)
            for t in range(4):
                stile = 4 * c + t
                r.wait("tensor", at_guard)
                for k in range(KT):
                    for nch in range(2):
                        r.op("tensor", lambda e, ob=ob, k=k, t=t, nch=nch: e.matmul(
                            ps_at[:, nch * 512:(nch + 1) * 512],
                            oT4[:, ob, k, t * 128:(t + 1) * 128],
                            wo3[:, k, nch * 512:(nch + 1) * 512],
                            start=(k == 0), stop=(k == KT - 1)),
                            inc="tE" if (k == KT - 1 and nch == 1) else None)
                mm = r.marker("tE")
                r.wait("vector", mm)
                r.wait("vector", xn_ev)
                r.wait("vector", yw_guard)
                at_guard = r.op("vector", lambda e: e.scalar_tensor_tensor(
                    yw[:, 0:D], ps_at[:, 0:1024], 1.0, boutv[:, 0:D],
                    ALU.mult, ALU.add), inc="vE")
                r.wait("gpsimd", at_guard)
                ev_y = r.op("gpsimd", lambda e, ob=ob, t=t: e.tensor_tensor(
                    yw[:, 0:D], yw[:, 0:D], xc4[:, ob, t, :], ALU.add),
                    inc="gE", inc_n=1)
                xnc_guard[ob] = ev_y
                r.wait("scalar", ev_y)
                r.wait("scalar", tw_guard)
                r.op("scalar", lambda e: e.activation(
                    tw[:, 0:D], yw[:, 0:D], AF.Square, accum_out=sta[:, 1:2]))
                ev = r.op("scalar", lambda e: e.activation(
                    tw[:, 0:D], yw[:, 0:D], AF.Copy, accum_out=sta[:, 0:1]),
                    inc="aE")
                r.wait("vector", ev)
                r.opc("vector", lambda e: e.tensor_scalar(
                    sta[:, 0:1], sta[:, 0:1], 1.0 / D, None, ALU.mult))
                r.opc("vector", lambda e: e.tensor_scalar(
                    sta[:, 2:3], sta[:, 1:2], 1.0 / D, None, ALU.mult))
                r.opc("vector", lambda e: e.tensor_tensor(
                    sta[:, 3:4], sta[:, 0:1], sta[:, 0:1], ALU.mult))
                r.opc("vector", lambda e: e.tensor_tensor(
                    sta[:, 2:3], sta[:, 2:3], sta[:, 3:4], ALU.subtract))
                ev = r.op("vector", lambda e: e.tensor_scalar(
                    sta[:, 2:3], sta[:, 2:3], LN_EPS, None, ALU.add), inc="vE")
                r.wait("scalar", ev)
                r.opc("scalar", lambda e: e.activation(sta[:, 2:3], sta[:, 2:3], AF.Ln))
                ev = r.op("scalar", lambda e: e.activation(
                    sta[:, 2:3], sta[:, 2:3], AF.Exp, scale=-0.5), inc="aE")
                r.wait("vector", ev)
                r.wait("vector", tw_guard)
                yw_guard = r.op("vector", lambda e: e.tensor_scalar(
                    tw[:, 0:D], yw[:, 0:D], sta[:, 0:1], sta[:, 2:3],
                    ALU.subtract, ALU.mult), inc="vE")
                r.wait("gpsimd", yw_guard)
                r.wait("gpsimd", sw_guard)
                tw_guard = r.op("gpsimd", lambda e: e.tensor_tensor(
                    sw[:, 0:D], tw[:, 0:D], gammav[:, 0:D], ALU.mult),
                    inc="gE", inc_n=1)
                ev = r.op("gpsimd", lambda e: e.tensor_tensor(
                    sw[:, 0:D], sw[:, 0:D], betav[:, 0:D], ALU.add),
                    inc="gE", inc_n=1)
                r.wait("sync", ev)
                sw_guard = r.dma("sync", self.scr["s"][stile * 128:(stile + 1) * 128, :],
                                 sw[:, 0:D])
            oT_guard[ob] = r.marker("tE")

        self.tap("vb", self.B["va"][:, 0:NT * HA * 65])
        if "sout" in self.debug:
            r.fence(engs=("scalar",))
            r.dma("scalar", self.taps["sout"], self.scr["s"])
        r.fence()
        for i in range(3):
            self.P.free(f"sc{i}")
        for i in range(2):
            self.P.free(f"u{i}")
        self.P.free("bc")
        self.P.free("at")
        self.free("qTa", "kTa", "va", "wob", "wst", "oT", "rr", "rrb", "ub",
                  "xnc", "yw", "tw", "sw", "sta")
        self.free("gamma_a", "beta_a", "bout_a")

    def p2(self, l):
        r, dr, scr = self.r, self.dr, self.scr
        r.fence()
        if l == 0:
            self.alloc("z", NT * D, F32)
        z = self.B["z"]
        zt = z.rearrange("p (t d) -> p t d", t=NT)
        if l == 0:
            for t in range(NT):
                r.dma("scalar", zt[:, t, :], scr["xn"][t * 128:(t + 1) * 128, :])
        else:
            for t in range(NT):
                r.dma("gpsimd", scr["tp"][t * 128:(t + 1) * 128, :], zt[:, t, :])
        zT = self.alloc("zT", KT * (S + 16), BF16)
        zT3 = zT.rearrange("p (k s) -> p k s", k=KT)
        zcol_ev = r.op("vector", lambda e: e.memset(zT3[:, :, 0:16], 0.0), inc="vE")
        r.wait("sync", r.marker("qG"))
        for k in range(KT):
            r.op("sync", lambda e, k=k: e.dma_start(
                zT3[:, k, 16:S + 16], scr["tp"][:, k * 128:(k + 1) * 128],
                transpose=True), inc="bS1", inc_n=16, is_dma=True)
        zT_ev = r.marker("bS1")
        z_ev = r.marker("qA")

        qs = self.alloc("qs8", HD, F32)
        ks = self.alloc("ks8", HD, F32)
        r.dma_b("sync", "bS0", qs[:, 0:HD], dr[f"qs_f{l}"].broadcast_to([128, HD]))
        r.dma_b("sync", "bS0", ks[:, 0:HD], dr[f"ks_f{l}"].broadcast_to([128, HD]))
        tabs = {}
        for tn in ("tqc", "tqs", "tkc", "tks"):
            tabs[tn] = self.alloc(tn, NT * HD, F32)
            r.dma_b("scalar", "bA0", tabs[tn][:, 0:NT * HD], dr[tn])
        tab3 = {tn: tabs[tn].rearrange("p (t d) -> p t d", t=NT) for tn in tabs}
        small_ev = r.marker("bS0")
        tab_ev = r.marker("bA0")
        r.wait("vector", small_ev)
        r.op("vector", lambda e: e.tensor_scalar(
            qs[:, 0:HD], qs[:, 0:HD], 8.0, None, ALU.mult), inc="vE")

        wb = {}
        for wn in ("wq", "wk", "wv"):
            buf = self.alloc(wn + "f", KT * 512, BF16)
            w3 = buf.rearrange("p (k c) -> p k c", k=KT)
            for k in range(KT):
                r.dma_b("gpsimd", "bG0", w3[:, k, :],
                        dr[f"{wn}_f{l}"][k * 128:(k + 1) * 128, :])
            wb[wn] = w3
        wf_ev = r.marker("bG0")

        qtok = self.alloc("qtok", NT * INNER, BF16)
        ktok = self.alloc("ktok", NT * INNER, BF16)
        vf = self.alloc("vf", NT * INNER, BF16)
        q3 = qtok.rearrange("p (t c) -> p t c", t=NT)
        k3 = ktok.rearrange("p (t c) -> p t c", t=NT)
        v3 = vf.rearrange("p (t c) -> p t c", t=NT)

        qw = self.alloc("qw", 512, F32)
        qq = self.alloc("qq", 512, F32)
        q1 = self.alloc("q1", 512, F32)
        t1 = self.alloc("t1", 512, F32)
        t2 = self.alloc("t2", 512, F32)
        nrm = self.alloc("nrm", 16, F32)

        ps = [self.P.alloc(f"pf{i}", 1) for i in range(4)]
        pev = [None] * 4
        pi = [0]
        r.wait("tensor", zT_ev)
        r.wait("tensor", zcol_ev)
        r.wait("tensor", wf_ev)
        r.wait("vector", tab_ev)
        chain = None

        for t in range(NT):
            for wn, dtok in (("wq", q3), ("wk", k3), ("wv", v3)):
                p = pi[0] % 4
                pi[0] += 1
                r.wait("tensor", pev[p])
                for k in range(KT):
                    coff = 16 if k < 4 else 15
                    r.op("tensor", lambda e, p=p, wn=wn, k=k, t=t, coff=coff: e.matmul(
                        ps[p][:, 0:512],
                        zT3[:, k, coff + t * 128:coff + t * 128 + 128],
                        wb[wn][:, k, :], start=(k == 0), stop=(k == KT - 1)),
                        inc="tE" if k == KT - 1 else None)
                mm = r.marker("tE")
                if wn == "wv":
                    r.wait("vector", mm)
                    pev[p] = r.op("vector", lambda e, p=p, t=t: e.tensor_copy(
                        v3[:, t, :], ps[p][:, 0:512]), inc="vE")
                    continue
                is_q = wn == "wq"
                sc = qs if is_q else ks
                ctn, stn = ("tqc", "tqs") if is_q else ("tkc", "tks")
                r.wait("vector", mm)
                r.wait("vector", chain)
                pev[p] = r.op("vector", lambda e, p=p: e.tensor_copy(
                    qw[:, 0:512], ps[p][:, 0:512]), inc="vE")
                r.wait("scalar", pev[p])
                ev1 = r.op("scalar", lambda e: e.activation(
                    qq[:, 0:512], qw[:, 0:512], AF.Square), inc="aE")
                r.wait("vector", ev1)
                r.opc("vector", lambda e: e.reduce_sum(
                    nrm[:, 0:8], qq[:, 0:512].rearrange("p (h d) -> p h d", h=8),
                    axis=AX.X))
                ev2 = r.op("vector", lambda e: e.tensor_scalar(
                    nrm[:, 0:8], nrm[:, 0:8], 1e-24, None, ALU.max), inc="vE")
                r.wait("scalar", ev2)
                r.opc("scalar", lambda e: e.activation(nrm[:, 8:16], nrm[:, 0:8], AF.Ln))
                ev3 = r.op("scalar", lambda e: e.activation(
                    nrm[:, 8:16], nrm[:, 8:16], AF.Exp, scale=-0.5), inc="aE")
                r.wait("vector", ev3)
                r.op("vector", lambda e: e.tensor_tensor(
                    q1[:, 0:512].rearrange("p (h d) -> p h d", h=8),
                    qw[:, 0:512].rearrange("p (h d) -> p h d", h=8),
                    nrm[:, 8:16].rearrange("p (h o) -> p h o", o=1
                                           ).broadcast_to([128, 8, HD]),
                    ALU.mult))
                r.op("vector", lambda e, sc=sc: e.tensor_tensor(
                    qw[:, 0:512].rearrange("p (h d) -> p h d", h=8),
                    q1[:, 0:512].rearrange("p (h d) -> p h d", h=8),
                    sc[:, 0:HD].rearrange("p (o d) -> p o d", o=1
                                          ).broadcast_to([128, 8, HD]),
                    ALU.mult))
                r.op("vector", lambda e, ctn=ctn, t=t: e.tensor_tensor(
                    t1[:, 0:512].rearrange("p (h d) -> p h d", h=8),
                    qw[:, 0:512].rearrange("p (h d) -> p h d", h=8),
                    tab3[ctn][:, t, :].rearrange("p (o d) -> p o d", o=1
                                                 ).broadcast_to([128, 8, HD]),
                    ALU.mult))
                q14 = qw[:, 0:512].rearrange("p (h r d) -> p h r d", h=8, r=2)
                t24 = t2[:, 0:512].rearrange("p (h r d) -> p h r d", h=8, r=2)
                st4 = tab3[stn][:, t, :].rearrange("p (o r d) -> p o r d", o=1, r=2)
                r.op("vector", lambda e, q14=q14, t24=t24, st4=st4: e.tensor_tensor(
                    t24[:, :, 0:1, :], q14[:, :, 1:2, :],
                    st4[:, :, 0:1, :].broadcast_to([128, 8, 1, 32]), ALU.mult))
                r.op("vector", lambda e, q14=q14, t24=t24, st4=st4: e.tensor_tensor(
                    t24[:, :, 1:2, :], q14[:, :, 0:1, :],
                    st4[:, :, 1:2, :].broadcast_to([128, 8, 1, 32]), ALU.mult))
                chain = r.op("vector", lambda e, dtok=dtok, t=t: e.tensor_tensor(
                    dtok[:, t, :], t1[:, 0:512], t2[:, 0:512], ALU.add), inc="vE")

        for i in range(4):
            self.P.free(f"pf{i}")
        self.free("wqf", "wkf", "wvf", "qw", "qq", "q1", "t1", "t2",
                  "tqc", "tqs", "tkc", "tks", "qs8", "ks8")
        self.tap("qf", self.B["qtok"][:, 0:NT * INNER])
        self.tap("kf", self.B["ktok"][:, 0:NT * INNER])
        self.tap("vf", self.B["vf"][:, 0:NT * INNER])
        r.fence()
        self.free("zT")

        for t in range(NT):
            r.dma("scalar", scr["tq"][t * 128:(t + 1) * 128, :], q3[:, t, :])
            r.dma("scalar", scr["tk"][t * 128:(t + 1) * 128, :], k3[:, t, :])
        dmp = r.marker("qA")
        qfT = self.alloc("qfT", 4 * S, BF16)
        kfT = self.alloc("kfT", 4 * S, BF16)
        qfT3 = qfT.rearrange("p (k s) -> p k s", k=4)
        kfT3 = kfT.rearrange("p (k s) -> p k s", k=4)
        r.wait("sync", dmp)
        for k in range(4):
            r.op("sync", lambda e, k=k: e.dma_start(
                qfT3[:, k, :], scr["tq"][:, k * 128:(k + 1) * 128],
                transpose=True), inc="bS1", inc_n=16, is_dma=True)
            r.op("sync", lambda e, k=k: e.dma_start(
                kfT3[:, k, :], scr["tk"][:, k * 128:(k + 1) * 128],
                transpose=True), inc="bS1", inc_n=16, is_dma=True)
        qkT_ev = r.marker("bS1")
        r.fence(engs=("vector", "scalar", "gpsimd", "tensor"), sems=("qA",))
        self.free("qtok", "ktok")

        wob = self.alloc("wobf", 4 * D, BF16)
        wo3 = wob.rearrange("p (k n) -> p k n", k=4)
        for k in range(4):
            r.dma_b("gpsimd", "bG1", wo3[:, k, :],
                    dr[f"wo_f{l}"][k * 128:(k + 1) * 128, :])
        wo_ev = r.marker("bG1")

        of = self.alloc("of", 4 * S, BF16)
        of3 = of.rearrange("p (k s) -> p k s", k=4)
        msc = self.alloc("msc", 2 * 640, F32)
        msc3 = msc.rearrange("p (b c) -> p b c", b=2)
        attn = self.alloc("attn", 2 * 640, BF16)
        attn3 = attn.rearrange("p (b c) -> p b c", b=2)
        atT = self.alloc("atT", 2 * 640, BF16)
        atT3 = atT.rearrange("p (b c) -> p b c", b=2)
        colsb = self.alloc("cols", 8, F32)
        cols = colsb.rearrange("p (b c) -> p b c", b=2)

        ps_sc = [self.P.alloc(f"fs{i}", 2) for i in range(2)]
        ps_tp = [self.P.alloc(f"ft{i}", 1, BF16) for i in range(2)]
        ps_o = [self.P.alloc(f"fo{i}", 1) for i in range(2)]

        pssc_guard = [None, None]
        msc_guard = [None, None]
        attn_guard = [None, None]
        pstp_guard = [None, None]
        atT_guard = [None, None]
        pso_guard = [None, None]

        r.wait("tensor", qkT_ev)
        mkv = self.B["mask"]
        identv = self.B["ident"]

        tiles = [(h, a) for h in range(HF) for a in range(NT)]
        nm_ev = [None] * len(tiles)
        tT_ev = [None] * len(tiles)

        def s_scores(i):
            h, a = tiles[i]
            kc, kp = h // 2, (h % 2) * 64
            wmax = min(a, 4)
            W = (wmax + 1) * 128
            k0 = a - wmax
            par = i % 2
            r.wait("tensor", pssc_guard[par])
            nleft, off = W, 0
            while nleft > 0:
                n = min(512, nleft)
                r.op("tensor", lambda e, kc=kc, kp=kp, a=a, k0=k0, off=off, n=n, par=par: e.matmul(
                    ps_sc[par][:, off:off + n],
                    qfT3[kp:kp + 64, kc, a * 128:(a + 1) * 128],
                    kfT3[kp:kp + 64, kc, k0 * 128 + off:k0 * 128 + off + n],
                    start=True, stop=True, tile_position=(kp, 0)),
                    inc="tE" if n == nleft else None)
                off += n
                nleft -= n
            mm = r.marker("tE")
            r.wait("vector", mm)
            r.wait("vector", msc_guard[par])
            pssc_guard[par] = r.op("vector", lambda e, par=par, W=W: e.scalar_tensor_tensor(
                msc3[:, par, 0:W], ps_sc[par][:, 0:W], 1.0,
                mkv[:, 640 - W:640], ALU.mult, ALU.add), inc="vE")
            r.opc("vector", lambda e, par=par, W=W: e.reduce_max(
                cols[:, par, 0:1], msc3[:, par, 0:W], axis=AX.X))
            ev_neg = r.op("vector", lambda e, par=par: e.tensor_scalar(
                cols[:, par, 1:2], cols[:, par, 0:1], -1.0, None, ALU.mult), inc="vE")
            r.wait("scalar", ev_neg)
            r.wait("scalar", attn_guard[par])
            ev_ex = r.op("scalar", lambda e, par=par, W=W: e.activation(
                attn3[:, par, 0:W], msc3[:, par, 0:W], AF.Exp,
                bias=cols[:, par, 1:2], accum_out=cols[:, par, 2:3]), inc="aE")
            msc_guard[par] = ev_ex
            r.wait("vector", ev_ex)
            r.opc("vector", lambda e, par=par: e.reciprocal(
                cols[:, par, 3:4], cols[:, par, 2:3]))
            nm_ev[i] = r.op("vector", lambda e, par=par, W=W: e.tensor_scalar(
                attn3[:, par, 0:W], attn3[:, par, 0:W], cols[:, par, 3:4], None,
                ALU.mult), inc="vE")

        def s_transp(i):
            h, a = tiles[i]
            wmax = min(a, 4)
            W = (wmax + 1) * 128
            par = i % 2
            r.wait("tensor", nm_ev[i])
            r.wait("tensor", pstp_guard[par])
            for j in range(wmax + 1):
                r.op("tensor", lambda e, par=par, j=j: e.transpose(
                    ps_tp[par][:, j * 128:(j + 1) * 128],
                    attn3[:, par, j * 128:(j + 1) * 128],
                    identv[:, 0:128]),
                    inc="tE" if j == wmax else None)
            attn_guard[par] = r.marker("tE")
            r.wait("vector", attn_guard[par])
            r.wait("vector", atT_guard[par])
            tT_ev[i] = r.op("vector", lambda e, par=par, W=W: e.tensor_copy(
                atT3[:, par, 0:W], ps_tp[par][:, 0:W]), inc="vE")
            pstp_guard[par] = tT_ev[i]

        def s_av(i):
            h, a = tiles[i]
            kc, kp = h // 2, (h % 2) * 64
            wmax = min(a, 4)
            k0 = a - wmax
            par = i % 2
            r.wait("tensor", pso_guard[par])
            r.wait("tensor", tT_ev[i])
            for j in range(wmax + 1):
                kt = k0 + j
                r.op("tensor", lambda e, h=h, kt=kt, j=j, par=par, wmax=wmax: e.matmul(
                    ps_o[par][0:64, 0:128], v3[:, kt, 64 * h:64 * h + 64],
                    atT3[:, par, j * 128:(j + 1) * 128],
                    start=(j == 0), stop=(j == wmax)),
                    inc="tE" if j == wmax else None)
            atT_guard[par] = r.marker("tE")
            r.wait("vector", atT_guard[par])
            pso_guard[par] = r.op("vector", lambda e, kp=kp, kc=kc, a=a, par=par: e.tensor_copy(
                of3[:, kc, a * 128:(a + 1) * 128][kp:kp + 64, :],
                ps_o[par][0:64, 0:128]), inc="vE")

        s_scores(0)
        for i in range(len(tiles)):
            if i + 1 < len(tiles):
                s_scores(i + 1)
            s_transp(i)
            s_av(i)

        of_ev = r.marker("vE")
        for i in range(2):
            self.P.free(f"fs{i}")
            self.P.free(f"ft{i}")
            self.P.free(f"fo{i}")
        if l == 0:
            self.tap("of", self.B["of"][:, 0:4 * S])
        self.free("msc", "attn", "atT", "cols", "qfT", "kfT", "vf")

        ps_d = [self.P.alloc(f"pd{i}", 2) for i in range(2)]
        dguard = [None, None]
        r.wait("tensor", of_ev)
        r.wait("tensor", wo_ev)
        r.wait("vector", z_ev)
        for t in range(NT):
            par = t % 2
            r.wait("tensor", dguard[par])
            for k in range(4):
                for nch in range(2):
                    r.op("tensor", lambda e, k=k, t=t, nch=nch, par=par: e.matmul(
                        ps_d[par][:, nch * 512:(nch + 1) * 512],
                        of3[:, k, t * 128:(t + 1) * 128],
                        wo3[:, k, nch * 512:(nch + 1) * 512],
                        start=(k == 0), stop=(k == 3)),
                        inc="tE" if (k == 3 and nch == 1) else None)
            mm = r.marker("tE")
            r.wait("vector", mm)
            dguard[par] = r.op("vector", lambda e, t=t, par=par: e.tensor_tensor(
                zt[:, t, :], zt[:, t, :], ps_d[par][:, 0:1024], ALU.add), inc="vE")
        for i in range(2):
            self.P.free(f"pd{i}")
        self.free("of", "wobf")
        if l == 0:
            self.tap("z1", z[:, 0:NT * D])
        r.fence()

        lg = self.alloc("lg", D, F32)
        lb = self.alloc("lb", D, F32)
        r.dma_b("sync", "bS0", lg[:, 0:D], dr[f"lg_f{l}"].broadcast_to([128, D]))
        r.dma_b("sync", "bS0", lb[:, 0:D], dr[f"lb_f{l}"].broadcast_to([128, D]))
        lgb_ev = r.marker("bS0")
        sh = self.alloc("sh", 2 * 512, F32)
        sh3 = sh.rearrange("p (b c) -> p b c", b=2)
        h2b = self.alloc("h2b", 2 * D, BF16)
        h2b3 = h2b.rearrange("p (b c) -> p b c", b=2)
        aw = self.alloc("aw", D, F32)
        stc = self.alloc("stc", 8, F32)
        sh_guard = [None, None]
        h2b_guard = [None, None]
        aw_guard = None
        r.wait("gpsimd", lgb_ev)
        for t in range(NT):
            par = t % 2
            r.wait("sync", sh_guard[par])
            bsh = "bS" + str(2 + par)
            if t == 0:
                ev0 = r.op("vector", lambda e, par=par: e.memset(
                    sh3[0:1, par, :], 0.0), inc="vE")
                r.wait("sync", ev0)
            else:
                r.dma_b("sync", bsh, sh3[0:1, par, :], zt[127:128, t - 1, 512:1024])
            r.dma_b("sync", bsh, sh3[1:128, par, :], zt[0:127, t, 512:1024])
            shd = r.marker(bsh)
            r.wait("scalar", shd)
            r.wait("scalar", aw_guard)
            r.op("scalar", lambda e, t=t: e.activation(
                aw[:, 0:512], zt[:, t, 0:512], AF.Copy, accum_out=stc[:, 0:1]))
            r.op("scalar", lambda e, par=par: e.activation(
                aw[:, 0:512], sh3[:, par, :], AF.Copy, accum_out=stc[:, 1:2]))
            r.op("scalar", lambda e, t=t: e.activation(
                aw[:, 0:512], zt[:, t, 0:512], AF.Square, accum_out=stc[:, 2:3]))
            ev = r.op("scalar", lambda e, par=par: e.activation(
                aw[:, 0:512], sh3[:, par, :], AF.Square, accum_out=stc[:, 3:4]),
                inc="aE")
            r.wait("vector", ev)
            r.opc("vector", lambda e: e.tensor_tensor(
                stc[:, 0:1], stc[:, 0:1], stc[:, 1:2], ALU.add))
            r.opc("vector", lambda e: e.tensor_tensor(
                stc[:, 2:3], stc[:, 2:3], stc[:, 3:4], ALU.add))
            r.opc("vector", lambda e: e.tensor_scalar(
                stc[:, 0:1], stc[:, 0:1], 1.0 / D, None, ALU.mult))
            r.opc("vector", lambda e: e.tensor_scalar(
                stc[:, 2:3], stc[:, 2:3], 1.0 / D, None, ALU.mult))
            r.opc("vector", lambda e: e.tensor_tensor(
                stc[:, 4:5], stc[:, 0:1], stc[:, 0:1], ALU.mult))
            r.opc("vector", lambda e: e.tensor_tensor(
                stc[:, 2:3], stc[:, 2:3], stc[:, 4:5], ALU.subtract))
            ev = r.op("vector", lambda e: e.tensor_scalar(
                stc[:, 2:3], stc[:, 2:3], LN_EPS, None, ALU.add), inc="vE")
            r.wait("scalar", ev)
            r.opc("scalar", lambda e: e.activation(stc[:, 2:3], stc[:, 2:3], AF.Ln))
            ev = r.op("scalar", lambda e: e.activation(
                stc[:, 2:3], stc[:, 2:3], AF.Exp, scale=-0.5), inc="aE")
            r.wait("vector", ev)
            r.wait("vector", h2b_guard[par])
            r.op("vector", lambda e, t=t: e.tensor_scalar(
                aw[:, 0:512], zt[:, t, 0:512], stc[:, 0:1], stc[:, 2:3],
                ALU.subtract, ALU.mult))
            ev = r.op("vector", lambda e, par=par: e.tensor_scalar(
                aw[:, 512:1024], sh3[:, par, :], stc[:, 0:1], stc[:, 2:3],
                ALU.subtract, ALU.mult), inc="vE")
            r.wait("gpsimd", ev)
            r.op("gpsimd", lambda e: e.tensor_tensor(
                aw[:, 0:D], aw[:, 0:D], lg[:, 0:D], ALU.mult), inc="gE", inc_n=1)
            ev = r.op("gpsimd", lambda e, par=par: e.tensor_tensor(
                h2b3[:, par, :], aw[:, 0:D], lb[:, 0:D], ALU.add),
                inc="gE", inc_n=1)
            aw_guard = ev
            sh_guard[par] = ev
            r.wait("scalar", ev)
            h2b_guard[par] = r.dma_b(
                "scalar", "bA" + str(par),
                scr["tp"][t * 128:(t + 1) * 128, :], h2b3[:, par, :])
        r.fence(engs=("sync",), sems=("qA", "bA0", "bA1"))
        dmp = r.marker("qA")
        h2T = self.alloc("h2T", KT * S, BF16)
        h2T3 = h2T.rearrange("p (k s) -> p k s", k=KT)
        r.wait("sync", dmp)
        for k in range(KT):
            r.op("sync", lambda e, k=k: e.dma_start(
                h2T3[:, k, :], scr["tp"][:, k * 128:(k + 1) * 128],
                transpose=True), inc="bS1", inc_n=16, is_dma=True)
        self.tap("h2T", self.B["h2T"][:, 0:KT * S])
        r.fence()
        self.free("lg", "lb", "sh", "h2b", "aw", "stc")

        b1a = self.alloc("b1a", HID_T, F32)
        b1g = self.alloc("b1g", HID_T, F32)
        b2 = self.alloc("b2", D, F32)
        r.dma_b("sync", "bS0", b1a[:, 0:HID_T], dr[f"b1a_f{l}"])
        r.dma_b("sync", "bS0", b1g[:, 0:HID_T], dr[f"b1g_f{l}"])
        r.dma_b("sync", "bS0", b2[:, 0:D], dr[f"b2_f{l}"].broadcast_to([128, D]))
        bia_ev = r.marker("bS0")
        w1b = self.alloc("w1b", 16 * 128, BF16)
        w1b3 = w1b.rearrange("p (b c) -> p b c", b=16)
        w2b = self.alloc("w2b", 11 * 1024, BF16)
        w2b3 = w2b.rearrange("p (k n) -> p k n", k=11)
        h4 = self.alloc("h4", HID_T * 1024, BF16)
        h43 = h4.rearrange("p (k s) -> p k s", k=HID_T)
        gelb = self.alloc("gel", 2 * 1024, F32)
        gel3 = gelb.rearrange("p (b c) -> p b c", b=2)
        tmp2 = self.alloc("tmp2", D, F32)
        slot_ev = {}
        si = [0]
        r.wait("vector", bia_ev)
        r.wait("scalar", bia_ev)
        h2T_mark = r.marker("bS1")
        r.wait("tensor", h2T_mark)

        pa_guard = [None, None]
        pg_guard = [None, None]
        gel_guard = [None, None]
        tmp2_guard = None
        pw_guard = [None, None]
        h4_guard = {}
        w2slot_guard = {}

        for half in range(2):
            scol = half * 1024
            ps_a = [self.P.alloc(f"pa{i}", 2) for i in range(2)]
            ps_g = [self.P.alloc(f"pg{i}", 2) for i in range(2)]
            for kt in range(HID_T):
                par = kt % 2
                rows = 128 if kt < HID_T - 1 else HID - 128 * (HID_T - 1)
                for ag in range(2):
                    wsrc = f"w1a_f{l}" if ag == 0 else f"w1g_f{l}"
                    psAG = ps_a[par] if ag == 0 else ps_g[par]
                    slots = []
                    bs = "bG" + str(ag)
                    for k in range(KT):
                        i = si[0] % 16
                        si[0] += 1
                        r.wait("gpsimd", slot_ev.get(i))
                        r.dma_b("gpsimd", bs, w1b3[:, i, 0:rows],
                                dr[wsrc][k * 128:(k + 1) * 128,
                                         kt * 128:kt * 128 + rows])
                        slots.append((i, None))
                    batch_ev = r.marker(bs)
                    r.wait("tensor", pa_guard[par] if ag == 0 else pg_guard[par])
                    r.wait("tensor", batch_ev)
                    for k in range(KT):
                        i, ev = slots[k]
                        for nch in range(2):
                            r.op("tensor", lambda e, i=i, k=k, nch=nch, psAG=psAG, scol=scol, rows=rows: e.matmul(
                                psAG[0:rows, nch * 512:(nch + 1) * 512],
                                w1b3[:, i, 0:rows],
                                h2T3[:, k, scol + nch * 512:scol + (nch + 1) * 512],
                                start=(k == 0), stop=(k == KT - 1)),
                                inc="tE" if (k == KT - 1 and nch == 1) else None)
                    mm = r.marker("tE")
                    for (i, _e) in slots:
                        slot_ev[i] = mm
                    if ag == 0:
                        pa_mm = mm
                    else:
                        pg_mm = mm
                r.wait("scalar", pg_mm)
                r.wait("scalar", gel_guard[par])
                pg_guard[par] = r.op("scalar", lambda e, kt=kt, rows=rows, par=par: e.activation(
                    gel3[0:rows, par, :], ps_g[par][0:rows, 0:1024], AF.Gelu,
                    bias=b1g[0:rows, kt:kt + 1]), inc="aE")
                r.wait("vector", pg_guard[par])
                r.wait("vector", pa_mm)
                r.wait("vector", h4_guard.get(kt))
                pa_guard[par] = r.op("vector", lambda e, kt=kt, rows=rows, par=par: e.scalar_tensor_tensor(
                    h43[0:rows, kt, 0:1024], ps_a[par][0:rows, 0:1024],
                    b1a[0:rows, kt:kt + 1], gel3[0:rows, par, :],
                    ALU.add, ALU.mult), inc="vE")
                gel_guard[par] = pa_guard[par]
                h4_guard[kt] = pa_guard[par]
            h4_done = r.marker("vE")
            r.wait("tensor", h4_done)
            for i in range(2):
                self.P.free(f"pa{i}")
                self.P.free(f"pg{i}")
            ps_w = [self.P.alloc(f"pw{i}", 2) for i in range(2)]
            pa_guard = [None, None]
            pg_guard = [None, None]

            for kh in range(2):
                kts = list(range(0, 11)) if kh == 0 else list(range(11, HID_T))
                bs = "bG" + str(2 + kh)
                for j, kt in enumerate(kts):
                    r.wait("gpsimd", w2slot_guard.get(j))
                    r.dma_b("gpsimd", bs, w2b3[:, j, :],
                            dr[f"w2_f{l}"][kt * 128:(kt + 1) * 128, :])
                batch_ev = r.marker(bs)
                r.wait("tensor", h4_done)
                r.wait("tensor", batch_ev)
                for ti in range(8):
                    t = half * 8 + ti
                    par = ti % 2
                    r.wait("tensor", pw_guard[par])
                    for j, kt in enumerate(kts):
                        for nch in range(2):
                            r.op("tensor", lambda e, j=j, kt=kt, ti=ti, nch=nch, par=par: e.matmul(
                                ps_w[par][:, nch * 512:(nch + 1) * 512],
                                h43[:, kt, ti * 128:(ti + 1) * 128],
                                w2b3[:, j, nch * 512:(nch + 1) * 512],
                                start=(j == 0), stop=(j == len(kts) - 1)),
                                inc="tE" if (j == len(kts) - 1 and nch == 1) else None)
                    mm = r.marker("tE")
                    r.wait("vector", mm)
                    if kh == 0:
                        pw_guard[par] = r.op("vector", lambda e, t=t, par=par: e.tensor_tensor(
                            zt[:, t, :], zt[:, t, :], ps_w[par][:, 0:1024], ALU.add),
                            inc="vE")
                    else:
                        r.wait("vector", tmp2_guard)
                        ev = r.op("vector", lambda e, par=par: e.scalar_tensor_tensor(
                            tmp2[:, 0:D], ps_w[par][:, 0:1024], 1.0, b2[:, 0:D],
                            ALU.mult, ALU.add), inc="vE")
                        r.wait("gpsimd", ev)
                        pw_guard[par] = r.op("gpsimd", lambda e, t=t: e.tensor_tensor(
                            zt[:, t, :], zt[:, t, :], tmp2[:, 0:D], ALU.add),
                            inc="gE", inc_n=1)
                        tmp2_guard = pw_guard[par]
                wmark = r.marker("tE")
                for j in range(len(kts)):
                    w2slot_guard[j] = wmark
            hm = r.marker("tE")
            for kt in range(HID_T):
                h4_guard[kt] = hm
            for i in range(2):
                self.P.free(f"pw{i}")
            pw_guard = [None, None]
            r.fence(engs=("tensor",), sems=("vE", "gE"))
        r.fence()
        self.free("b1a", "b1g", "b2", "w1b", "w2b", "h4", "gel", "tmp2", "h2T")
        self.tap("z2" if l == 0 else "zf", z[:, 0:NT * D])

    def p3(self):
        r, dr, scr = self.r, self.dr, self.scr
        r.fence()
        zt = self.B["z"].rearrange("p (t d) -> p t d", t=NT)
        xb = self.alloc("xb3", 2 * D, F32)
        xb3 = xb.rearrange("p (b c) -> p b c", b=2)
        sb = self.alloc("sb3", 2 * D, F32)
        sb3 = sb.rearrange("p (b c) -> p b c", b=2)
        gw = self.alloc("gw3", D, F32)
        ow = self.alloc("ow3", D, F32)
        buf_guard = [None, None]
        ow_guard = None
        gw_guard = None
        for t in range(NT):
            par = t % 2
            r.wait("scalar", buf_guard[par])
            e1 = r.dma_b("scalar", "bA" + str(par), xb3[:, par, :],
                         dr["x"][t * 128:(t + 1) * 128, :])
            e2 = r.dma_b("scalar", "bA" + str(par), sb3[:, par, :],
                         scr["s"][t * 128:(t + 1) * 128, :])
            r.wait("scalar", gw_guard)
            ev_g = r.op("scalar", lambda e, t=t: e.activation(
                gw[:, 0:D], zt[:, t, :], AF.Silu), inc="aE")
            r.wait("vector", ev_g)
            r.wait("vector", e2)
            r.wait("vector", ow_guard)
            gw_guard = r.op("vector", lambda e, par=par: e.tensor_tensor(
                ow[:, 0:D], gw[:, 0:D], sb3[:, par, :], ALU.mult), inc="vE")
            ev_o = r.op("vector", lambda e, par=par: e.tensor_tensor(
                ow[:, 0:D], ow[:, 0:D], xb3[:, par, :], ALU.add), inc="vE")
            r.wait("sync", ev_o)
            st_ev = r.dma("sync", dr["out"][t * 128:(t + 1) * 128, :], ow[:, 0:D])
            ow_guard = st_ev
            buf_guard[par] = st_ev

    def fin(self):
        self.r.fence(engs=("sync",))



def kernel(**inputs):
    g = host_layout(inputs)
    consts = host_constants()
    nc = build()
    x = np.asarray(inputs["L0_attn_x"], dtype=np.float32)
    in_maps = []
    for core in range(8):
        m = {"x": np.ascontiguousarray(x[core])}
        m.update(g)
        m.update(consts)
        in_maps.append(m)
    res = run_bass_kernel_spmd(nc, in_maps, core_ids=list(range(8)))
    return np.stack([r["out"] for r in res.results], axis=0)


# version 05
# speedup vs baseline: 1.2805x; 1.0981x over previous
import numpy as np
import ml_dtypes
import concourse.bass as bass
import concourse.mybir as mybir
from concourse.bass_utils import run_bass_kernel_spmd

F32 = mybir.dt.float32
BF16 = mybir.dt.bfloat16
AF = mybir.ActivationFunctionType
ALU = mybir.AluOpType
AX = mybir.AxisListType

S = 2048
NT = 16
D = 1024
KT = 8
HA = 16
HF = 8
HD = 64
INNER = 512
HID = 2730
HID_T = 22
WIN = 512
NEG = -3.0e38
LN_EPS = 1e-5

ENGINES = ("tensor", "vector", "scalar", "sync", "gpsimd")


class Rec:
    def __init__(self, nc):
        self.nc = nc
        self.ops = {e: [] for e in ENGINES}
        self.sems = {}
        self.cnt = {}
        self.last_wait = {e: {} for e in ENGINES}

    def sem(self, name):
        self.sems[name] = self.nc.semaphore(name).__enter__()
        self.cnt[name] = 0

    SELF_SEM = {"vector": "vE", "scalar": "aE", "gpsimd": "gE"}

    def op(self, eng, fn, inc=None, inc_n=1, is_dma=False):
        own = None if is_dma else self.SELF_SEM.get(eng)
        if own is not None:
            self.wait(eng, (own, self.cnt[own]))
            if inc is None:
                inc = own
        if inc is None:
            self.ops[eng].append(fn)
            return None
        self.cnt[inc] += inc_n
        tgt = self.cnt[inc]
        sem = self.sems[inc]
        if own is not None and inc != own:
            self.cnt[own] += 1
            tgt2 = self.cnt[own]
            sem2 = self.sems[own]
            self.ops[eng].append(
                lambda e, f=fn, s=sem, n=inc_n, s2=sem2: f(e).then_inc(s, n).then_inc(s2, 1))
            return (inc, tgt)
        self.ops[eng].append(lambda e, f=fn, s=sem, n=inc_n: f(e).then_inc(s, n))
        return (inc, tgt)

    def dma_b(self, q, bsem, dst, src, **kw):
        return self.op(q, lambda e: e.dma_start(dst, src, **kw), inc=bsem,
                       inc_n=16, is_dma=True)

    def dma(self, q, dst, src, **kw):
        inc = {"sync": "qS", "scalar": "qA", "gpsimd": "qG"}[q]
        return self.op(q, lambda e: e.dma_start(dst, src, **kw), inc=inc,
                       inc_n=16, is_dma=True)

    def dmat(self, q, dst, src):
        inc = {"sync": "qS", "scalar": "qA"}[q]
        return self.op(q, lambda e: e.dma_start(dst, src, transpose=True),
                       inc=inc, inc_n=16, is_dma=True)

    def wait(self, eng, ev):
        if ev is None:
            return
        sname, val = ev
        if sname in ("qS", "qA", "qG"):
            val = self.cnt[sname]
        lw = self.last_wait[eng]
        if lw.get(sname, -1) >= val:
            return
        lw[sname] = val
        sem = self.sems[sname]
        self.ops[eng].append(lambda e, s=sem, v=val: e.wait_ge(s, v))

    def marker(self, sname):
        return (sname, self.cnt[sname])

    def opc(self, eng, fn):
        inc = {"vector": "vE", "scalar": "aE", "gpsimd": "qG"}[eng]
        ev = self.op(eng, fn, inc=inc)
        self.wait(eng, ev)
        return ev

    def fence(self, engs=ENGINES, sems=("qS", "qA", "qG", "tE", "vE", "aE", "gE",
                                        "bG0", "bG1", "bG2", "bG3", "bG4", "bG5",
                                        "bS0", "bS1", "bS2", "bS3", "bA0", "bA1")):
        marks = [self.marker(s) for s in sems if self.cnt[s] > 0]
        for e in engs:
            for m in marks:
                self.wait(e, m)

    def replay(self, block):
        for eng in ENGINES:
            ops = self.ops[eng]

            def body(e, ops=ops):
                for f in ops:
                    f(e)

            getattr(block, eng)(body)


class Arena:
    def __init__(self, nc, elems_f32):
        self.t = nc.alloc_sbuf_tensor("arena", [128, elems_f32], F32)
        self.free_list = [(0, elems_f32)]
        self.allocs = {}

    def alloc(self, name, elems, dtype):
        f32e = elems if dtype == F32 else (elems + 1) // 2
        align = 16
        for i, (off, size) in enumerate(self.free_list):
            aoff = (off + align - 1) // align * align
            pad = aoff - off
            if size >= f32e + pad:
                rest = size - f32e - pad
                frags = []
                if pad:
                    frags.append((off, pad))
                if rest:
                    frags.append((aoff + f32e, rest))
                self.free_list[i:i + 1] = frags
                self.allocs[name] = (aoff, f32e)
                ap = self.t[:, aoff:aoff + f32e]
                if dtype == BF16:
                    ap = ap.bitcast(BF16)[:, 0:elems]
                return ap
        raise MemoryError(f"arena: cannot alloc {name} ({f32e}); free={self.free_list}")

    def free(self, name):
        off, elems = self.allocs.pop(name)
        self.free_list.append((off, elems))
        self.free_list.sort()
        merged = []
        for o, sz in self.free_list:
            if merged and merged[-1][0] + merged[-1][1] == o:
                merged[-1] = (merged[-1][0], merged[-1][1] + sz)
            else:
                merged.append([o, sz])
        self.free_list = [tuple(x) for x in merged]


class PsumMgr:
    def __init__(self, nc):
        self.t = nc.alloc_psum_tensor("psum", [128, 4096], F32)
        self.free_banks = set(range(8))
        self.allocs = {}

    def alloc(self, name, banks, dtype=F32):
        for b0 in range(8 - banks + 1):
            if all(b0 + j in self.free_banks for j in range(banks)):
                for j in range(banks):
                    self.free_banks.discard(b0 + j)
                self.allocs[name] = (b0, banks)
                ap = self.t[:, b0 * 512:(b0 + banks) * 512]
                return ap.bitcast(BF16) if dtype == BF16 else ap
        raise MemoryError(f"psum {name}: need {banks}, free={sorted(self.free_banks)}")

    def free(self, name):
        b0, banks = self.allocs.pop(name)
        for j in range(banks):
            self.free_banks.add(b0 + j)



def host_constants():
    c = {}
    c["ident"] = np.eye(128, dtype=ml_dtypes.bfloat16)
    c["onesrow"] = np.ones((1, 64), dtype=ml_dtypes.bfloat16)
    r = np.arange(128)[:, None]
    cc = np.arange(640)[None, :]
    c["mask"] = np.where((cc >= r) & (cc <= r + 512), 0.0, NEG).astype(np.float32)
    half = np.arange(0, HD, 2, dtype=np.float64)
    inv_freq = 1.0 / 10000.0 ** (half / HD)
    t = np.arange(S, dtype=np.float64)
    freqs = t[:, None] * inv_freq[None, :]
    freqs = np.concatenate([freqs, freqs], axis=-1)
    base = (half + 0.4 * HD) / (1.4 * HD)
    power = (t - S // 2) / float(WIN // 2)
    scale = base[None, :] ** power[:, None]
    scale = np.concatenate([scale, scale], axis=-1)
    cos, sin = np.cos(freqs), np.sin(freqs)
    sign = np.concatenate([-np.ones(32), np.ones(32)])[None, :]

    def tok(x):
        return np.ascontiguousarray(
            x.reshape(NT, 128, HD).transpose(1, 0, 2).reshape(128, NT * HD)
        ).astype(np.float32)

    c["tqc"] = tok(cos * scale)
    c["tqs"] = tok(sin * scale * sign)
    c["tkc"] = tok(cos / scale)
    c["tks"] = tok(sin / scale * sign)
    return c


def host_layout(inputs):
    g = {}
    w3 = np.asarray(inputs["L0_attn_w_qkv"]).reshape(D, HA, 3, HD)
    g["wq_a"] = np.ascontiguousarray(w3[:, :, 0, :].reshape(D, D))
    g["wk_a"] = np.ascontiguousarray(w3[:, :, 1, :].reshape(D, D))
    g["wv_a"] = np.ascontiguousarray(w3[:, :, 2, :].reshape(D, D))
    g["wout_a"] = np.ascontiguousarray(np.asarray(inputs["L0_attn_w_out"]))
    g["bout_a"] = np.ascontiguousarray(np.asarray(inputs["L0_attn_b_out"]).reshape(1, D))
    g["gamma_a"] = np.ascontiguousarray(np.asarray(inputs["L0_attn_gamma"]).reshape(1, D))
    g["beta_a"] = np.ascontiguousarray(np.asarray(inputs["L0_attn_beta"]).reshape(1, D))
    for l in range(2):
        wf = np.asarray(inputs["L0_ffn_wqkv"])[l]
        g[f"wq_f{l}"] = np.ascontiguousarray(wf[:, 0:INNER])
        g[f"wk_f{l}"] = np.ascontiguousarray(wf[:, INNER:2 * INNER])
        g[f"wv_f{l}"] = np.ascontiguousarray(wf[:, 2 * INNER:3 * INNER])
        g[f"qs_f{l}"] = np.ascontiguousarray(
            np.asarray(inputs["L0_ffn_q_scale"])[l].reshape(1, HD))
        g[f"ks_f{l}"] = np.ascontiguousarray(
            np.asarray(inputs["L0_ffn_k_scale"])[l].reshape(1, HD))
        g[f"wo_f{l}"] = np.ascontiguousarray(np.asarray(inputs["L0_ffn_wo"])[l])
        g[f"lg_f{l}"] = np.ascontiguousarray(
            np.asarray(inputs["L0_ffn_ln_g"])[l].reshape(1, D))
        g[f"lb_f{l}"] = np.ascontiguousarray(
            np.asarray(inputs["L0_ffn_ln_b"])[l].reshape(1, D))
        w1 = np.asarray(inputs["L0_ffn_w1"])[l]
        g[f"w1a_f{l}"] = np.ascontiguousarray(w1[:, 0:HID])
        g[f"w1g_f{l}"] = np.ascontiguousarray(w1[:, HID:2 * HID])
        b1 = np.asarray(inputs["L0_ffn_b1"])[l]
        b1a = np.zeros((HID_T, 128), np.float32)
        b1g = np.zeros((HID_T, 128), np.float32)
        b1a.reshape(-1)[:HID] = b1[0:HID]
        b1g.reshape(-1)[:HID] = b1[HID:2 * HID]
        g[f"b1a_f{l}"] = np.ascontiguousarray(b1a.T)
        g[f"b1g_f{l}"] = np.ascontiguousarray(b1g.T)
        w2p = np.zeros((HID_T * 128, D), np.float32)
        w2p[:HID] = np.asarray(inputs["L0_ffn_w2"])[l]
        g[f"w2_f{l}"] = w2p
        g[f"b2_f{l}"] = np.ascontiguousarray(
            np.asarray(inputs["L0_ffn_b2"])[l].reshape(1, D))
    return g



def build(debug_taps=(), phases=("p0", "p1", "p2a", "p2b", "p3")):
    nc = bass.Bass("TRN2", target_bir_lowering=False)
    dr = {}
    dr["x"] = nc.dram_tensor("x", [S, D], F32, kind="ExternalInput").ap()
    f32_ins = {
        "wq_a": [D, D], "wk_a": [D, D], "wv_a": [D, D], "wout_a": [D, D],
        "bout_a": [1, D], "gamma_a": [1, D], "beta_a": [1, D],
        "mask": [128, 640],
        "tqc": [128, NT * HD], "tqs": [128, NT * HD],
        "tkc": [128, NT * HD], "tks": [128, NT * HD],
    }
    for l in range(2):
        f32_ins.update({
            f"wq_f{l}": [D, INNER], f"wk_f{l}": [D, INNER], f"wv_f{l}": [D, INNER],
            f"qs_f{l}": [1, HD], f"ks_f{l}": [1, HD],
            f"wo_f{l}": [INNER, D], f"lg_f{l}": [1, D], f"lb_f{l}": [1, D],
            f"w1a_f{l}": [D, HID], f"w1g_f{l}": [D, HID],
            f"b1a_f{l}": [128, HID_T], f"b1g_f{l}": [128, HID_T],
            f"w2_f{l}": [HID_T * 128, D], f"b2_f{l}": [1, D],
        })
    for n, shp in f32_ins.items():
        dr[n] = nc.dram_tensor(n, shp, F32, kind="ExternalInput").ap()
    dr["ident"] = nc.dram_tensor("ident", [128, 128], BF16, kind="ExternalInput").ap()
    dr["onesrow"] = nc.dram_tensor("onesrow", [1, 64], BF16, kind="ExternalInput").ap()
    dr["out"] = nc.dram_tensor("out", [S, D], F32, kind="ExternalOutput").ap()

    scr = {
        "tp": nc.dram_tensor("scr_tp", [S, D], BF16, kind="Internal").ap(),
        "tq": nc.dram_tensor("scr_tq", [S, INNER], BF16, kind="Internal").ap(),
        "tk": nc.dram_tensor("scr_tk", [S, INNER], BF16, kind="Internal").ap(),
        "xn": nc.dram_tensor("scr_xn", [S, D], F32, kind="Internal").ap(),
        "s": nc.dram_tensor("scr_s", [S, D], F32, kind="Internal").ap(),
    }
    tap_shapes = {
        "xn": ([S, D], F32), "sout": ([S, D], F32),
        "qTa": ([128, KT * S], BF16), "kTa": ([128, KT * S], BF16),
        "va": ([128, NT * HA * 65], BF16),
        "qf": ([128, NT * INNER], BF16), "kf": ([128, NT * INNER], BF16),
        "vf": ([128, NT * INNER], BF16), "of": ([128, 4 * S], BF16),
        "z1": ([S, D], F32), "z2": ([S, D], F32), "zf": ([S, D], F32),
        "xnT": ([128, KT * (S + 16)], BF16),
        "vb": ([128, NT * HA * 65], BF16),
        "h2T": ([128, KT * S], BF16),
    }
    taps = {}
    for n in debug_taps:
        shp, dt = tap_shapes[n]
        taps[n] = nc.dram_tensor("tap_" + n, shp, dt, kind="ExternalOutput").ap()

    r = Rec(nc)
    for s_ in ("qS", "qA", "qG", "tE", "vE", "aE", "gE",
               "bG0", "bG1", "bG2", "bG3", "bG4", "bG5", "bS0", "bS1", "bS2", "bS3", "bA0", "bA1"):
        r.sem(s_)
    avail = nc.sbuf_bytes_remaining // 4 - 16
    ar = Arena(nc, avail)
    P = PsumMgr(nc)
    b = _B(nc, r, ar, P, dr, scr, taps, set(debug_taps))
    b.prelude()
    if "p0" in phases:
        b.p0()
    if "p1" in phases:
        b.p1()
    if "p2a" in phases:
        b.p2(0)
    if "p2b" in phases:
        b.p2(1)
    if "p3" in phases:
        b.p3()
    b.fin()
    blk = nc.Block().__enter__()
    r.replay(blk)
    blk.__exit__(None, None, None)
    return nc


class _B:
    def __init__(self, nc, r, ar, P, dr, scr, taps, debug):
        self.nc, self.r, self.ar, self.P = nc, r, ar, P
        self.dr, self.scr, self.taps, self.debug = dr, scr, taps, debug
        self.B = {}

    def alloc(self, name, elems, dtype):
        self.B[name] = self.ar.alloc(name, elems, dtype)
        return self.B[name]

    def free(self, *names):
        for n in names:
            self.ar.free(n)
            del self.B[n]

    def tap(self, name, src_ap):
        if name in self.debug:
            self.r.fence(engs=("scalar",))
            self.r.dma("scalar", self.taps[name], src_ap)

    def prelude(self):
        r, dr = self.r, self.dr
        idb = self.alloc("ident", 128, BF16)
        ob = self.alloc("ones64", 64, BF16)
        mk = self.alloc("mask", 640, F32)
        r.dma("sync", idb[:, 0:128], dr["ident"])
        r.dma("sync", ob[64:65, 0:64], dr["onesrow"])
        r.dma("sync", mk[:, 0:640], dr["mask"])
        for n in ("gamma_a", "beta_a", "bout_a"):
            buf = self.alloc(n, D, F32)
            r.dma("sync", buf[:, 0:D], dr[n].broadcast_to([128, D]))

    def p0(self):
        r, dr, scr = self.r, self.dr, self.scr
        x = self.alloc("xw", NT * D, F32)
        xt = x.rearrange("p (t d) -> p t d", t=NT)
        st = self.alloc("st0", 4 * NT, F32)
        st3 = st.rearrange("p (a t) -> p a t", a=4)
        sq = self.alloc("sq0", D, F32)
        for t in range(NT):
            r.dma_b("scalar", "bA0", xt[:, t, :], dr["x"][t * 128:(t + 1) * 128, :])
        r.wait("scalar", r.marker("bA0"))
        for t in range(NT):
            r.op("scalar", lambda e, t=t: e.activation(
                sq[:, 0:D], xt[:, t, :], AF.Square, accum_out=st3[:, 1, t:t + 1]))
            r.op("scalar", lambda e, t=t: e.activation(
                sq[:, 0:D], xt[:, t, :], AF.Copy, accum_out=st3[:, 0, t:t + 1]),
                inc="aE")
        r.wait("vector", r.marker("aE"))
        r.opc("vector", lambda e: e.tensor_scalar(
            st3[:, 0, :], st3[:, 0, :], 1.0 / D, None, ALU.mult))
        r.opc("vector", lambda e: e.tensor_scalar(
            st3[:, 1, :], st3[:, 1, :], 1.0 / D, None, ALU.mult))
        r.opc("vector", lambda e: e.tensor_tensor(
            st3[:, 2, :], st3[:, 0, :], st3[:, 0, :], ALU.mult))
        r.opc("vector", lambda e: e.tensor_tensor(
            st3[:, 1, :], st3[:, 1, :], st3[:, 2, :], ALU.subtract))
        ev = r.op("vector", lambda e: e.tensor_scalar(
            st3[:, 1, :], st3[:, 1, :], LN_EPS, None, ALU.add), inc="vE")
        r.wait("scalar", ev)
        r.opc("scalar", lambda e: e.activation(st3[:, 2, :], st3[:, 1, :], AF.Ln))
        ev = r.op("scalar", lambda e: e.activation(
            st3[:, 2, :], st3[:, 2, :], AF.Exp, scale=-0.5), inc="aE")
        r.wait("vector", ev)
        for t in range(NT):
            ev = r.op("vector", lambda e, t=t: e.tensor_scalar(
                xt[:, t, :], xt[:, t, :], st3[:, 0, t:t + 1], st3[:, 2, t:t + 1],
                ALU.subtract, ALU.mult), inc="vE")
            r.wait("scalar", ev)
            r.wait("gpsimd", ev)
            r.dma("scalar", scr["xn"][t * 128:(t + 1) * 128, :], xt[:, t, :])
            r.dma("gpsimd", scr["tp"][t * 128:(t + 1) * 128, :], xt[:, t, :])
        self.tap("xn", x[:, 0:NT * D])
        dump = r.marker("qG")
        xnT = self.alloc("xnT", KT * (S + 16), BF16)
        xT3 = xnT.rearrange("p (k s) -> p k s", k=KT)
        r.op("vector", lambda e: e.memset(xT3[:, :, 0:16], 0.0), inc="vE")
        r.wait("sync", dump)
        for k in range(KT):
            r.dmat("sync", xT3[:, k, 16:S + 16], scr["tp"][:, k * 128:(k + 1) * 128])
        self.tap("xnT", xnT[:, 0:KT * (S + 16)])
        self.free("xw", "st0", "sq0")

    def p1(self):
        r, dr = self.r, self.dr
        r.fence()
        xT3 = self.B["xnT"].rearrange("p (k s) -> p k s", k=KT)
        qT = self.alloc("qTa", KT * S, BF16)
        kT = self.alloc("kTa", KT * S, BF16)
        va = self.alloc("va", NT * HA * 65, BF16)
        qT3 = qT.rearrange("p (k s) -> p k s", k=KT)
        kT3 = kT.rearrange("p (k s) -> p k s", k=KT)
        va3 = va.rearrange("p (t f) -> p t f", t=NT)
        va4 = va.rearrange("p (t h c) -> p t h c", t=NT, h=HA)
        r.op("vector", lambda e: e.memset(va4[:, :, :, 64:65], 1.0), inc="vE")

        wqk = self.alloc("wqk", 16 * 128, BF16)
        wqk3 = wqk.rearrange("p (b c) -> p b c", b=16)
        wv = self.alloc("wv", 16 * 512, BF16)
        wv3 = wv.rearrange("p (b c) -> p b c", b=16)
        slot_ev = {}
        ps = [self.P.alloc(f"pq{i}", 1) for i in range(4)]
        pev = [None] * 4
        pi = [0]
        r.wait("tensor", r.marker("qS"))

        def qk_part(wname, dstT, scale):
            for m in range(8):
                slots = []
                bs = "bG" + str(m % 2)
                mpar = m % 2
                r.wait("gpsimd", slot_ev.get(("qk", mpar)))
                r.dma_b("gpsimd", bs,
                        wqk3[:, mpar * 8:(mpar + 1) * 8, :].rearrange(
                            "p a c -> p a c"),
                        dr[wname][:, m * 128:(m + 1) * 128].rearrange(
                            "(k p) c -> p k c", p=128))
                for k in range(KT):
                    slots.append((mpar * 8 + k, None))
                batch_ev = r.marker(bs)
                for c in range(4):
                    p = pi[0] % 4
                    pi[0] += 1
                    r.wait("tensor", pev[p])
                    r.wait("tensor", batch_ev)
                    for k in range(KT):
                        i, ev = slots[k]
                        r.op("tensor", lambda e, p=p, i=i, k=k, c=c: e.matmul(
                            ps[p][:, 0:512], wqk3[:, i, :],
                            xT3[:, k, 16 + c * 512:16 + (c + 1) * 512],
                            start=(k == 0), stop=(k == KT - 1)),
                            inc="tE" if k == KT - 1 else None)
                    mm = r.marker("tE")
                    r.wait("vector", mm)
                    if scale == 1.0:
                        pev[p] = r.op("vector", lambda e, p=p, m=m, c=c: e.tensor_copy(
                            dstT[:, m, c * 512:(c + 1) * 512], ps[p][:, 0:512]),
                            inc="vE")
                    else:
                        pev[p] = r.op("vector", lambda e, p=p, m=m, c=c: e.tensor_scalar(
                            dstT[:, m, c * 512:(c + 1) * 512], ps[p][:, 0:512],
                            scale, None, ALU.mult), inc="vE")
                slot_ev[("qk", m % 2)] = r.marker("tE")

        qk_part("wq_a", qT3, 0.125)
        qk_part("wk_a", kT3, 1.0)

        for nch in range(2):
            slots = [(nch * 8 + k, None) for k in range(KT)]
            bs = "bG" + str(2 + nch)
            r.dma_b("gpsimd", bs, wv3[:, nch * 8:(nch + 1) * 8, :],
                    dr["wv_a"][:, nch * 512:(nch + 1) * 512].rearrange(
                        "(k p) c -> p k c", p=128))
            batch_ev = r.marker(bs)
            for t in range(NT):
                p = pi[0] % 4
                pi[0] += 1
                r.wait("tensor", pev[p])
                r.wait("tensor", batch_ev)
                for k in range(KT):
                    i, ev = slots[k]
                    r.op("tensor", lambda e, p=p, i=i, k=k, t=t: e.matmul(
                        ps[p][:, 0:512], xT3[:, k, 16 + t * 128:16 + (t + 1) * 128],
                        wv3[:, i, :], start=(k == 0), stop=(k == KT - 1)),
                        inc="tE" if k == KT - 1 else None)
                mm = r.marker("tE")
                r.wait("vector", mm)
                dst = va4[:, t, nch * 8:(nch + 1) * 8, 0:64]
                srcv = ps[p][:, 0:512].rearrange("p (h c) -> p h c", h=8)
                pev[p] = r.op("vector", lambda e, dst=dst, srcv=srcv: e.tensor_copy(
                    dst, srcv), inc="vE")
            for (i, _e) in slots:
                slot_ev[("wv", i)] = r.marker("tE")

        for i in range(4):
            self.P.free(f"pq{i}")
        self.free("wqk", "wv")
        self.tap("qTa", self.B["qTa"][:, 0:KT * S])
        self.tap("kTa", self.B["kTa"][:, 0:KT * S])
        self.tap("va", self.B["va"][:, 0:NT * HA * 65])
        r.fence()
        self.free("xnT")

        wob = self.alloc("wob", KT * D, BF16)
        wo3 = wob.rearrange("p (k n) -> p k n", k=KT)
        r.dma_b("gpsimd", "bG0", wo3[:, :, :],
                dr["wout_a"].rearrange("(k p) c -> p k c", p=128))
        wout_ev = r.marker("bG0")

        wst = self.alloc("wst", 3 * 512, BF16)
        ws3 = wst.rearrange("p (b c) -> p b c", b=3)
        oT = self.alloc("oT", 2 * 8 * 512, BF16)
        oT4 = oT.rearrange("p (b k c) -> p b k c", b=2, k=8)
        rr = self.alloc("rr", 512, F32)
        rrb = self.alloc("rrb", 512, BF16)
        ub = self.alloc("ub", 512, BF16)
        xnc = self.alloc("xnc", 2 * 4 * D, F32)
        xc4 = xnc.rearrange("p (b t d) -> p b t d", b=2, t=4)
        yw = self.alloc("yw", D, F32)
        tw = self.alloc("tw", D, F32)
        sw = self.alloc("sw", D, F32)
        sta = self.alloc("sta", 8, F32)

        ps_sc = [self.P.alloc(f"sc{i}", 1) for i in range(3)]
        ps_u = [self.P.alloc(f"u{i}", 1) for i in range(2)]
        ps_bc = self.P.alloc("bc", 1)
        ps_at = self.P.alloc("at", 2)

        sc_guard = [None, None, None]
        ws_guard = [None, None, None]
        u_guard = [None, None]
        oT_guard = [None, None]
        bc_guard = None
        rr_guard = None
        at_guard = None
        yw_guard = None
        tw_guard = None
        sw_guard = None

        ones64v = self.B["ones64"]
        boutv = self.B["bout_a"]
        gammav = self.B["gamma_a"]
        betav = self.B["beta_a"]
        xnc_guard = [None, None]
        for c in range(4):
            ob = c % 2
            r.wait("vector", oT_guard[ob])
            r.wait("scalar", xnc_guard[ob])
            for t in range(4):
                r.dma_b("scalar", "bA" + str(ob), xc4[:, ob, t, :],
                        self.scr["xn"][(4 * c + t) * 128:(4 * c + t + 1) * 128, :])
            xn_ev = r.marker("bA" + str(ob))
            for h in range(HA):
                kc, kp = h // 2, (h % 2) * 64
                up = h % 2
                r.wait("tensor", u_guard[up])
                ev_evs = [None] * NT

                def emit_scores(t, sp):
                    r.wait("tensor", sc_guard[sp])
                    ev_mm = r.op("tensor", lambda e, kc=kc, kp=kp, t=t, sp=sp, c=c: e.matmul(
                        ps_sc[sp][:, 0:512],
                        kT3[kp:kp + 64, kc, t * 128:(t + 1) * 128],
                        qT3[kp:kp + 64, kc, c * 512:(c + 1) * 512],
                        start=True, stop=True, tile_position=(kp, 0)), inc="tE")
                    eng = "vector" if t % 2 == 0 else "scalar"
                    r.wait(eng, ev_mm)
                    r.wait(eng, ws_guard[sp])
                    if eng == "vector":
                        ev_evs[t] = r.op("vector", lambda e, sp=sp: e.tensor_scalar(
                            ws3[:, sp, :], ps_sc[sp][:, 0:512], 0.0, None,
                            ALU.max), inc="vE")
                    else:
                        ev_evs[t] = r.op("scalar", lambda e, sp=sp: e.activation(
                            ws3[:, sp, :], ps_sc[sp][:, 0:512], AF.Relu), inc="aE")

                def emit_av(t, sp):
                    r.wait("tensor", ev_evs[t])
                    r.op("tensor", lambda e, h=h, t=t, sp=sp, up=up: e.matmul(
                        ps_u[up][0:65, 0:512], va3[:, t, 65 * h:65 * h + 65],
                        ws3[:, sp, :], start=(t == 0), stop=(t == NT - 1)),
                        inc="tE")
                    sc_guard[sp] = r.marker("tE")
                    ws_guard[sp] = sc_guard[sp]

                emit_scores(0, 0)
                emit_scores(1, 1)
                for t in range(NT):
                    if t + 2 < NT:
                        emit_scores(t + 2, (t + 2) % 3)
                    emit_av(t, t % 3)
                u_done = r.marker("tE")
                r.wait("vector", u_done)
                r.wait("vector", rr_guard)
                r.op("vector", lambda e, up=up: e.tensor_scalar(
                    rr[64:65, 0:512], ps_u[up][64:65, 0:512], 1e-6, None, ALU.add))
                with_lp = self.nc.allow_low_precision
                def _recip(e):
                    with with_lp("bf16 normalizer is a matmul operand"):
                        return e.reciprocal(rrb[64:65, 0:512], rr[64:65, 0:512])
                ev_r = r.op("vector", _recip, inc="vE")
                u_guard[up] = r.op("vector", lambda e, up=up: e.tensor_copy(
                    ub[0:64, 0:512], ps_u[up][0:64, 0:512]), inc="vE")
                r.wait("tensor", ev_r)
                r.wait("tensor", bc_guard)
                ev_bc = r.op("tensor", lambda e: e.matmul(
                    ps_bc[0:64, 0:512], ones64v[64:65, 0:64],
                    rrb[64:65, 0:512], start=True, stop=True), inc="tE")
                rr_guard = ev_bc
                r.wait("vector", ev_bc)
                dsto = oT4[:, ob, h // 2, :][kp:kp + 64, :]
                bc_guard = r.op("vector", lambda e, dsto=dsto: e.tensor_tensor(
                    dsto, ub[0:64, 0:512], ps_bc[0:64, 0:512], ALU.mult), inc="vE")
            oT_ready = r.marker("vE")

            r.wait("tensor", wout_ev)
            r.wait("tensor", oT_ready)
            for t in range(4):
                stile = 4 * c + t
                r.wait("tensor", at_guard)
                for k in range(KT):
                    for nch in range(2):
                        r.op("tensor", lambda e, ob=ob, k=k, t=t, nch=nch: e.matmul(
                            ps_at[:, nch * 512:(nch + 1) * 512],
                            oT4[:, ob, k, t * 128:(t + 1) * 128],
                            wo3[:, k, nch * 512:(nch + 1) * 512],
                            start=(k == 0), stop=(k == KT - 1)),
                            inc="tE" if (k == KT - 1 and nch == 1) else None)
                mm = r.marker("tE")
                r.wait("vector", mm)
                r.wait("vector", xn_ev)
                r.wait("vector", yw_guard)
                at_guard = r.op("vector", lambda e: e.scalar_tensor_tensor(
                    yw[:, 0:D], ps_at[:, 0:1024], 1.0, boutv[:, 0:D],
                    ALU.mult, ALU.add), inc="vE")
                r.wait("gpsimd", at_guard)
                ev_y = r.op("gpsimd", lambda e, ob=ob, t=t: e.tensor_tensor(
                    yw[:, 0:D], yw[:, 0:D], xc4[:, ob, t, :], ALU.add),
                    inc="gE", inc_n=1)
                xnc_guard[ob] = ev_y
                r.wait("scalar", ev_y)
                r.wait("scalar", tw_guard)
                r.op("scalar", lambda e: e.activation(
                    tw[:, 0:D], yw[:, 0:D], AF.Square, accum_out=sta[:, 1:2]))
                ev = r.op("scalar", lambda e: e.activation(
                    tw[:, 0:D], yw[:, 0:D], AF.Copy, accum_out=sta[:, 0:1]),
                    inc="aE")
                r.wait("vector", ev)
                r.opc("vector", lambda e: e.tensor_scalar(
                    sta[:, 0:1], sta[:, 0:1], 1.0 / D, None, ALU.mult))
                r.opc("vector", lambda e: e.tensor_scalar(
                    sta[:, 2:3], sta[:, 1:2], 1.0 / D, None, ALU.mult))
                r.opc("vector", lambda e: e.tensor_tensor(
                    sta[:, 3:4], sta[:, 0:1], sta[:, 0:1], ALU.mult))
                r.opc("vector", lambda e: e.tensor_tensor(
                    sta[:, 2:3], sta[:, 2:3], sta[:, 3:4], ALU.subtract))
                ev = r.op("vector", lambda e: e.tensor_scalar(
                    sta[:, 2:3], sta[:, 2:3], LN_EPS, None, ALU.add), inc="vE")
                r.wait("scalar", ev)
                r.opc("scalar", lambda e: e.activation(sta[:, 2:3], sta[:, 2:3], AF.Ln))
                ev = r.op("scalar", lambda e: e.activation(
                    sta[:, 2:3], sta[:, 2:3], AF.Exp, scale=-0.5), inc="aE")
                r.wait("vector", ev)
                r.wait("vector", tw_guard)
                yw_guard = r.op("vector", lambda e: e.tensor_scalar(
                    tw[:, 0:D], yw[:, 0:D], sta[:, 0:1], sta[:, 2:3],
                    ALU.subtract, ALU.mult), inc="vE")
                r.wait("gpsimd", yw_guard)
                r.wait("gpsimd", sw_guard)
                tw_guard = r.op("gpsimd", lambda e: e.tensor_tensor(
                    sw[:, 0:D], tw[:, 0:D], gammav[:, 0:D], ALU.mult),
                    inc="gE", inc_n=1)
                ev = r.op("gpsimd", lambda e: e.tensor_tensor(
                    sw[:, 0:D], sw[:, 0:D], betav[:, 0:D], ALU.add),
                    inc="gE", inc_n=1)
                r.wait("sync", ev)
                sw_guard = r.dma("sync", self.scr["s"][stile * 128:(stile + 1) * 128, :],
                                 sw[:, 0:D])
            oT_guard[ob] = r.marker("tE")

        self.tap("vb", self.B["va"][:, 0:NT * HA * 65])
        if "sout" in self.debug:
            r.fence(engs=("scalar",))
            r.dma("scalar", self.taps["sout"], self.scr["s"])
        r.fence()
        for i in range(3):
            self.P.free(f"sc{i}")
        for i in range(2):
            self.P.free(f"u{i}")
        self.P.free("bc")
        self.P.free("at")
        self.free("qTa", "kTa", "va", "wob", "wst", "oT", "rr", "rrb", "ub",
                  "xnc", "yw", "tw", "sw", "sta")
        self.free("gamma_a", "beta_a", "bout_a")

    def p2(self, l):
        r, dr, scr = self.r, self.dr, self.scr
        r.fence()
        if l == 0:
            self.alloc("z", NT * D, F32)
        z = self.B["z"]
        zt = z.rearrange("p (t d) -> p t d", t=NT)
        if l == 0:
            for t in range(NT):
                r.dma("scalar", zt[:, t, :], scr["xn"][t * 128:(t + 1) * 128, :])
        else:
            for t in range(NT):
                r.dma("gpsimd", scr["tp"][t * 128:(t + 1) * 128, :], zt[:, t, :])
        zT = self.alloc("zT", KT * (S + 16), BF16)
        zT3 = zT.rearrange("p (k s) -> p k s", k=KT)
        zcol_ev = r.op("vector", lambda e: e.memset(zT3[:, :, 0:16], 0.0), inc="vE")
        r.wait("sync", r.marker("qG"))
        for k in range(KT):
            r.op("sync", lambda e, k=k: e.dma_start(
                zT3[:, k, 16:S + 16], scr["tp"][:, k * 128:(k + 1) * 128],
                transpose=True), inc="bS1", inc_n=16, is_dma=True)
        zT_ev = r.marker("bS1")
        z_ev = r.marker("qA")

        qs = self.alloc("qs8", HD, F32)
        ks = self.alloc("ks8", HD, F32)
        r.dma_b("sync", "bS0", qs[:, 0:HD], dr[f"qs_f{l}"].broadcast_to([128, HD]))
        r.dma_b("sync", "bS0", ks[:, 0:HD], dr[f"ks_f{l}"].broadcast_to([128, HD]))
        tabs = {}
        for tn in ("tqc", "tqs", "tkc", "tks"):
            tabs[tn] = self.alloc(tn, NT * HD, F32)
            r.dma_b("scalar", "bA0", tabs[tn][:, 0:NT * HD], dr[tn])
        tab3 = {tn: tabs[tn].rearrange("p (t d) -> p t d", t=NT) for tn in tabs}
        small_ev = r.marker("bS0")
        tab_ev = r.marker("bA0")
        r.wait("vector", small_ev)
        r.op("vector", lambda e: e.tensor_scalar(
            qs[:, 0:HD], qs[:, 0:HD], 8.0, None, ALU.mult), inc="vE")

        wb = {}
        for wn in ("wq", "wk", "wv"):
            buf = self.alloc(wn + "f", KT * 512, BF16)
            w3 = buf.rearrange("p (k c) -> p k c", k=KT)
            r.dma_b("gpsimd", "bG0", w3[:, :, :],
                    dr[f"{wn}_f{l}"].rearrange("(k p) c -> p k c", p=128))
            wb[wn] = w3
        wf_ev = r.marker("bG0")

        qtok = self.alloc("qtok", NT * INNER, BF16)
        ktok = self.alloc("ktok", NT * INNER, BF16)
        vf = self.alloc("vf", NT * INNER, BF16)
        q3 = qtok.rearrange("p (t c) -> p t c", t=NT)
        k3 = ktok.rearrange("p (t c) -> p t c", t=NT)
        v3 = vf.rearrange("p (t c) -> p t c", t=NT)

        qw = self.alloc("qw", 512, F32)
        qq = self.alloc("qq", 512, F32)
        q1 = self.alloc("q1", 512, F32)
        t1 = self.alloc("t1", 512, F32)
        t2 = self.alloc("t2", 512, F32)
        nrm = self.alloc("nrm", 16, F32)

        ps = [self.P.alloc(f"pf{i}", 1) for i in range(4)]
        pev = [None] * 4
        pi = [0]
        r.wait("tensor", zT_ev)
        r.wait("tensor", zcol_ev)
        r.wait("tensor", wf_ev)
        r.wait("vector", tab_ev)
        chain = None

        for t in range(NT):
            for wn, dtok in (("wq", q3), ("wk", k3), ("wv", v3)):
                p = pi[0] % 4
                pi[0] += 1
                r.wait("tensor", pev[p])
                for k in range(KT):
                    coff = 16 if k < 4 else 15
                    r.op("tensor", lambda e, p=p, wn=wn, k=k, t=t, coff=coff: e.matmul(
                        ps[p][:, 0:512],
                        zT3[:, k, coff + t * 128:coff + t * 128 + 128],
                        wb[wn][:, k, :], start=(k == 0), stop=(k == KT - 1)),
                        inc="tE" if k == KT - 1 else None)
                mm = r.marker("tE")
                if wn == "wv":
                    r.wait("vector", mm)
                    pev[p] = r.op("vector", lambda e, p=p, t=t: e.tensor_copy(
                        v3[:, t, :], ps[p][:, 0:512]), inc="vE")
                    continue
                is_q = wn == "wq"
                sc = qs if is_q else ks
                ctn, stn = ("tqc", "tqs") if is_q else ("tkc", "tks")
                r.wait("vector", mm)
                r.wait("vector", chain)
                pev[p] = r.op("vector", lambda e, p=p: e.tensor_copy(
                    qw[:, 0:512], ps[p][:, 0:512]), inc="vE")
                r.wait("scalar", pev[p])
                ev1 = r.op("scalar", lambda e: e.activation(
                    qq[:, 0:512], qw[:, 0:512], AF.Square), inc="aE")
                r.wait("vector", ev1)
                r.opc("vector", lambda e: e.reduce_sum(
                    nrm[:, 0:8], qq[:, 0:512].rearrange("p (h d) -> p h d", h=8),
                    axis=AX.X))
                ev2 = r.op("vector", lambda e: e.tensor_scalar(
                    nrm[:, 0:8], nrm[:, 0:8], 1e-24, None, ALU.max), inc="vE")
                r.wait("scalar", ev2)
                r.opc("scalar", lambda e: e.activation(nrm[:, 8:16], nrm[:, 0:8], AF.Ln))
                ev3 = r.op("scalar", lambda e: e.activation(
                    nrm[:, 8:16], nrm[:, 8:16], AF.Exp, scale=-0.5), inc="aE")
                r.wait("vector", ev3)
                r.op("vector", lambda e: e.tensor_tensor(
                    q1[:, 0:512].rearrange("p (h d) -> p h d", h=8),
                    qw[:, 0:512].rearrange("p (h d) -> p h d", h=8),
                    nrm[:, 8:16].rearrange("p (h o) -> p h o", o=1
                                           ).broadcast_to([128, 8, HD]),
                    ALU.mult))
                r.op("vector", lambda e, sc=sc: e.tensor_tensor(
                    qw[:, 0:512].rearrange("p (h d) -> p h d", h=8),
                    q1[:, 0:512].rearrange("p (h d) -> p h d", h=8),
                    sc[:, 0:HD].rearrange("p (o d) -> p o d", o=1
                                          ).broadcast_to([128, 8, HD]),
                    ALU.mult))
                r.op("vector", lambda e, ctn=ctn, t=t: e.tensor_tensor(
                    t1[:, 0:512].rearrange("p (h d) -> p h d", h=8),
                    qw[:, 0:512].rearrange("p (h d) -> p h d", h=8),
                    tab3[ctn][:, t, :].rearrange("p (o d) -> p o d", o=1
                                                 ).broadcast_to([128, 8, HD]),
                    ALU.mult))
                q14 = qw[:, 0:512].rearrange("p (h r d) -> p h r d", h=8, r=2)
                t24 = t2[:, 0:512].rearrange("p (h r d) -> p h r d", h=8, r=2)
                st4 = tab3[stn][:, t, :].rearrange("p (o r d) -> p o r d", o=1, r=2)
                r.op("vector", lambda e, q14=q14, t24=t24, st4=st4: e.tensor_tensor(
                    t24[:, :, 0:1, :], q14[:, :, 1:2, :],
                    st4[:, :, 0:1, :].broadcast_to([128, 8, 1, 32]), ALU.mult))
                r.op("vector", lambda e, q14=q14, t24=t24, st4=st4: e.tensor_tensor(
                    t24[:, :, 1:2, :], q14[:, :, 0:1, :],
                    st4[:, :, 1:2, :].broadcast_to([128, 8, 1, 32]), ALU.mult))
                chain = r.op("vector", lambda e, dtok=dtok, t=t: e.tensor_tensor(
                    dtok[:, t, :], t1[:, 0:512], t2[:, 0:512], ALU.add), inc="vE")

        for i in range(4):
            self.P.free(f"pf{i}")
        self.free("wqf", "wkf", "wvf", "qw", "qq", "q1", "t1", "t2",
                  "tqc", "tqs", "tkc", "tks", "qs8", "ks8")
        self.tap("qf", self.B["qtok"][:, 0:NT * INNER])
        self.tap("kf", self.B["ktok"][:, 0:NT * INNER])
        self.tap("vf", self.B["vf"][:, 0:NT * INNER])
        r.fence()
        self.free("zT")

        for t in range(NT):
            r.dma("scalar", scr["tq"][t * 128:(t + 1) * 128, :], q3[:, t, :])
            r.dma("scalar", scr["tk"][t * 128:(t + 1) * 128, :], k3[:, t, :])
        dmp = r.marker("qA")
        qfT = self.alloc("qfT", 4 * S, BF16)
        kfT = self.alloc("kfT", 4 * S, BF16)
        qfT3 = qfT.rearrange("p (k s) -> p k s", k=4)
        kfT3 = kfT.rearrange("p (k s) -> p k s", k=4)
        r.wait("sync", dmp)
        for k in range(4):
            r.op("sync", lambda e, k=k: e.dma_start(
                qfT3[:, k, :], scr["tq"][:, k * 128:(k + 1) * 128],
                transpose=True), inc="bS1", inc_n=16, is_dma=True)
            r.op("sync", lambda e, k=k: e.dma_start(
                kfT3[:, k, :], scr["tk"][:, k * 128:(k + 1) * 128],
                transpose=True), inc="bS1", inc_n=16, is_dma=True)
        qkT_ev = r.marker("bS1")
        r.fence(engs=("vector", "scalar", "gpsimd", "tensor"), sems=("qA",))
        self.free("qtok", "ktok")

        wob = self.alloc("wobf", 4 * D, BF16)
        wo3 = wob.rearrange("p (k n) -> p k n", k=4)
        r.dma_b("gpsimd", "bG1", wo3[:, :, :],
                dr[f"wo_f{l}"].rearrange("(k p) c -> p k c", p=128))
        wo_ev = r.marker("bG1")

        of = self.alloc("of", 4 * S, BF16)
        of3 = of.rearrange("p (k s) -> p k s", k=4)
        msc = self.alloc("msc", 2 * 640, F32)
        msc3 = msc.rearrange("p (b c) -> p b c", b=2)
        attn = self.alloc("attn", 2 * 640, BF16)
        attn3 = attn.rearrange("p (b c) -> p b c", b=2)
        atT = self.alloc("atT", 2 * 640, BF16)
        atT3 = atT.rearrange("p (b c) -> p b c", b=2)
        colsb = self.alloc("cols", 8, F32)
        cols = colsb.rearrange("p (b c) -> p b c", b=2)

        ps_sc = [self.P.alloc(f"fs{i}", 2) for i in range(2)]
        ps_tp = [self.P.alloc(f"ft{i}", 1, BF16) for i in range(2)]
        ps_o = [self.P.alloc(f"fo{i}", 1) for i in range(2)]

        pssc_guard = [None, None]
        msc_guard = [None, None]
        attn_guard = [None, None]
        pstp_guard = [None, None]
        atT_guard = [None, None]
        pso_guard = [None, None]

        r.wait("tensor", qkT_ev)
        mkv = self.B["mask"]
        identv = self.B["ident"]

        tiles = [(h, a) for h in range(HF) for a in range(NT)]
        nm_ev = [None] * len(tiles)
        tT_ev = [None] * len(tiles)

        def s_scores(i):
            h, a = tiles[i]
            kc, kp = h // 2, (h % 2) * 64
            wmax = min(a, 4)
            W = (wmax + 1) * 128
            k0 = a - wmax
            par = i % 2
            r.wait("tensor", pssc_guard[par])
            nleft, off = W, 0
            while nleft > 0:
                n = min(512, nleft)
                r.op("tensor", lambda e, kc=kc, kp=kp, a=a, k0=k0, off=off, n=n, par=par: e.matmul(
                    ps_sc[par][:, off:off + n],
                    qfT3[kp:kp + 64, kc, a * 128:(a + 1) * 128],
                    kfT3[kp:kp + 64, kc, k0 * 128 + off:k0 * 128 + off + n],
                    start=True, stop=True, tile_position=(kp, 0)),
                    inc="tE" if n == nleft else None)
                off += n
                nleft -= n
            mm = r.marker("tE")
            r.wait("vector", mm)
            r.wait("vector", msc_guard[par])
            pssc_guard[par] = r.op("vector", lambda e, par=par, W=W: e.scalar_tensor_tensor(
                msc3[:, par, 0:W], ps_sc[par][:, 0:W], 1.0,
                mkv[:, 640 - W:640], ALU.mult, ALU.add), inc="vE")
            r.opc("vector", lambda e, par=par, W=W: e.reduce_max(
                cols[:, par, 0:1], msc3[:, par, 0:W], axis=AX.X))
            ev_neg = r.op("vector", lambda e, par=par: e.tensor_scalar(
                cols[:, par, 1:2], cols[:, par, 0:1], -1.0, None, ALU.mult), inc="vE")
            r.wait("scalar", ev_neg)
            r.wait("scalar", attn_guard[par])
            ev_ex = r.op("scalar", lambda e, par=par, W=W: e.activation(
                attn3[:, par, 0:W], msc3[:, par, 0:W], AF.Exp,
                bias=cols[:, par, 1:2], accum_out=cols[:, par, 2:3]), inc="aE")
            msc_guard[par] = ev_ex
            r.wait("vector", ev_ex)
            r.opc("vector", lambda e, par=par: e.reciprocal(
                cols[:, par, 3:4], cols[:, par, 2:3]))
            nm_ev[i] = r.op("vector", lambda e, par=par, W=W: e.tensor_scalar(
                attn3[:, par, 0:W], attn3[:, par, 0:W], cols[:, par, 3:4], None,
                ALU.mult), inc="vE")

        def s_transp(i):
            h, a = tiles[i]
            wmax = min(a, 4)
            W = (wmax + 1) * 128
            par = i % 2
            r.wait("tensor", nm_ev[i])
            r.wait("tensor", pstp_guard[par])
            for j in range(wmax + 1):
                r.op("tensor", lambda e, par=par, j=j: e.transpose(
                    ps_tp[par][:, j * 128:(j + 1) * 128],
                    attn3[:, par, j * 128:(j + 1) * 128],
                    identv[:, 0:128]),
                    inc="tE" if j == wmax else None)
            attn_guard[par] = r.marker("tE")
            r.wait("vector", attn_guard[par])
            r.wait("vector", atT_guard[par])
            tT_ev[i] = r.op("vector", lambda e, par=par, W=W: e.tensor_copy(
                atT3[:, par, 0:W], ps_tp[par][:, 0:W]), inc="vE")
            pstp_guard[par] = tT_ev[i]

        def s_av(i):
            h, a = tiles[i]
            kc, kp = h // 2, (h % 2) * 64
            wmax = min(a, 4)
            k0 = a - wmax
            par = i % 2
            r.wait("tensor", pso_guard[par])
            r.wait("tensor", tT_ev[i])
            for j in range(wmax + 1):
                kt = k0 + j
                r.op("tensor", lambda e, h=h, kt=kt, j=j, par=par, wmax=wmax: e.matmul(
                    ps_o[par][0:64, 0:128], v3[:, kt, 64 * h:64 * h + 64],
                    atT3[:, par, j * 128:(j + 1) * 128],
                    start=(j == 0), stop=(j == wmax)),
                    inc="tE" if j == wmax else None)
            atT_guard[par] = r.marker("tE")
            r.wait("vector", atT_guard[par])
            pso_guard[par] = r.op("vector", lambda e, kp=kp, kc=kc, a=a, par=par: e.tensor_copy(
                of3[:, kc, a * 128:(a + 1) * 128][kp:kp + 64, :],
                ps_o[par][0:64, 0:128]), inc="vE")

        s_scores(0)
        for i in range(len(tiles)):
            if i + 1 < len(tiles):
                s_scores(i + 1)
            s_transp(i)
            s_av(i)

        of_ev = r.marker("vE")
        for i in range(2):
            self.P.free(f"fs{i}")
            self.P.free(f"ft{i}")
            self.P.free(f"fo{i}")
        if l == 0:
            self.tap("of", self.B["of"][:, 0:4 * S])
        self.free("msc", "attn", "atT", "cols", "qfT", "kfT", "vf")

        ps_d = [self.P.alloc(f"pd{i}", 2) for i in range(2)]
        dguard = [None, None]
        r.wait("tensor", of_ev)
        r.wait("tensor", wo_ev)
        r.wait("vector", z_ev)
        for t in range(NT):
            par = t % 2
            r.wait("tensor", dguard[par])
            for k in range(4):
                for nch in range(2):
                    r.op("tensor", lambda e, k=k, t=t, nch=nch, par=par: e.matmul(
                        ps_d[par][:, nch * 512:(nch + 1) * 512],
                        of3[:, k, t * 128:(t + 1) * 128],
                        wo3[:, k, nch * 512:(nch + 1) * 512],
                        start=(k == 0), stop=(k == 3)),
                        inc="tE" if (k == 3 and nch == 1) else None)
            mm = r.marker("tE")
            r.wait("vector", mm)
            dguard[par] = r.op("vector", lambda e, t=t, par=par: e.tensor_tensor(
                zt[:, t, :], zt[:, t, :], ps_d[par][:, 0:1024], ALU.add), inc="vE")
        for i in range(2):
            self.P.free(f"pd{i}")
        self.free("of", "wobf")
        if l == 0:
            self.tap("z1", z[:, 0:NT * D])
        r.fence()

        lg = self.alloc("lg", D, F32)
        lb = self.alloc("lb", D, F32)
        r.dma_b("sync", "bS0", lg[:, 0:D], dr[f"lg_f{l}"].broadcast_to([128, D]))
        r.dma_b("sync", "bS0", lb[:, 0:D], dr[f"lb_f{l}"].broadcast_to([128, D]))
        lgb_ev = r.marker("bS0")
        sh = self.alloc("sh", 2 * 512, F32)
        sh3 = sh.rearrange("p (b c) -> p b c", b=2)
        h2b = self.alloc("h2b", 2 * D, BF16)
        h2b3 = h2b.rearrange("p (b c) -> p b c", b=2)
        aw = self.alloc("aw", D, F32)
        stc = self.alloc("stc", 8, F32)
        sh_guard = [None, None]
        h2b_guard = [None, None]
        aw_guard = None
        r.wait("gpsimd", lgb_ev)
        for t in range(NT):
            par = t % 2
            r.wait("sync", sh_guard[par])
            bsh = "bS" + str(2 + par)
            if t == 0:
                ev0 = r.op("vector", lambda e, par=par: e.memset(
                    sh3[0:1, par, :], 0.0), inc="vE")
                r.wait("sync", ev0)
            else:
                r.dma_b("sync", bsh, sh3[0:1, par, :], zt[127:128, t - 1, 512:1024])
            r.dma_b("sync", bsh, sh3[1:128, par, :], zt[0:127, t, 512:1024])
            shd = r.marker(bsh)
            r.wait("scalar", shd)
            r.wait("scalar", aw_guard)
            r.op("scalar", lambda e, t=t: e.activation(
                aw[:, 0:512], zt[:, t, 0:512], AF.Copy, accum_out=stc[:, 0:1]))
            r.op("scalar", lambda e, par=par: e.activation(
                aw[:, 0:512], sh3[:, par, :], AF.Copy, accum_out=stc[:, 1:2]))
            r.op("scalar", lambda e, t=t: e.activation(
                aw[:, 0:512], zt[:, t, 0:512], AF.Square, accum_out=stc[:, 2:3]))
            ev = r.op("scalar", lambda e, par=par: e.activation(
                aw[:, 0:512], sh3[:, par, :], AF.Square, accum_out=stc[:, 3:4]),
                inc="aE")
            r.wait("vector", ev)
            r.opc("vector", lambda e: e.tensor_tensor(
                stc[:, 0:1], stc[:, 0:1], stc[:, 1:2], ALU.add))
            r.opc("vector", lambda e: e.tensor_tensor(
                stc[:, 2:3], stc[:, 2:3], stc[:, 3:4], ALU.add))
            r.opc("vector", lambda e: e.tensor_scalar(
                stc[:, 0:1], stc[:, 0:1], 1.0 / D, None, ALU.mult))
            r.opc("vector", lambda e: e.tensor_scalar(
                stc[:, 2:3], stc[:, 2:3], 1.0 / D, None, ALU.mult))
            r.opc("vector", lambda e: e.tensor_tensor(
                stc[:, 4:5], stc[:, 0:1], stc[:, 0:1], ALU.mult))
            r.opc("vector", lambda e: e.tensor_tensor(
                stc[:, 2:3], stc[:, 2:3], stc[:, 4:5], ALU.subtract))
            ev = r.op("vector", lambda e: e.tensor_scalar(
                stc[:, 2:3], stc[:, 2:3], LN_EPS, None, ALU.add), inc="vE")
            r.wait("scalar", ev)
            r.opc("scalar", lambda e: e.activation(stc[:, 2:3], stc[:, 2:3], AF.Ln))
            ev = r.op("scalar", lambda e: e.activation(
                stc[:, 2:3], stc[:, 2:3], AF.Exp, scale=-0.5), inc="aE")
            r.wait("vector", ev)
            r.wait("vector", h2b_guard[par])
            r.op("vector", lambda e, t=t: e.tensor_scalar(
                aw[:, 0:512], zt[:, t, 0:512], stc[:, 0:1], stc[:, 2:3],
                ALU.subtract, ALU.mult))
            ev = r.op("vector", lambda e, par=par: e.tensor_scalar(
                aw[:, 512:1024], sh3[:, par, :], stc[:, 0:1], stc[:, 2:3],
                ALU.subtract, ALU.mult), inc="vE")
            r.wait("gpsimd", ev)
            r.op("gpsimd", lambda e: e.tensor_tensor(
                aw[:, 0:D], aw[:, 0:D], lg[:, 0:D], ALU.mult), inc="gE", inc_n=1)
            ev = r.op("gpsimd", lambda e, par=par: e.tensor_tensor(
                h2b3[:, par, :], aw[:, 0:D], lb[:, 0:D], ALU.add),
                inc="gE", inc_n=1)
            aw_guard = ev
            sh_guard[par] = ev
            r.wait("scalar", ev)
            h2b_guard[par] = r.dma_b(
                "scalar", "bA" + str(par),
                scr["tp"][t * 128:(t + 1) * 128, :], h2b3[:, par, :])
        r.fence(engs=("sync",), sems=("qA", "bA0", "bA1"))
        dmp = r.marker("qA")
        h2T = self.alloc("h2T", KT * S, BF16)
        h2T3 = h2T.rearrange("p (k s) -> p k s", k=KT)
        r.wait("sync", dmp)
        for k in range(KT):
            r.op("sync", lambda e, k=k: e.dma_start(
                h2T3[:, k, :], scr["tp"][:, k * 128:(k + 1) * 128],
                transpose=True), inc="bS1", inc_n=16, is_dma=True)
        self.tap("h2T", self.B["h2T"][:, 0:KT * S])
        r.fence()
        self.free("lg", "lb", "sh", "h2b", "aw", "stc")

        b1a = self.alloc("b1a", HID_T, F32)
        b1g = self.alloc("b1g", HID_T, F32)
        b2 = self.alloc("b2", D, F32)
        r.dma_b("sync", "bS0", b1a[:, 0:HID_T], dr[f"b1a_f{l}"])
        r.dma_b("sync", "bS0", b1g[:, 0:HID_T], dr[f"b1g_f{l}"])
        r.dma_b("sync", "bS0", b2[:, 0:D], dr[f"b2_f{l}"].broadcast_to([128, D]))
        bia_ev = r.marker("bS0")
        w1b = self.alloc("w1b", 32 * 128, BF16)
        w1b4 = w1b.rearrange("p (g k c) -> p g k c", g=4, k=KT)
        w2b = self.alloc("w2b", 11 * 1024, BF16)
        w2b3 = w2b.rearrange("p (k n) -> p k n", k=11)
        h4 = self.alloc("h4", HID_T * 1024, BF16)
        h43 = h4.rearrange("p (k s) -> p k s", k=HID_T)
        gelb = self.alloc("gel", 2 * 1024, F32)
        gel3 = gelb.rearrange("p (b c) -> p b c", b=2)
        tmp2 = self.alloc("tmp2", D, F32)
        slot_ev = {}
        si = [0]
        r.wait("vector", bia_ev)
        r.wait("scalar", bia_ev)
        h2T_mark = r.marker("bS1")
        r.wait("tensor", h2T_mark)

        pa_guard = [None, None]
        pg_guard = [None, None]
        gel_guard = [None, None]
        tmp2_guard = None
        pw_guard = [None, None]
        h4_guard = {}
        w2slot_guard = {}

        for half in range(2):
            scol = half * 1024
            ps_a = [self.P.alloc(f"pa{i}", 2) for i in range(2)]
            ps_g = [self.P.alloc(f"pg{i}", 2) for i in range(2)]
            for kt in range(HID_T):
                par = kt % 2
                rows = 128 if kt < HID_T - 1 else HID - 128 * (HID_T - 1)
                for ag in range(2):
                    wsrc = f"w1a_f{l}" if ag == 0 else f"w1g_f{l}"
                    psAG = ps_a[par] if ag == 0 else ps_g[par]
                    sgrp = par * 2 + ag
                    bs = "bG" + str(sgrp)
                    r.wait("gpsimd", slot_ev.get(sgrp))
                    r.dma_b("gpsimd", bs,
                            w1b4[:, sgrp, :, 0:rows],
                            dr[wsrc][:, kt * 128:kt * 128 + rows].rearrange(
                                "(k p) c -> p k c", p=128))
                    slots = [(sgrp, k) for k in range(KT)]
                    batch_ev = r.marker(bs)
                    r.wait("tensor", pa_guard[par] if ag == 0 else pg_guard[par])
                    r.wait("tensor", batch_ev)
                    for k in range(KT):
                        for nch in range(2):
                            r.op("tensor", lambda e, sgrp=sgrp, k=k, nch=nch, psAG=psAG, scol=scol, rows=rows: e.matmul(
                                psAG[0:rows, nch * 512:(nch + 1) * 512],
                                w1b4[:, sgrp, k, 0:rows],
                                h2T3[:, k, scol + nch * 512:scol + (nch + 1) * 512],
                                start=(k == 0), stop=(k == KT - 1)),
                                inc="tE" if (k == KT - 1 and nch == 1) else None)
                    mm = r.marker("tE")
                    slot_ev[sgrp] = mm
                    if ag == 0:
                        pa_mm = mm
                    else:
                        pg_mm = mm
                r.wait("scalar", pg_mm)
                r.wait("scalar", gel_guard[par])
                pg_guard[par] = r.op("scalar", lambda e, kt=kt, rows=rows, par=par: e.activation(
                    gel3[0:rows, par, :], ps_g[par][0:rows, 0:1024], AF.Gelu,
                    bias=b1g[0:rows, kt:kt + 1]), inc="aE")
                r.wait("vector", pg_guard[par])
                r.wait("vector", pa_mm)
                r.wait("vector", h4_guard.get(kt))
                pa_guard[par] = r.op("vector", lambda e, kt=kt, rows=rows, par=par: e.scalar_tensor_tensor(
                    h43[0:rows, kt, 0:1024], ps_a[par][0:rows, 0:1024],
                    b1a[0:rows, kt:kt + 1], gel3[0:rows, par, :],
                    ALU.add, ALU.mult), inc="vE")
                gel_guard[par] = pa_guard[par]
                h4_guard[kt] = pa_guard[par]
            h4_done = r.marker("vE")
            r.wait("tensor", h4_done)
            for i in range(2):
                self.P.free(f"pa{i}")
                self.P.free(f"pg{i}")
            ps_w = [self.P.alloc(f"pw{i}", 2) for i in range(2)]
            pa_guard = [None, None]
            pg_guard = [None, None]

            for kh in range(2):
                kts = list(range(0, 11)) if kh == 0 else list(range(11, HID_T))
                bs = "bG" + str(4 + kh)
                r.wait("gpsimd", w2slot_guard.get(0))
                r.dma_b("gpsimd", bs, w2b3[:, :, :],
                        dr[f"w2_f{l}"][kh * 1408:kh * 1408 + 11 * 128, :].rearrange(
                            "(k p) c -> p k c", p=128))
                batch_ev = r.marker(bs)
                r.wait("tensor", h4_done)
                r.wait("tensor", batch_ev)
                for ti in range(8):
                    t = half * 8 + ti
                    par = ti % 2
                    r.wait("tensor", pw_guard[par])
                    for j, kt in enumerate(kts):
                        for nch in range(2):
                            r.op("tensor", lambda e, j=j, kt=kt, ti=ti, nch=nch, par=par: e.matmul(
                                ps_w[par][:, nch * 512:(nch + 1) * 512],
                                h43[:, kt, ti * 128:(ti + 1) * 128],
                                w2b3[:, j, nch * 512:(nch + 1) * 512],
                                start=(j == 0), stop=(j == len(kts) - 1)),
                                inc="tE" if (j == len(kts) - 1 and nch == 1) else None)
                    mm = r.marker("tE")
                    r.wait("vector", mm)
                    if kh == 0:
                        pw_guard[par] = r.op("vector", lambda e, t=t, par=par: e.tensor_tensor(
                            zt[:, t, :], zt[:, t, :], ps_w[par][:, 0:1024], ALU.add),
                            inc="vE")
                    else:
                        r.wait("vector", tmp2_guard)
                        ev = r.op("vector", lambda e, par=par: e.scalar_tensor_tensor(
                            tmp2[:, 0:D], ps_w[par][:, 0:1024], 1.0, b2[:, 0:D],
                            ALU.mult, ALU.add), inc="vE")
                        r.wait("gpsimd", ev)
                        pw_guard[par] = r.op("gpsimd", lambda e, t=t: e.tensor_tensor(
                            zt[:, t, :], zt[:, t, :], tmp2[:, 0:D], ALU.add),
                            inc="gE", inc_n=1)
                        tmp2_guard = pw_guard[par]
                wmark = r.marker("tE")
                w2slot_guard[0] = wmark
            hm = r.marker("tE")
            for kt in range(HID_T):
                h4_guard[kt] = hm
            for i in range(2):
                self.P.free(f"pw{i}")
            pw_guard = [None, None]
            r.fence(engs=("tensor",), sems=("vE", "gE"))
        r.fence()
        self.free("b1a", "b1g", "b2", "w1b", "w2b", "h4", "gel", "tmp2", "h2T")
        self.tap("z2" if l == 0 else "zf", z[:, 0:NT * D])

    def p3(self):
        r, dr, scr = self.r, self.dr, self.scr
        r.fence()
        zt = self.B["z"].rearrange("p (t d) -> p t d", t=NT)
        xb = self.alloc("xb3", 2 * D, F32)
        xb3 = xb.rearrange("p (b c) -> p b c", b=2)
        sb = self.alloc("sb3", 2 * D, F32)
        sb3 = sb.rearrange("p (b c) -> p b c", b=2)
        gw = self.alloc("gw3", D, F32)
        ow = self.alloc("ow3", D, F32)
        buf_guard = [None, None]
        ow_guard = None
        gw_guard = None
        for t in range(NT):
            par = t % 2
            r.wait("scalar", buf_guard[par])
            e1 = r.dma_b("scalar", "bA" + str(par), xb3[:, par, :],
                         dr["x"][t * 128:(t + 1) * 128, :])
            e2 = r.dma_b("scalar", "bA" + str(par), sb3[:, par, :],
                         scr["s"][t * 128:(t + 1) * 128, :])
            r.wait("scalar", gw_guard)
            ev_g = r.op("scalar", lambda e, t=t: e.activation(
                gw[:, 0:D], zt[:, t, :], AF.Silu), inc="aE")
            r.wait("vector", ev_g)
            r.wait("vector", e2)
            r.wait("vector", ow_guard)
            gw_guard = r.op("vector", lambda e, par=par: e.tensor_tensor(
                ow[:, 0:D], gw[:, 0:D], sb3[:, par, :], ALU.mult), inc="vE")
            ev_o = r.op("vector", lambda e, par=par: e.tensor_tensor(
                ow[:, 0:D], ow[:, 0:D], xb3[:, par, :], ALU.add), inc="vE")
            r.wait("sync", ev_o)
            st_ev = r.dma("sync", dr["out"][t * 128:(t + 1) * 128, :], ow[:, 0:D])
            ow_guard = st_ev
            buf_guard[par] = st_ev

    def fin(self):
        self.r.fence(engs=("sync",))



def kernel(**inputs):
    g = host_layout(inputs)
    consts = host_constants()
    nc = build()
    x = np.asarray(inputs["L0_attn_x"], dtype=np.float32)
    in_maps = []
    for core in range(8):
        m = {"x": np.ascontiguousarray(x[core])}
        m.update(g)
        m.update(consts)
        in_maps.append(m)
    res = run_bass_kernel_spmd(nc, in_maps, core_ids=list(range(8)))
    return np.stack([r["out"] for r in res.results], axis=0)


# version 07 (final)
# speedup vs baseline: 1.3745x; 1.0734x over previous
import numpy as np
import ml_dtypes
import concourse.bass as bass
import concourse.mybir as mybir
from concourse.bass_utils import run_bass_kernel_spmd

F32 = mybir.dt.float32
BF16 = mybir.dt.bfloat16
AF = mybir.ActivationFunctionType
ALU = mybir.AluOpType
AX = mybir.AxisListType

S = 2048
NT = 16
D = 1024
KT = 8
HA = 16
HF = 8
HD = 64
INNER = 512
HID = 2730
HID_T = 22
WIN = 512
NEG = -3.0e38
LN_EPS = 1e-5

ENGINES = ("tensor", "vector", "scalar", "sync", "gpsimd")


class Rec:
    def __init__(self, nc):
        self.nc = nc
        self.ops = {e: [] for e in ENGINES}
        self.sems = {}
        self.cnt = {}
        self.last_wait = {e: {} for e in ENGINES}

    def sem(self, name):
        self.sems[name] = self.nc.semaphore(name).__enter__()
        self.cnt[name] = 0

    SELF_SEM = {"vector": "vE", "scalar": "aE", "gpsimd": "gE"}

    def op(self, eng, fn, inc=None, inc_n=1, is_dma=False):
        own = None if is_dma else self.SELF_SEM.get(eng)
        if own is not None:
            self.wait(eng, (own, self.cnt[own]))
            if inc is None:
                inc = own
        if inc is None:
            self.ops[eng].append(fn)
            return None
        self.cnt[inc] += inc_n
        tgt = self.cnt[inc]
        sem = self.sems[inc]
        if own is not None and inc != own:
            self.cnt[own] += 1
            tgt2 = self.cnt[own]
            sem2 = self.sems[own]
            self.ops[eng].append(
                lambda e, f=fn, s=sem, n=inc_n, s2=sem2: f(e).then_inc(s, n).then_inc(s2, 1))
            return (inc, tgt)
        self.ops[eng].append(lambda e, f=fn, s=sem, n=inc_n: f(e).then_inc(s, n))
        return (inc, tgt)

    def dma_b(self, q, bsem, dst, src, **kw):
        return self.op(q, lambda e: e.dma_start(dst, src, **kw), inc=bsem,
                       inc_n=16, is_dma=True)

    def dma(self, q, dst, src, **kw):
        inc = {"sync": "qS", "scalar": "qA", "gpsimd": "qG"}[q]
        return self.op(q, lambda e: e.dma_start(dst, src, **kw), inc=inc,
                       inc_n=16, is_dma=True)

    def dmat(self, q, dst, src):
        inc = {"sync": "qS", "scalar": "qA"}[q]
        return self.op(q, lambda e: e.dma_start(dst, src, transpose=True),
                       inc=inc, inc_n=16, is_dma=True)

    def wait(self, eng, ev):
        if ev is None:
            return
        sname, val = ev
        if sname in ("qS", "qA", "qG"):
            val = self.cnt[sname]
        lw = self.last_wait[eng]
        if lw.get(sname, -1) >= val:
            return
        lw[sname] = val
        sem = self.sems[sname]
        self.ops[eng].append(lambda e, s=sem, v=val: e.wait_ge(s, v))

    def marker(self, sname):
        return (sname, self.cnt[sname])

    def opc(self, eng, fn):
        inc = {"vector": "vE", "scalar": "aE", "gpsimd": "qG"}[eng]
        ev = self.op(eng, fn, inc=inc)
        self.wait(eng, ev)
        return ev

    def fence(self, engs=ENGINES, sems=("qS", "qA", "qG", "tE", "vE", "aE", "gE",
                                        "bG0", "bG1", "bG2", "bG3", "bG4", "bG5",
                                        "bS0", "bS1", "bS2", "bS3", "bA0", "bA1")):
        marks = [self.marker(s) for s in sems if self.cnt[s] > 0]
        for e in engs:
            for m in marks:
                self.wait(e, m)

    def replay(self, block):
        for eng in ENGINES:
            ops = self.ops[eng]

            def body(e, ops=ops):
                for f in ops:
                    f(e)

            getattr(block, eng)(body)


class Arena:
    def __init__(self, nc, elems_f32):
        self.t = nc.alloc_sbuf_tensor("arena", [128, elems_f32], F32)
        self.free_list = [(0, elems_f32)]
        self.allocs = {}

    def alloc(self, name, elems, dtype):
        f32e = elems if dtype == F32 else (elems + 1) // 2
        align = 16
        for i, (off, size) in enumerate(self.free_list):
            aoff = (off + align - 1) // align * align
            pad = aoff - off
            if size >= f32e + pad:
                rest = size - f32e - pad
                frags = []
                if pad:
                    frags.append((off, pad))
                if rest:
                    frags.append((aoff + f32e, rest))
                self.free_list[i:i + 1] = frags
                self.allocs[name] = (aoff, f32e)
                ap = self.t[:, aoff:aoff + f32e]
                if dtype == BF16:
                    ap = ap.bitcast(BF16)[:, 0:elems]
                return ap
        raise MemoryError(f"arena: cannot alloc {name} ({f32e}); free={self.free_list}")

    def free(self, name):
        off, elems = self.allocs.pop(name)
        self.free_list.append((off, elems))
        self.free_list.sort()
        merged = []
        for o, sz in self.free_list:
            if merged and merged[-1][0] + merged[-1][1] == o:
                merged[-1] = (merged[-1][0], merged[-1][1] + sz)
            else:
                merged.append([o, sz])
        self.free_list = [tuple(x) for x in merged]


class PsumMgr:
    def __init__(self, nc):
        self.t = nc.alloc_psum_tensor("psum", [128, 4096], F32)
        self.free_banks = set(range(8))
        self.allocs = {}

    def alloc(self, name, banks, dtype=F32):
        for b0 in range(8 - banks + 1):
            if all(b0 + j in self.free_banks for j in range(banks)):
                for j in range(banks):
                    self.free_banks.discard(b0 + j)
                self.allocs[name] = (b0, banks)
                ap = self.t[:, b0 * 512:(b0 + banks) * 512]
                return ap.bitcast(BF16) if dtype == BF16 else ap
        raise MemoryError(f"psum {name}: need {banks}, free={sorted(self.free_banks)}")

    def free(self, name):
        b0, banks = self.allocs.pop(name)
        for j in range(banks):
            self.free_banks.add(b0 + j)



def host_constants():
    c = {}
    c["ident"] = np.eye(128, dtype=ml_dtypes.bfloat16)
    c["onesrow"] = np.ones((1, 64), dtype=ml_dtypes.bfloat16)
    r = np.arange(128)[:, None]
    cc = np.arange(640)[None, :]
    c["mask"] = np.where((cc >= r) & (cc <= r + 512), 0.0, NEG).astype(np.float32)
    half = np.arange(0, HD, 2, dtype=np.float64)
    inv_freq = 1.0 / 10000.0 ** (half / HD)
    t = np.arange(S, dtype=np.float64)
    freqs = t[:, None] * inv_freq[None, :]
    freqs = np.concatenate([freqs, freqs], axis=-1)
    base = (half + 0.4 * HD) / (1.4 * HD)
    power = (t - S // 2) / float(WIN // 2)
    scale = base[None, :] ** power[:, None]
    scale = np.concatenate([scale, scale], axis=-1)
    cos, sin = np.cos(freqs), np.sin(freqs)
    sign = np.concatenate([-np.ones(32), np.ones(32)])[None, :]

    def tok(x):
        return np.ascontiguousarray(
            x.reshape(NT, 128, HD).transpose(1, 0, 2).reshape(128, NT * HD)
        ).astype(np.float32)

    c["tqc"] = tok(cos * scale)
    c["tqs"] = tok(sin * scale * sign)
    c["tkc"] = tok(cos / scale)
    c["tks"] = tok(sin / scale * sign)
    return c


def host_layout(inputs):
    g = {}
    w3 = np.asarray(inputs["L0_attn_w_qkv"]).reshape(D, HA, 3, HD)
    g["wq_a"] = np.ascontiguousarray(w3[:, :, 0, :].reshape(D, D))
    g["wk_a"] = np.ascontiguousarray(w3[:, :, 1, :].reshape(D, D))
    g["wv_a"] = np.ascontiguousarray(w3[:, :, 2, :].reshape(D, D))
    g["wout_a"] = np.ascontiguousarray(np.asarray(inputs["L0_attn_w_out"]))
    g["bout_a"] = np.ascontiguousarray(np.asarray(inputs["L0_attn_b_out"]).reshape(1, D))
    g["gamma_a"] = np.ascontiguousarray(np.asarray(inputs["L0_attn_gamma"]).reshape(1, D))
    g["beta_a"] = np.ascontiguousarray(np.asarray(inputs["L0_attn_beta"]).reshape(1, D))
    for l in range(2):
        wf = np.asarray(inputs["L0_ffn_wqkv"])[l]
        g[f"wq_f{l}"] = np.ascontiguousarray(wf[:, 0:INNER])
        g[f"wk_f{l}"] = np.ascontiguousarray(wf[:, INNER:2 * INNER])
        g[f"wv_f{l}"] = np.ascontiguousarray(wf[:, 2 * INNER:3 * INNER])
        g[f"qs_f{l}"] = np.ascontiguousarray(
            np.asarray(inputs["L0_ffn_q_scale"])[l].reshape(1, HD))
        g[f"ks_f{l}"] = np.ascontiguousarray(
            np.asarray(inputs["L0_ffn_k_scale"])[l].reshape(1, HD))
        g[f"wo_f{l}"] = np.ascontiguousarray(np.asarray(inputs["L0_ffn_wo"])[l])
        g[f"lg_f{l}"] = np.ascontiguousarray(
            np.asarray(inputs["L0_ffn_ln_g"])[l].reshape(1, D))
        g[f"lb_f{l}"] = np.ascontiguousarray(
            np.asarray(inputs["L0_ffn_ln_b"])[l].reshape(1, D))
        w1 = np.asarray(inputs["L0_ffn_w1"])[l]
        g[f"w1a_f{l}"] = np.ascontiguousarray(w1[:, 0:HID])
        g[f"w1g_f{l}"] = np.ascontiguousarray(w1[:, HID:2 * HID])
        b1 = np.asarray(inputs["L0_ffn_b1"])[l]
        b1a = np.zeros((HID_T, 128), np.float32)
        b1g = np.zeros((HID_T, 128), np.float32)
        b1a.reshape(-1)[:HID] = b1[0:HID]
        b1g.reshape(-1)[:HID] = b1[HID:2 * HID]
        g[f"b1a_f{l}"] = np.ascontiguousarray(b1a.T)
        g[f"b1g_f{l}"] = np.ascontiguousarray(b1g.T)
        w2p = np.zeros((HID_T * 128, D), np.float32)
        w2p[:HID] = np.asarray(inputs["L0_ffn_w2"])[l]
        g[f"w2_f{l}"] = w2p
        g[f"b2_f{l}"] = np.ascontiguousarray(
            np.asarray(inputs["L0_ffn_b2"])[l].reshape(1, D))
    return g



def build(debug_taps=(), phases=("p0", "p1", "p2a", "p2b", "p3")):
    nc = bass.Bass("TRN2", target_bir_lowering=False)
    dr = {}
    dr["x"] = nc.dram_tensor("x", [S, D], F32, kind="ExternalInput").ap()
    f32_ins = {
        "wq_a": [D, D], "wk_a": [D, D], "wv_a": [D, D], "wout_a": [D, D],
        "bout_a": [1, D], "gamma_a": [1, D], "beta_a": [1, D],
        "mask": [128, 640],
        "tqc": [128, NT * HD], "tqs": [128, NT * HD],
        "tkc": [128, NT * HD], "tks": [128, NT * HD],
    }
    for l in range(2):
        f32_ins.update({
            f"wq_f{l}": [D, INNER], f"wk_f{l}": [D, INNER], f"wv_f{l}": [D, INNER],
            f"qs_f{l}": [1, HD], f"ks_f{l}": [1, HD],
            f"wo_f{l}": [INNER, D], f"lg_f{l}": [1, D], f"lb_f{l}": [1, D],
            f"w1a_f{l}": [D, HID], f"w1g_f{l}": [D, HID],
            f"b1a_f{l}": [128, HID_T], f"b1g_f{l}": [128, HID_T],
            f"w2_f{l}": [HID_T * 128, D], f"b2_f{l}": [1, D],
        })
    for n, shp in f32_ins.items():
        dr[n] = nc.dram_tensor(n, shp, F32, kind="ExternalInput").ap()
    dr["ident"] = nc.dram_tensor("ident", [128, 128], BF16, kind="ExternalInput").ap()
    dr["onesrow"] = nc.dram_tensor("onesrow", [1, 64], BF16, kind="ExternalInput").ap()
    dr["out"] = nc.dram_tensor("out", [S, D], F32, kind="ExternalOutput").ap()

    scr = {
        "tp": nc.dram_tensor("scr_tp", [S, D], BF16, kind="Internal").ap(),
        "tq": nc.dram_tensor("scr_tq", [S, INNER], BF16, kind="Internal").ap(),
        "tk": nc.dram_tensor("scr_tk", [S, INNER], BF16, kind="Internal").ap(),
        "xn": nc.dram_tensor("scr_xn", [S, D], F32, kind="Internal").ap(),
        "s": nc.dram_tensor("scr_s", [S, D], F32, kind="Internal").ap(),
    }
    tap_shapes = {
        "xn": ([S, D], F32), "sout": ([S, D], F32),
        "qTa": ([128, KT * S], BF16), "kTa": ([128, KT * S], BF16),
        "va": ([128, NT * HA * 65], BF16),
        "qf": ([128, NT * INNER], BF16), "kf": ([128, NT * INNER], BF16),
        "vf": ([128, NT * INNER], BF16), "of": ([128, 4 * S], BF16),
        "z1": ([S, D], F32), "z2": ([S, D], F32), "zf": ([S, D], F32),
        "xnT": ([128, KT * (S + 16)], BF16),
        "vb": ([128, NT * HA * 65], BF16),
        "h2T": ([128, KT * S], BF16),
    }
    taps = {}
    for n in debug_taps:
        shp, dt = tap_shapes[n]
        taps[n] = nc.dram_tensor("tap_" + n, shp, dt, kind="ExternalOutput").ap()

    r = Rec(nc)
    for s_ in ("qS", "qA", "qG", "tE", "vE", "aE", "gE",
               "bG0", "bG1", "bG2", "bG3", "bG4", "bG5", "bS0", "bS1", "bS2", "bS3", "bA0", "bA1"):
        r.sem(s_)
    avail = nc.sbuf_bytes_remaining // 4 - 16
    ar = Arena(nc, avail)
    P = PsumMgr(nc)
    b = _B(nc, r, ar, P, dr, scr, taps, set(debug_taps))
    b.prelude()
    if "p0" in phases:
        b.p0()
    if "p1" in phases:
        b.p1()
    if "p2a" in phases:
        b.p2(0)
    if "p2b" in phases:
        b.p2(1)
    if "p3" in phases:
        b.p3()
    b.fin()
    blk = nc.Block().__enter__()
    r.replay(blk)
    blk.__exit__(None, None, None)
    return nc


class _B:
    def __init__(self, nc, r, ar, P, dr, scr, taps, debug):
        self.nc, self.r, self.ar, self.P = nc, r, ar, P
        self.dr, self.scr, self.taps, self.debug = dr, scr, taps, debug
        self.B = {}

    def alloc(self, name, elems, dtype):
        self.B[name] = self.ar.alloc(name, elems, dtype)
        return self.B[name]

    def free(self, *names):
        for n in names:
            self.ar.free(n)
            del self.B[n]

    def tap(self, name, src_ap):
        if name in self.debug:
            self.r.fence(engs=("scalar",))
            self.r.dma("scalar", self.taps[name], src_ap)

    def prelude(self):
        r, dr = self.r, self.dr
        idb = self.alloc("ident", 128, BF16)
        ob = self.alloc("ones64", 64, BF16)
        mk = self.alloc("mask", 640, F32)
        r.dma("sync", idb[:, 0:128], dr["ident"])
        r.dma("sync", ob[64:65, 0:64], dr["onesrow"])
        r.dma("sync", mk[:, 0:640], dr["mask"])
        for n in ("gamma_a", "beta_a", "bout_a"):
            buf = self.alloc(n, D, F32)
            r.dma("sync", buf[:, 0:D], dr[n].broadcast_to([128, D]))

    def p0(self):
        r, dr, scr = self.r, self.dr, self.scr
        x = self.alloc("xw", NT * D, F32)
        xt = x.rearrange("p (t d) -> p t d", t=NT)
        st = self.alloc("st0", 4 * NT, F32)
        st3 = st.rearrange("p (a t) -> p a t", a=4)
        sq = self.alloc("sq0", D, F32)
        for t in range(NT):
            r.dma_b("scalar", "bA0", xt[:, t, :], dr["x"][t * 128:(t + 1) * 128, :])
        r.wait("scalar", r.marker("bA0"))
        for t in range(NT):
            r.op("scalar", lambda e, t=t: e.activation(
                sq[:, 0:D], xt[:, t, :], AF.Square, accum_out=st3[:, 1, t:t + 1]))
            r.op("scalar", lambda e, t=t: e.activation(
                sq[:, 0:D], xt[:, t, :], AF.Copy, accum_out=st3[:, 0, t:t + 1]),
                inc="aE")
        r.wait("vector", r.marker("aE"))
        r.opc("vector", lambda e: e.tensor_scalar(
            st3[:, 0, :], st3[:, 0, :], 1.0 / D, None, ALU.mult))
        r.opc("vector", lambda e: e.tensor_scalar(
            st3[:, 1, :], st3[:, 1, :], 1.0 / D, None, ALU.mult))
        r.opc("vector", lambda e: e.tensor_tensor(
            st3[:, 2, :], st3[:, 0, :], st3[:, 0, :], ALU.mult))
        r.opc("vector", lambda e: e.tensor_tensor(
            st3[:, 1, :], st3[:, 1, :], st3[:, 2, :], ALU.subtract))
        ev = r.op("vector", lambda e: e.tensor_scalar(
            st3[:, 1, :], st3[:, 1, :], LN_EPS, None, ALU.add), inc="vE")
        r.wait("scalar", ev)
        r.opc("scalar", lambda e: e.activation(st3[:, 2, :], st3[:, 1, :], AF.Ln))
        ev = r.op("scalar", lambda e: e.activation(
            st3[:, 2, :], st3[:, 2, :], AF.Exp, scale=-0.5), inc="aE")
        r.wait("vector", ev)
        for t in range(NT):
            ev = r.op("vector", lambda e, t=t: e.tensor_scalar(
                xt[:, t, :], xt[:, t, :], st3[:, 0, t:t + 1], st3[:, 2, t:t + 1],
                ALU.subtract, ALU.mult), inc="vE")
            r.wait("scalar", ev)
            r.wait("gpsimd", ev)
            r.dma("scalar", scr["xn"][t * 128:(t + 1) * 128, :], xt[:, t, :])
            r.dma("gpsimd", scr["tp"][t * 128:(t + 1) * 128, :], xt[:, t, :])
        self.tap("xn", x[:, 0:NT * D])
        dump = r.marker("qG")
        xnT = self.alloc("xnT", KT * (S + 16), BF16)
        xT3 = xnT.rearrange("p (k s) -> p k s", k=KT)
        r.op("vector", lambda e: e.memset(xT3[:, :, 0:16], 0.0), inc="vE")
        r.wait("sync", dump)
        for k in range(KT):
            r.dmat("sync", xT3[:, k, 16:S + 16], scr["tp"][:, k * 128:(k + 1) * 128])
        self.tap("xnT", xnT[:, 0:KT * (S + 16)])
        self.free("xw", "st0", "sq0")

    def p1(self):
        r, dr = self.r, self.dr
        r.fence()
        xT3 = self.B["xnT"].rearrange("p (k s) -> p k s", k=KT)
        qT = self.alloc("qTa", KT * S, BF16)
        kT = self.alloc("kTa", KT * S, BF16)
        va = self.alloc("va", NT * HA * 65, BF16)
        qT3 = qT.rearrange("p (k s) -> p k s", k=KT)
        kT3 = kT.rearrange("p (k s) -> p k s", k=KT)
        va3 = va.rearrange("p (t f) -> p t f", t=NT)
        va4 = va.rearrange("p (t h c) -> p t h c", t=NT, h=HA)
        r.op("vector", lambda e: e.memset(va4[:, :, :, 64:65], 1.0), inc="vE")

        wqk = self.alloc("wqk", 16 * 128, BF16)
        wqk3 = wqk.rearrange("p (b c) -> p b c", b=16)
        wv = self.alloc("wv", 16 * 512, BF16)
        wv3 = wv.rearrange("p (b c) -> p b c", b=16)
        slot_ev = {}
        ps = [self.P.alloc(f"pq{i}", 1) for i in range(4)]
        pev = [None] * 4
        pi = [0]
        r.wait("tensor", r.marker("qS"))

        def qk_part(wname, dstT, scale):
            for m in range(8):
                slots = []
                bs = "bG" + str(m % 2)
                mpar = m % 2
                r.wait("gpsimd", slot_ev.get(("qk", mpar)))
                r.dma_b("gpsimd", bs,
                        wqk3[:, mpar * 8:(mpar + 1) * 8, :].rearrange(
                            "p a c -> p a c"),
                        dr[wname][:, m * 128:(m + 1) * 128].rearrange(
                            "(k p) c -> p k c", p=128))
                for k in range(KT):
                    slots.append((mpar * 8 + k, None))
                batch_ev = r.marker(bs)
                for c in range(4):
                    p = pi[0] % 4
                    pi[0] += 1
                    r.wait("tensor", pev[p])
                    r.wait("tensor", batch_ev)
                    for k in range(KT):
                        i, ev = slots[k]
                        r.op("tensor", lambda e, p=p, i=i, k=k, c=c: e.matmul(
                            ps[p][:, 0:512], wqk3[:, i, :],
                            xT3[:, k, 16 + c * 512:16 + (c + 1) * 512],
                            start=(k == 0), stop=(k == KT - 1)),
                            inc="tE" if k == KT - 1 else None)
                    mm = r.marker("tE")
                    r.wait("vector", mm)
                    if scale == 1.0:
                        pev[p] = r.op("vector", lambda e, p=p, m=m, c=c: e.tensor_copy(
                            dstT[:, m, c * 512:(c + 1) * 512], ps[p][:, 0:512]),
                            inc="vE")
                    else:
                        pev[p] = r.op("vector", lambda e, p=p, m=m, c=c: e.tensor_scalar(
                            dstT[:, m, c * 512:(c + 1) * 512], ps[p][:, 0:512],
                            scale, None, ALU.mult), inc="vE")
                slot_ev[("qk", m % 2)] = r.marker("tE")

        qk_part("wq_a", qT3, 0.125)
        qk_part("wk_a", kT3, 1.0)

        for nch in range(2):
            slots = [(nch * 8 + k, None) for k in range(KT)]
            bs = "bG" + str(2 + nch)
            r.dma_b("gpsimd", bs, wv3[:, nch * 8:(nch + 1) * 8, :],
                    dr["wv_a"][:, nch * 512:(nch + 1) * 512].rearrange(
                        "(k p) c -> p k c", p=128))
            batch_ev = r.marker(bs)
            for t in range(NT):
                p = pi[0] % 4
                pi[0] += 1
                r.wait("tensor", pev[p])
                r.wait("tensor", batch_ev)
                for k in range(KT):
                    i, ev = slots[k]
                    r.op("tensor", lambda e, p=p, i=i, k=k, t=t: e.matmul(
                        ps[p][:, 0:512], xT3[:, k, 16 + t * 128:16 + (t + 1) * 128],
                        wv3[:, i, :], start=(k == 0), stop=(k == KT - 1)),
                        inc="tE" if k == KT - 1 else None)
                mm = r.marker("tE")
                r.wait("vector", mm)
                dst = va4[:, t, nch * 8:(nch + 1) * 8, 0:64]
                srcv = ps[p][:, 0:512].rearrange("p (h c) -> p h c", h=8)
                pev[p] = r.op("vector", lambda e, dst=dst, srcv=srcv: e.tensor_copy(
                    dst, srcv), inc="vE")
            for (i, _e) in slots:
                slot_ev[("wv", i)] = r.marker("tE")

        for i in range(4):
            self.P.free(f"pq{i}")
        self.free("wqk", "wv")
        self.tap("qTa", self.B["qTa"][:, 0:KT * S])
        self.tap("kTa", self.B["kTa"][:, 0:KT * S])
        self.tap("va", self.B["va"][:, 0:NT * HA * 65])
        r.fence()
        self.free("xnT")

        wob = self.alloc("wob", KT * D, BF16)
        wo3 = wob.rearrange("p (k n) -> p k n", k=KT)
        r.dma_b("gpsimd", "bG0", wo3[:, :, :],
                dr["wout_a"].rearrange("(k p) c -> p k c", p=128))
        wout_ev = r.marker("bG0")

        wst = self.alloc("wst", 4 * 512, BF16)
        ws3 = wst.rearrange("p (b c) -> p b c", b=4)
        oT = self.alloc("oT", 2 * 8 * 512, BF16)
        oT4 = oT.rearrange("p (b k c) -> p b k c", b=2, k=8)
        rr = self.alloc("rr", 512, BF16)
        rrb = self.alloc("rrb", 512, BF16)
        ub = self.alloc("ub", 512, BF16)
        xnc = self.alloc("xnc", 2 * 4 * D, F32)
        xc4 = xnc.rearrange("p (b t d) -> p b t d", b=2, t=4)
        yw = self.alloc("yw", D, F32)
        tw = self.alloc("tw", D, F32)
        sw = self.alloc("sw", D, F32)
        sta = self.alloc("sta", 8, F32)

        ps_sc = [self.P.alloc(f"sc{i}", 1) for i in range(4)]
        ps_u = [self.P.alloc(f"u{i}", 1) for i in range(2)]
        ps_bc = self.P.alloc("bc", 1)
        ps_at = self.P.alloc("at", 1)

        sc_guard = [None] * 4
        ws_guard = [None] * 4
        u_guard = [None, None]
        oT_guard = [None, None]
        bc_guard = None
        rr_guard = None
        at_guard = None
        yw_guard = None
        tw_guard = None
        sw_guard = None

        ones64v = self.B["ones64"]
        boutv = self.B["bout_a"]
        gammav = self.B["gamma_a"]
        betav = self.B["beta_a"]
        xnc_guard = [None, None]
        for c in range(4):
            ob = c % 2
            r.wait("vector", oT_guard[ob])
            r.wait("scalar", xnc_guard[ob])
            for t in range(4):
                r.dma_b("scalar", "bA" + str(ob), xc4[:, ob, t, :],
                        self.scr["xn"][(4 * c + t) * 128:(4 * c + t + 1) * 128, :])
            xn_ev = r.marker("bA" + str(ob))
            for hp in range(8):
                kc = hp
                r.wait("tensor", u_guard[0])
                r.wait("tensor", u_guard[1])
                ev_evs = [[None] * NT, [None] * NT]

                def emit_scores(hh, t):
                    sp = hh * 2 + t % 2
                    kp = hh * 64
                    r.wait("tensor", sc_guard[sp])
                    ev_mm = r.op("tensor", lambda e, kc=kc, kp=kp, t=t, sp=sp, c=c: e.matmul(
                        ps_sc[sp][:, 0:512],
                        kT3[kp:kp + 64, kc, t * 128:(t + 1) * 128],
                        qT3[kp:kp + 64, kc, c * 512:(c + 1) * 512],
                        start=True, stop=True, tile_position=(kp, 0)), inc="tE")
                    eng = "vector" if (t + hh) % 2 == 0 else "scalar"
                    r.wait(eng, ev_mm)
                    r.wait(eng, ws_guard[sp])
                    if eng == "vector":
                        ev_evs[hh][t] = r.op("vector", lambda e, sp=sp: e.tensor_scalar(
                            ws3[:, sp, :], ps_sc[sp][:, 0:512], 0.0, None,
                            ALU.max), inc="vE")
                    else:
                        ev_evs[hh][t] = r.op("scalar", lambda e, sp=sp: e.activation(
                            ws3[:, sp, :], ps_sc[sp][:, 0:512], AF.Relu), inc="aE")

                def emit_av(hh, t):
                    sp = hh * 2 + t % 2
                    h = 2 * kc + hh
                    r.wait("tensor", ev_evs[hh][t])
                    r.op("tensor", lambda e, h=h, t=t, sp=sp, hh=hh: e.matmul(
                        ps_u[hh][0:65, 0:512], va3[:, t, 65 * h:65 * h + 65],
                        ws3[:, sp, :], start=(t == 0), stop=(t == NT - 1)),
                        inc="tE")
                    sc_guard[sp] = r.marker("tE")
                    ws_guard[sp] = sc_guard[sp]

                for hh in range(2):
                    emit_scores(hh, 0)
                for hh in range(2):
                    emit_scores(hh, 1)
                for t in range(NT):
                    for hh in range(2):
                        emit_av(hh, t)
                    if t + 2 < NT:
                        for hh in range(2):
                            emit_scores(hh, t + 2)
                with_lp = self.nc.allow_low_precision
                for hh in range(2):
                    u_done = r.marker("tE")
                    kp = hh * 64
                    r.wait("vector", u_done)
                    r.wait("vector", rr_guard)
                    def _radd(e, hh=hh):
                        with with_lp("bf16 normalizer is a matmul operand"):
                            return e.tensor_scalar(
                                rr[64:65, 0:512], ps_u[hh][64:65, 0:512], 1e-6,
                                None, ALU.add)
                    r.op("vector", _radd)

                    def _recip(e):
                        with with_lp("bf16 normalizer is a matmul operand"):
                            return e.reciprocal(rrb[64:65, 0:512], rr[64:65, 0:512])
                    ev_r = r.op("vector", _recip, inc="vE")
                    u_guard[hh] = r.op("vector", lambda e, hh=hh: e.tensor_copy(
                        ub[0:64, 0:512], ps_u[hh][0:64, 0:512]), inc="vE")
                    r.wait("tensor", ev_r)
                    r.wait("tensor", bc_guard)
                    ev_bc = r.op("tensor", lambda e: e.matmul(
                        ps_bc[0:64, 0:512], ones64v[64:65, 0:64],
                        rrb[64:65, 0:512], start=True, stop=True), inc="tE")
                    rr_guard = ev_bc
                    r.wait("vector", ev_bc)
                    dsto = oT4[:, ob, kc, :][kp:kp + 64, :]
                    bc_guard = r.op("vector", lambda e, dsto=dsto: e.tensor_tensor(
                        dsto, ub[0:64, 0:512], ps_bc[0:64, 0:512], ALU.mult),
                        inc="vE")
            oT_ready = r.marker("vE")

            r.wait("tensor", wout_ev)
            r.wait("tensor", oT_ready)
            for t in range(4):
                stile = 4 * c + t
                for nch in range(2):
                    r.wait("tensor", at_guard)
                    for k in range(KT):
                        r.op("tensor", lambda e, ob=ob, k=k, t=t, nch=nch: e.matmul(
                            ps_at[:, 0:512],
                            oT4[:, ob, k, t * 128:(t + 1) * 128],
                            wo3[:, k, nch * 512:(nch + 1) * 512],
                            start=(k == 0), stop=(k == KT - 1)),
                            inc="tE" if k == KT - 1 else None)
                    mm = r.marker("tE")
                    r.wait("vector", mm)
                    r.wait("vector", xn_ev)
                    if nch == 0:
                        r.wait("vector", yw_guard)
                    at_guard = r.op("vector", lambda e, nch=nch: e.scalar_tensor_tensor(
                        yw[:, nch * 512:(nch + 1) * 512], ps_at[:, 0:512], 1.0,
                        boutv[:, nch * 512:(nch + 1) * 512],
                        ALU.mult, ALU.add), inc="vE")
                r.wait("gpsimd", at_guard)
                ev_y = r.op("gpsimd", lambda e, ob=ob, t=t: e.tensor_tensor(
                    yw[:, 0:D], yw[:, 0:D], xc4[:, ob, t, :], ALU.add),
                    inc="gE", inc_n=1)
                xnc_guard[ob] = ev_y
                r.wait("scalar", ev_y)
                r.wait("scalar", tw_guard)
                r.op("scalar", lambda e: e.activation(
                    tw[:, 0:D], yw[:, 0:D], AF.Square, accum_out=sta[:, 1:2]))
                ev = r.op("scalar", lambda e: e.activation(
                    tw[:, 0:D], yw[:, 0:D], AF.Copy, accum_out=sta[:, 0:1]),
                    inc="aE")
                r.wait("vector", ev)
                r.opc("vector", lambda e: e.tensor_scalar(
                    sta[:, 0:1], sta[:, 0:1], 1.0 / D, None, ALU.mult))
                r.opc("vector", lambda e: e.tensor_scalar(
                    sta[:, 2:3], sta[:, 1:2], 1.0 / D, None, ALU.mult))
                r.opc("vector", lambda e: e.tensor_tensor(
                    sta[:, 3:4], sta[:, 0:1], sta[:, 0:1], ALU.mult))
                r.opc("vector", lambda e: e.tensor_tensor(
                    sta[:, 2:3], sta[:, 2:3], sta[:, 3:4], ALU.subtract))
                ev = r.op("vector", lambda e: e.tensor_scalar(
                    sta[:, 2:3], sta[:, 2:3], LN_EPS, None, ALU.add), inc="vE")
                r.wait("scalar", ev)
                r.opc("scalar", lambda e: e.activation(sta[:, 2:3], sta[:, 2:3], AF.Ln))
                ev = r.op("scalar", lambda e: e.activation(
                    sta[:, 2:3], sta[:, 2:3], AF.Exp, scale=-0.5), inc="aE")
                r.wait("vector", ev)
                r.wait("vector", tw_guard)
                yw_guard = r.op("vector", lambda e: e.tensor_scalar(
                    tw[:, 0:D], yw[:, 0:D], sta[:, 0:1], sta[:, 2:3],
                    ALU.subtract, ALU.mult), inc="vE")
                r.wait("gpsimd", yw_guard)
                r.wait("gpsimd", sw_guard)
                tw_guard = r.op("gpsimd", lambda e: e.tensor_tensor(
                    sw[:, 0:D], tw[:, 0:D], gammav[:, 0:D], ALU.mult),
                    inc="gE", inc_n=1)
                ev = r.op("gpsimd", lambda e: e.tensor_tensor(
                    sw[:, 0:D], sw[:, 0:D], betav[:, 0:D], ALU.add),
                    inc="gE", inc_n=1)
                r.wait("sync", ev)
                sw_guard = r.dma("sync", self.scr["s"][stile * 128:(stile + 1) * 128, :],
                                 sw[:, 0:D])
            oT_guard[ob] = r.marker("tE")

        self.tap("vb", self.B["va"][:, 0:NT * HA * 65])
        if "sout" in self.debug:
            r.fence(engs=("scalar",))
            r.dma("scalar", self.taps["sout"], self.scr["s"])
        r.fence()
        for i in range(4):
            self.P.free(f"sc{i}")
        for i in range(2):
            self.P.free(f"u{i}")
        self.P.free("bc")
        self.P.free("at")
        self.free("qTa", "kTa", "va", "wob", "wst", "oT", "rr", "rrb", "ub",
                  "xnc", "yw", "tw", "sw", "sta")
        self.free("gamma_a", "beta_a", "bout_a")

    def p2(self, l):
        r, dr, scr = self.r, self.dr, self.scr
        r.fence()
        if l == 0:
            self.alloc("z", NT * D, F32)
        z = self.B["z"]
        zt = z.rearrange("p (t d) -> p t d", t=NT)
        if l == 0:
            for t in range(NT):
                r.dma("scalar", zt[:, t, :], scr["xn"][t * 128:(t + 1) * 128, :])
        else:
            for t in range(NT):
                r.dma("gpsimd", scr["tp"][t * 128:(t + 1) * 128, :], zt[:, t, :])
        zT = self.alloc("zT", KT * (S + 16), BF16)
        zT3 = zT.rearrange("p (k s) -> p k s", k=KT)
        zcol_ev = r.op("vector", lambda e: e.memset(zT3[:, :, 0:16], 0.0), inc="vE")
        r.wait("sync", r.marker("qG"))
        for k in range(KT):
            r.op("sync", lambda e, k=k: e.dma_start(
                zT3[:, k, 16:S + 16], scr["tp"][:, k * 128:(k + 1) * 128],
                transpose=True), inc="bS1", inc_n=16, is_dma=True)
        zT_ev = r.marker("bS1")
        z_ev = r.marker("qA")

        qs = self.alloc("qs8", HD, F32)
        ks = self.alloc("ks8", HD, F32)
        r.dma_b("sync", "bS0", qs[:, 0:HD], dr[f"qs_f{l}"].broadcast_to([128, HD]))
        r.dma_b("sync", "bS0", ks[:, 0:HD], dr[f"ks_f{l}"].broadcast_to([128, HD]))
        tabs = {}
        for tn in ("tqc", "tqs", "tkc", "tks"):
            tabs[tn] = self.alloc(tn, NT * HD, F32)
            r.dma_b("scalar", "bA0", tabs[tn][:, 0:NT * HD], dr[tn])
        tab3 = {tn: tabs[tn].rearrange("p (t d) -> p t d", t=NT) for tn in tabs}
        small_ev = r.marker("bS0")
        tab_ev = r.marker("bA0")
        r.wait("vector", small_ev)
        r.op("vector", lambda e: e.tensor_scalar(
            qs[:, 0:HD], qs[:, 0:HD], 8.0, None, ALU.mult), inc="vE")

        wb = {}
        for wn in ("wq", "wk", "wv"):
            buf = self.alloc(wn + "f", KT * 512, BF16)
            w3 = buf.rearrange("p (k c) -> p k c", k=KT)
            r.dma_b("gpsimd", "bG0", w3[:, :, :],
                    dr[f"{wn}_f{l}"].rearrange("(k p) c -> p k c", p=128))
            wb[wn] = w3
        wf_ev = r.marker("bG0")

        qtok = self.alloc("qtok", NT * INNER, BF16)
        ktok = self.alloc("ktok", NT * INNER, BF16)
        vf = self.alloc("vf", NT * INNER, BF16)
        q3 = qtok.rearrange("p (t c) -> p t c", t=NT)
        k3 = ktok.rearrange("p (t c) -> p t c", t=NT)
        v3 = vf.rearrange("p (t c) -> p t c", t=NT)

        qw = self.alloc("qw", 512, F32)
        qq = self.alloc("qq", 512, F32)
        q1 = self.alloc("q1", 512, F32)
        t1 = self.alloc("t1", 512, F32)
        t2 = self.alloc("t2", 512, F32)
        nrm = self.alloc("nrm", 16, F32)

        ps = [self.P.alloc(f"pf{i}", 1) for i in range(4)]
        pev = [None] * 4
        pi = [0]
        r.wait("tensor", zT_ev)
        r.wait("tensor", zcol_ev)
        r.wait("tensor", wf_ev)
        r.wait("vector", tab_ev)
        chain = None

        for t in range(NT):
            for wn, dtok in (("wq", q3), ("wk", k3), ("wv", v3)):
                p = pi[0] % 4
                pi[0] += 1
                r.wait("tensor", pev[p])
                for k in range(KT):
                    coff = 16 if k < 4 else 15
                    r.op("tensor", lambda e, p=p, wn=wn, k=k, t=t, coff=coff: e.matmul(
                        ps[p][:, 0:512],
                        zT3[:, k, coff + t * 128:coff + t * 128 + 128],
                        wb[wn][:, k, :], start=(k == 0), stop=(k == KT - 1)),
                        inc="tE" if k == KT - 1 else None)
                mm = r.marker("tE")
                if wn == "wv":
                    r.wait("vector", mm)
                    pev[p] = r.op("vector", lambda e, p=p, t=t: e.tensor_copy(
                        v3[:, t, :], ps[p][:, 0:512]), inc="vE")
                    continue
                is_q = wn == "wq"
                sc = qs if is_q else ks
                ctn, stn = ("tqc", "tqs") if is_q else ("tkc", "tks")
                r.wait("vector", mm)
                r.wait("vector", chain)
                pev[p] = r.op("vector", lambda e, p=p: e.tensor_copy(
                    qw[:, 0:512], ps[p][:, 0:512]), inc="vE")
                r.wait("scalar", pev[p])
                ev1 = r.op("scalar", lambda e: e.activation(
                    qq[:, 0:512], qw[:, 0:512], AF.Square), inc="aE")
                r.wait("vector", ev1)
                r.opc("vector", lambda e: e.reduce_sum(
                    nrm[:, 0:8], qq[:, 0:512].rearrange("p (h d) -> p h d", h=8),
                    axis=AX.X))
                ev2 = r.op("vector", lambda e: e.tensor_scalar(
                    nrm[:, 0:8], nrm[:, 0:8], 1e-24, None, ALU.max), inc="vE")
                r.wait("scalar", ev2)
                r.opc("scalar", lambda e: e.activation(nrm[:, 8:16], nrm[:, 0:8], AF.Ln))
                ev3 = r.op("scalar", lambda e: e.activation(
                    nrm[:, 8:16], nrm[:, 8:16], AF.Exp, scale=-0.5), inc="aE")
                r.wait("vector", ev3)
                r.op("vector", lambda e: e.tensor_tensor(
                    q1[:, 0:512].rearrange("p (h d) -> p h d", h=8),
                    qw[:, 0:512].rearrange("p (h d) -> p h d", h=8),
                    nrm[:, 8:16].rearrange("p (h o) -> p h o", o=1
                                           ).broadcast_to([128, 8, HD]),
                    ALU.mult))
                r.op("vector", lambda e, sc=sc: e.tensor_tensor(
                    qw[:, 0:512].rearrange("p (h d) -> p h d", h=8),
                    q1[:, 0:512].rearrange("p (h d) -> p h d", h=8),
                    sc[:, 0:HD].rearrange("p (o d) -> p o d", o=1
                                          ).broadcast_to([128, 8, HD]),
                    ALU.mult))
                r.op("vector", lambda e, ctn=ctn, t=t: e.tensor_tensor(
                    t1[:, 0:512].rearrange("p (h d) -> p h d", h=8),
                    qw[:, 0:512].rearrange("p (h d) -> p h d", h=8),
                    tab3[ctn][:, t, :].rearrange("p (o d) -> p o d", o=1
                                                 ).broadcast_to([128, 8, HD]),
                    ALU.mult))
                q14 = qw[:, 0:512].rearrange("p (h r d) -> p h r d", h=8, r=2)
                t24 = t2[:, 0:512].rearrange("p (h r d) -> p h r d", h=8, r=2)
                st4 = tab3[stn][:, t, :].rearrange("p (o r d) -> p o r d", o=1, r=2)
                r.op("vector", lambda e, q14=q14, t24=t24, st4=st4: e.tensor_tensor(
                    t24[:, :, 0:1, :], q14[:, :, 1:2, :],
                    st4[:, :, 0:1, :].broadcast_to([128, 8, 1, 32]), ALU.mult))
                r.op("vector", lambda e, q14=q14, t24=t24, st4=st4: e.tensor_tensor(
                    t24[:, :, 1:2, :], q14[:, :, 0:1, :],
                    st4[:, :, 1:2, :].broadcast_to([128, 8, 1, 32]), ALU.mult))
                chain = r.op("vector", lambda e, dtok=dtok, t=t: e.tensor_tensor(
                    dtok[:, t, :], t1[:, 0:512], t2[:, 0:512], ALU.add), inc="vE")

        for i in range(4):
            self.P.free(f"pf{i}")
        self.free("wqf", "wkf", "wvf", "qw", "qq", "q1", "t1", "t2",
                  "tqc", "tqs", "tkc", "tks", "qs8", "ks8")
        self.tap("qf", self.B["qtok"][:, 0:NT * INNER])
        self.tap("kf", self.B["ktok"][:, 0:NT * INNER])
        self.tap("vf", self.B["vf"][:, 0:NT * INNER])
        r.fence()
        self.free("zT")

        for t in range(NT):
            r.dma("scalar", scr["tq"][t * 128:(t + 1) * 128, :], q3[:, t, :])
            r.dma("scalar", scr["tk"][t * 128:(t + 1) * 128, :], k3[:, t, :])
        dmp = r.marker("qA")
        qfT = self.alloc("qfT", 4 * S, BF16)
        kfT = self.alloc("kfT", 4 * S, BF16)
        qfT3 = qfT.rearrange("p (k s) -> p k s", k=4)
        kfT3 = kfT.rearrange("p (k s) -> p k s", k=4)
        r.wait("sync", dmp)
        for k in range(4):
            r.op("sync", lambda e, k=k: e.dma_start(
                qfT3[:, k, :], scr["tq"][:, k * 128:(k + 1) * 128],
                transpose=True), inc="bS1", inc_n=16, is_dma=True)
            r.op("sync", lambda e, k=k: e.dma_start(
                kfT3[:, k, :], scr["tk"][:, k * 128:(k + 1) * 128],
                transpose=True), inc="bS1", inc_n=16, is_dma=True)
        qkT_ev = r.marker("bS1")
        r.fence(engs=("vector", "scalar", "gpsimd", "tensor"), sems=("qA",))
        self.free("qtok", "ktok")

        wob = self.alloc("wobf", 4 * D, BF16)
        wo3 = wob.rearrange("p (k n) -> p k n", k=4)
        r.dma_b("gpsimd", "bG1", wo3[:, :, :],
                dr[f"wo_f{l}"].rearrange("(k p) c -> p k c", p=128))
        wo_ev = r.marker("bG1")

        of = self.alloc("of", 4 * S, BF16)
        of3 = of.rearrange("p (k s) -> p k s", k=4)
        msc = self.alloc("msc", 2 * 640, F32)
        msc3 = msc.rearrange("p (b c) -> p b c", b=2)
        attn = self.alloc("attn", 2 * 640, BF16)
        attn3 = attn.rearrange("p (b c) -> p b c", b=2)
        atT = self.alloc("atT", 2 * 640, BF16)
        atT3 = atT.rearrange("p (b c) -> p b c", b=2)
        colsb = self.alloc("cols", 8, F32)
        cols = colsb.rearrange("p (b c) -> p b c", b=2)

        ps_sc = [self.P.alloc(f"fs{i}", 2) for i in range(2)]
        ps_tp = [self.P.alloc(f"ft{i}", 1, BF16) for i in range(2)]
        ps_o = [self.P.alloc(f"fo{i}", 1) for i in range(2)]

        pssc_guard = [None, None]
        msc_guard = [None, None]
        attn_guard = [None, None]
        pstp_guard = [None, None]
        atT_guard = [None, None]
        pso_guard = [None, None]

        r.wait("tensor", qkT_ev)
        mkv = self.B["mask"]
        identv = self.B["ident"]

        tiles = [(h, a) for h in range(HF) for a in range(NT)]
        nm_ev = [None] * len(tiles)
        tT_ev = [None] * len(tiles)

        def s_scores(i):
            h, a = tiles[i]
            kc, kp = h // 2, (h % 2) * 64
            wmax = min(a, 4)
            W = (wmax + 1) * 128
            k0 = a - wmax
            par = i % 2
            r.wait("tensor", pssc_guard[par])
            nleft, off = W, 0
            while nleft > 0:
                n = min(512, nleft)
                r.op("tensor", lambda e, kc=kc, kp=kp, a=a, k0=k0, off=off, n=n, par=par: e.matmul(
                    ps_sc[par][:, off:off + n],
                    qfT3[kp:kp + 64, kc, a * 128:(a + 1) * 128],
                    kfT3[kp:kp + 64, kc, k0 * 128 + off:k0 * 128 + off + n],
                    start=True, stop=True, tile_position=(kp, 0)),
                    inc="tE" if n == nleft else None)
                off += n
                nleft -= n
            mm = r.marker("tE")
            r.wait("vector", mm)
            r.wait("vector", msc_guard[par])
            pssc_guard[par] = r.op("vector", lambda e, par=par, W=W: e.scalar_tensor_tensor(
                msc3[:, par, 0:W], ps_sc[par][:, 0:W], 1.0,
                mkv[:, 640 - W:640], ALU.mult, ALU.add), inc="vE")
            r.opc("vector", lambda e, par=par, W=W: e.reduce_max(
                cols[:, par, 0:1], msc3[:, par, 0:W], axis=AX.X))
            ev_neg = r.op("vector", lambda e, par=par: e.tensor_scalar(
                cols[:, par, 1:2], cols[:, par, 0:1], -1.0, None, ALU.mult), inc="vE")
            r.wait("scalar", ev_neg)
            r.wait("scalar", attn_guard[par])
            ev_ex = r.op("scalar", lambda e, par=par, W=W: e.activation(
                attn3[:, par, 0:W], msc3[:, par, 0:W], AF.Exp,
                bias=cols[:, par, 1:2], accum_out=cols[:, par, 2:3]), inc="aE")
            msc_guard[par] = ev_ex
            r.wait("vector", ev_ex)
            r.opc("vector", lambda e, par=par: e.reciprocal(
                cols[:, par, 3:4], cols[:, par, 2:3]))
            nm_ev[i] = r.op("vector", lambda e, par=par, W=W: e.tensor_scalar(
                attn3[:, par, 0:W], attn3[:, par, 0:W], cols[:, par, 3:4], None,
                ALU.mult), inc="vE")

        def s_transp(i):
            h, a = tiles[i]
            wmax = min(a, 4)
            W = (wmax + 1) * 128
            par = i % 2
            r.wait("tensor", nm_ev[i])
            r.wait("tensor", pstp_guard[par])
            for j in range(wmax + 1):
                r.op("tensor", lambda e, par=par, j=j: e.transpose(
                    ps_tp[par][:, j * 128:(j + 1) * 128],
                    attn3[:, par, j * 128:(j + 1) * 128],
                    identv[:, 0:128]),
                    inc="tE" if j == wmax else None)
            attn_guard[par] = r.marker("tE")
            r.wait("vector", attn_guard[par])
            r.wait("vector", atT_guard[par])
            tT_ev[i] = r.op("vector", lambda e, par=par, W=W: e.tensor_copy(
                atT3[:, par, 0:W], ps_tp[par][:, 0:W]), inc="vE")
            pstp_guard[par] = tT_ev[i]

        def s_av(i):
            h, a = tiles[i]
            kc, kp = h // 2, (h % 2) * 64
            wmax = min(a, 4)
            k0 = a - wmax
            par = i % 2
            r.wait("tensor", pso_guard[par])
            r.wait("tensor", tT_ev[i])
            for j in range(wmax + 1):
                kt = k0 + j
                r.op("tensor", lambda e, h=h, kt=kt, j=j, par=par, wmax=wmax: e.matmul(
                    ps_o[par][0:64, 0:128], v3[:, kt, 64 * h:64 * h + 64],
                    atT3[:, par, j * 128:(j + 1) * 128],
                    start=(j == 0), stop=(j == wmax)),
                    inc="tE" if j == wmax else None)
            atT_guard[par] = r.marker("tE")
            r.wait("vector", atT_guard[par])
            pso_guard[par] = r.op("vector", lambda e, kp=kp, kc=kc, a=a, par=par: e.tensor_copy(
                of3[:, kc, a * 128:(a + 1) * 128][kp:kp + 64, :],
                ps_o[par][0:64, 0:128]), inc="vE")

        s_scores(0)
        for i in range(len(tiles)):
            if i + 1 < len(tiles):
                s_scores(i + 1)
            s_transp(i)
            s_av(i)

        of_ev = r.marker("vE")
        for i in range(2):
            self.P.free(f"fs{i}")
            self.P.free(f"ft{i}")
            self.P.free(f"fo{i}")
        if l == 0:
            self.tap("of", self.B["of"][:, 0:4 * S])
        self.free("msc", "attn", "atT", "cols", "qfT", "kfT", "vf")

        ps_d = [self.P.alloc(f"pd{i}", 2) for i in range(2)]
        dguard = [None, None]
        r.wait("tensor", of_ev)
        r.wait("tensor", wo_ev)
        r.wait("vector", z_ev)
        for t in range(NT):
            par = t % 2
            r.wait("tensor", dguard[par])
            for k in range(4):
                for nch in range(2):
                    r.op("tensor", lambda e, k=k, t=t, nch=nch, par=par: e.matmul(
                        ps_d[par][:, nch * 512:(nch + 1) * 512],
                        of3[:, k, t * 128:(t + 1) * 128],
                        wo3[:, k, nch * 512:(nch + 1) * 512],
                        start=(k == 0), stop=(k == 3)),
                        inc="tE" if (k == 3 and nch == 1) else None)
            mm = r.marker("tE")
            r.wait("vector", mm)
            dguard[par] = r.op("vector", lambda e, t=t, par=par: e.tensor_tensor(
                zt[:, t, :], zt[:, t, :], ps_d[par][:, 0:1024], ALU.add), inc="vE")
        for i in range(2):
            self.P.free(f"pd{i}")
        self.free("of", "wobf")
        if l == 0:
            self.tap("z1", z[:, 0:NT * D])
        r.fence()

        lg = self.alloc("lg", D, F32)
        lb = self.alloc("lb", D, F32)
        r.dma_b("sync", "bS0", lg[:, 0:D], dr[f"lg_f{l}"].broadcast_to([128, D]))
        r.dma_b("sync", "bS0", lb[:, 0:D], dr[f"lb_f{l}"].broadcast_to([128, D]))
        lgb_ev = r.marker("bS0")
        sh = self.alloc("sh", 2 * 512, F32)
        sh3 = sh.rearrange("p (b c) -> p b c", b=2)
        h2b = self.alloc("h2b", 2 * D, BF16)
        h2b3 = h2b.rearrange("p (b c) -> p b c", b=2)
        aw = self.alloc("aw", D, F32)
        stc = self.alloc("stc", 8, F32)
        sh_guard = [None, None]
        h2b_guard = [None, None]
        aw_guard = None
        r.wait("gpsimd", lgb_ev)
        for t in range(NT):
            par = t % 2
            r.wait("sync", sh_guard[par])
            bsh = "bS" + str(2 + par)
            if t == 0:
                ev0 = r.op("vector", lambda e, par=par: e.memset(
                    sh3[0:1, par, :], 0.0), inc="vE")
                r.wait("sync", ev0)
            else:
                r.dma_b("sync", bsh, sh3[0:1, par, :], zt[127:128, t - 1, 512:1024])
            r.dma_b("sync", bsh, sh3[1:128, par, :], zt[0:127, t, 512:1024])
            shd = r.marker(bsh)
            r.wait("scalar", shd)
            r.wait("scalar", aw_guard)
            r.op("scalar", lambda e, t=t: e.activation(
                aw[:, 0:512], zt[:, t, 0:512], AF.Copy, accum_out=stc[:, 0:1]))
            r.op("scalar", lambda e, par=par: e.activation(
                aw[:, 0:512], sh3[:, par, :], AF.Copy, accum_out=stc[:, 1:2]))
            r.op("scalar", lambda e, t=t: e.activation(
                aw[:, 0:512], zt[:, t, 0:512], AF.Square, accum_out=stc[:, 2:3]))
            ev = r.op("scalar", lambda e, par=par: e.activation(
                aw[:, 0:512], sh3[:, par, :], AF.Square, accum_out=stc[:, 3:4]),
                inc="aE")
            r.wait("vector", ev)
            r.opc("vector", lambda e: e.tensor_tensor(
                stc[:, 0:1], stc[:, 0:1], stc[:, 1:2], ALU.add))
            r.opc("vector", lambda e: e.tensor_tensor(
                stc[:, 2:3], stc[:, 2:3], stc[:, 3:4], ALU.add))
            r.opc("vector", lambda e: e.tensor_scalar(
                stc[:, 0:1], stc[:, 0:1], 1.0 / D, None, ALU.mult))
            r.opc("vector", lambda e: e.tensor_scalar(
                stc[:, 2:3], stc[:, 2:3], 1.0 / D, None, ALU.mult))
            r.opc("vector", lambda e: e.tensor_tensor(
                stc[:, 4:5], stc[:, 0:1], stc[:, 0:1], ALU.mult))
            r.opc("vector", lambda e: e.tensor_tensor(
                stc[:, 2:3], stc[:, 2:3], stc[:, 4:5], ALU.subtract))
            ev = r.op("vector", lambda e: e.tensor_scalar(
                stc[:, 2:3], stc[:, 2:3], LN_EPS, None, ALU.add), inc="vE")
            r.wait("scalar", ev)
            r.opc("scalar", lambda e: e.activation(stc[:, 2:3], stc[:, 2:3], AF.Ln))
            ev = r.op("scalar", lambda e: e.activation(
                stc[:, 2:3], stc[:, 2:3], AF.Exp, scale=-0.5), inc="aE")
            r.wait("vector", ev)
            r.wait("vector", h2b_guard[par])
            r.op("vector", lambda e, t=t: e.tensor_scalar(
                aw[:, 0:512], zt[:, t, 0:512], stc[:, 0:1], stc[:, 2:3],
                ALU.subtract, ALU.mult))
            ev = r.op("vector", lambda e, par=par: e.tensor_scalar(
                aw[:, 512:1024], sh3[:, par, :], stc[:, 0:1], stc[:, 2:3],
                ALU.subtract, ALU.mult), inc="vE")
            r.wait("gpsimd", ev)
            r.op("gpsimd", lambda e: e.tensor_tensor(
                aw[:, 0:D], aw[:, 0:D], lg[:, 0:D], ALU.mult), inc="gE", inc_n=1)
            ev = r.op("gpsimd", lambda e, par=par: e.tensor_tensor(
                h2b3[:, par, :], aw[:, 0:D], lb[:, 0:D], ALU.add),
                inc="gE", inc_n=1)
            aw_guard = ev
            sh_guard[par] = ev
            r.wait("scalar", ev)
            h2b_guard[par] = r.dma_b(
                "scalar", "bA" + str(par),
                scr["tp"][t * 128:(t + 1) * 128, :], h2b3[:, par, :])
        r.fence(engs=("sync",), sems=("qA", "bA0", "bA1"))
        dmp = r.marker("qA")
        h2T = self.alloc("h2T", KT * S, BF16)
        h2T3 = h2T.rearrange("p (k s) -> p k s", k=KT)
        r.wait("sync", dmp)
        for k in range(KT):
            r.op("sync", lambda e, k=k: e.dma_start(
                h2T3[:, k, :], scr["tp"][:, k * 128:(k + 1) * 128],
                transpose=True), inc="bS1", inc_n=16, is_dma=True)
        self.tap("h2T", self.B["h2T"][:, 0:KT * S])
        r.fence()
        self.free("lg", "lb", "sh", "h2b", "aw", "stc")

        b1a = self.alloc("b1a", HID_T, F32)
        b1g = self.alloc("b1g", HID_T, F32)
        b2 = self.alloc("b2", D, F32)
        r.dma_b("sync", "bS0", b1a[:, 0:HID_T], dr[f"b1a_f{l}"])
        r.dma_b("sync", "bS0", b1g[:, 0:HID_T], dr[f"b1g_f{l}"])
        r.dma_b("sync", "bS0", b2[:, 0:D], dr[f"b2_f{l}"].broadcast_to([128, D]))
        bia_ev = r.marker("bS0")
        w1b = self.alloc("w1b", 32 * 128, BF16)
        w1b4 = w1b.rearrange("p (g k c) -> p g k c", g=4, k=KT)
        w2b = self.alloc("w2b", 11 * 1024, BF16)
        w2b3 = w2b.rearrange("p (k n) -> p k n", k=11)
        h4 = self.alloc("h4", HID_T * 1024, BF16)
        h43 = h4.rearrange("p (k s) -> p k s", k=HID_T)
        gelb = self.alloc("gel", 2 * 1024, F32)
        gel3 = gelb.rearrange("p (b c) -> p b c", b=2)
        tmp2 = self.alloc("tmp2", D, F32)
        slot_ev = {}
        si = [0]
        r.wait("vector", bia_ev)
        r.wait("scalar", bia_ev)
        h2T_mark = r.marker("bS1")
        r.wait("tensor", h2T_mark)

        pa_guard = [None, None]
        pg_guard = [None, None]
        gel_guard = [None, None]
        tmp2_guard = None
        pw_guard = [None, None]
        h4_guard = {}
        w2slot_guard = {}

        for half in range(2):
            scol = half * 1024
            ps_a = [self.P.alloc(f"pa{i}", 2) for i in range(2)]
            ps_g = [self.P.alloc(f"pg{i}", 2) for i in range(2)]
            for kt in range(HID_T):
                par = kt % 2
                rows = 128 if kt < HID_T - 1 else HID - 128 * (HID_T - 1)
                for ag in range(2):
                    wsrc = f"w1a_f{l}" if ag == 0 else f"w1g_f{l}"
                    psAG = ps_a[par] if ag == 0 else ps_g[par]
                    sgrp = par * 2 + ag
                    bs = "bG" + str(sgrp)
                    r.wait("gpsimd", slot_ev.get(sgrp))
                    r.dma_b("gpsimd", bs,
                            w1b4[:, sgrp, :, 0:rows],
                            dr[wsrc][:, kt * 128:kt * 128 + rows].rearrange(
                                "(k p) c -> p k c", p=128))
                    slots = [(sgrp, k) for k in range(KT)]
                    batch_ev = r.marker(bs)
                    r.wait("tensor", pa_guard[par] if ag == 0 else pg_guard[par])
                    r.wait("tensor", batch_ev)
                    for k in range(KT):
                        for nch in range(2):
                            r.op("tensor", lambda e, sgrp=sgrp, k=k, nch=nch, psAG=psAG, scol=scol, rows=rows: e.matmul(
                                psAG[0:rows, nch * 512:(nch + 1) * 512],
                                w1b4[:, sgrp, k, 0:rows],
                                h2T3[:, k, scol + nch * 512:scol + (nch + 1) * 512],
                                start=(k == 0), stop=(k == KT - 1)),
                                inc="tE" if (k == KT - 1 and nch == 1) else None)
                    mm = r.marker("tE")
                    slot_ev[sgrp] = mm
                    if ag == 0:
                        pa_mm = mm
                    else:
                        pg_mm = mm
                r.wait("scalar", pg_mm)
                r.wait("scalar", gel_guard[par])
                pg_guard[par] = r.op("scalar", lambda e, kt=kt, rows=rows, par=par: e.activation(
                    gel3[0:rows, par, :], ps_g[par][0:rows, 0:1024], AF.Gelu,
                    bias=b1g[0:rows, kt:kt + 1]), inc="aE")
                r.wait("vector", pg_guard[par])
                r.wait("vector", pa_mm)
                r.wait("vector", h4_guard.get(kt))
                pa_guard[par] = r.op("vector", lambda e, kt=kt, rows=rows, par=par: e.scalar_tensor_tensor(
                    h43[0:rows, kt, 0:1024], ps_a[par][0:rows, 0:1024],
                    b1a[0:rows, kt:kt + 1], gel3[0:rows, par, :],
                    ALU.add, ALU.mult), inc="vE")
                gel_guard[par] = pa_guard[par]
                h4_guard[kt] = pa_guard[par]
            h4_done = r.marker("vE")
            r.wait("tensor", h4_done)
            for i in range(2):
                self.P.free(f"pa{i}")
                self.P.free(f"pg{i}")
            ps_w = [self.P.alloc(f"pw{i}", 2) for i in range(2)]
            pa_guard = [None, None]
            pg_guard = [None, None]

            for kh in range(2):
                kts = list(range(0, 11)) if kh == 0 else list(range(11, HID_T))
                bs = "bG" + str(4 + kh)
                r.wait("gpsimd", w2slot_guard.get(0))
                r.dma_b("gpsimd", bs, w2b3[:, :, :],
                        dr[f"w2_f{l}"][kh * 1408:kh * 1408 + 11 * 128, :].rearrange(
                            "(k p) c -> p k c", p=128))
                batch_ev = r.marker(bs)
                r.wait("tensor", h4_done)
                r.wait("tensor", batch_ev)
                for ti in range(8):
                    t = half * 8 + ti
                    par = ti % 2
                    r.wait("tensor", pw_guard[par])
                    for j, kt in enumerate(kts):
                        for nch in range(2):
                            r.op("tensor", lambda e, j=j, kt=kt, ti=ti, nch=nch, par=par: e.matmul(
                                ps_w[par][:, nch * 512:(nch + 1) * 512],
                                h43[:, kt, ti * 128:(ti + 1) * 128],
                                w2b3[:, j, nch * 512:(nch + 1) * 512],
                                start=(j == 0), stop=(j == len(kts) - 1)),
                                inc="tE" if (j == len(kts) - 1 and nch == 1) else None)
                    mm = r.marker("tE")
                    r.wait("vector", mm)
                    if kh == 0:
                        pw_guard[par] = r.op("vector", lambda e, t=t, par=par: e.tensor_tensor(
                            zt[:, t, :], zt[:, t, :], ps_w[par][:, 0:1024], ALU.add),
                            inc="vE")
                    else:
                        r.wait("vector", tmp2_guard)
                        ev = r.op("vector", lambda e, par=par: e.scalar_tensor_tensor(
                            tmp2[:, 0:D], ps_w[par][:, 0:1024], 1.0, b2[:, 0:D],
                            ALU.mult, ALU.add), inc="vE")
                        r.wait("gpsimd", ev)
                        pw_guard[par] = r.op("gpsimd", lambda e, t=t: e.tensor_tensor(
                            zt[:, t, :], zt[:, t, :], tmp2[:, 0:D], ALU.add),
                            inc="gE", inc_n=1)
                        tmp2_guard = pw_guard[par]
                wmark = r.marker("tE")
                w2slot_guard[0] = wmark
            hm = r.marker("tE")
            for kt in range(HID_T):
                h4_guard[kt] = hm
            for i in range(2):
                self.P.free(f"pw{i}")
            pw_guard = [None, None]
            r.fence(engs=("tensor",), sems=("vE", "gE"))
        r.fence()
        self.free("b1a", "b1g", "b2", "w1b", "w2b", "h4", "gel", "tmp2", "h2T")
        self.tap("z2" if l == 0 else "zf", z[:, 0:NT * D])

    def p3(self):
        r, dr, scr = self.r, self.dr, self.scr
        r.fence()
        zt = self.B["z"].rearrange("p (t d) -> p t d", t=NT)
        xb = self.alloc("xb3", 2 * D, F32)
        xb3 = xb.rearrange("p (b c) -> p b c", b=2)
        sb = self.alloc("sb3", 2 * D, F32)
        sb3 = sb.rearrange("p (b c) -> p b c", b=2)
        gw = self.alloc("gw3", D, F32)
        ow = self.alloc("ow3", D, F32)
        buf_guard = [None, None]
        ow_guard = None
        gw_guard = None
        for t in range(NT):
            par = t % 2
            r.wait("scalar", buf_guard[par])
            e1 = r.dma_b("scalar", "bA" + str(par), xb3[:, par, :],
                         dr["x"][t * 128:(t + 1) * 128, :])
            e2 = r.dma_b("scalar", "bA" + str(par), sb3[:, par, :],
                         scr["s"][t * 128:(t + 1) * 128, :])
            r.wait("scalar", gw_guard)
            ev_g = r.op("scalar", lambda e, t=t: e.activation(
                gw[:, 0:D], zt[:, t, :], AF.Silu), inc="aE")
            r.wait("vector", ev_g)
            r.wait("vector", e2)
            r.wait("vector", ow_guard)
            gw_guard = r.op("vector", lambda e, par=par: e.tensor_tensor(
                ow[:, 0:D], gw[:, 0:D], sb3[:, par, :], ALU.mult), inc="vE")
            ev_o = r.op("vector", lambda e, par=par: e.tensor_tensor(
                ow[:, 0:D], ow[:, 0:D], xb3[:, par, :], ALU.add), inc="vE")
            r.wait("sync", ev_o)
            st_ev = r.dma("sync", dr["out"][t * 128:(t + 1) * 128, :], ow[:, 0:D])
            ow_guard = st_ev
            buf_guard[par] = st_ev

    def fin(self):
        self.r.fence(engs=("sync",))



def kernel(**inputs):
    g = host_layout(inputs)
    consts = host_constants()
    nc = build()
    x = np.asarray(inputs["L0_attn_x"], dtype=np.float32)
    in_maps = []
    for core in range(8):
        m = {"x": np.ascontiguousarray(x[core])}
        m.update(g)
        m.update(consts)
        in_maps.append(m)
    res = run_bass_kernel_spmd(nc, in_maps, core_ids=list(range(8)))
    return np.stack([r["out"] for r in res.results], axis=0)
